# version 04
# baseline (speedup 1.0000x reference)
import contextlib
import math

import numpy as np

import concourse.bass as bass
import concourse.mybir as mybir
from bass_rust import RegisterHandle, RuntimeValue
from concourse.ordered_engine_set import OrderedEngineSet
from concourse.alu_op_type import AluOpType as OP
from concourse.bass_utils import run_bass_kernel_spmd

F32 = mybir.dt.float32
BF16 = mybir.dt.bfloat16
AF = mybir.ActivationFunctionType
AX = mybir.AxisListType.X

NCORES = 8
DEBUG_TAPS = set()
B_PER_CORE = 2
S = 2048
D = 1024
T = B_PER_CORE * S
NCT = D // 128
NH = 16
HD = 64
EPS = 1e-5
TCH = 512
NCHUNK = T // TCH


class Sem:
    def __init__(self, handle):
        self.h = handle
        self.n = 0

    def inc(self, k=1):
        self.n += k
        return self.n


class Ctx:
    def __init__(self, nc, es: contextlib.ExitStack):
        self.nc = nc
        self.es = es
        self.dma = Sem(es.enter_context(nc.semaphore("s_dma")))
        self.dmg = Sem(es.enter_context(nc.semaphore("s_dmg")))
        self.mm = Sem(es.enter_context(nc.semaphore("s_mm")))
        self.ev = Sem(es.enter_context(nc.semaphore("s_ev")))
        self.sc = Sem(es.enter_context(nc.semaphore("s_sc")))
        self.sq = Sem(es.enter_context(nc.semaphore("s_sq")))
        self.consts = {}
        self._name_i = 0
        self._scopes = [es]
        self._reclaim_pos = 0
        self._freed_tmp = set()
        self.regs = {}
        for eng in (nc.sync, nc.tensor, nc.scalar, nc.vector, nc.gpsimd):
            nm = str(eng.engine)
            self.regs[eng.engine] = {
                "loops": [eng.alloc_register(f"lp{j}_{nm}") for j in range(2)],
                "walk": [eng.alloc_register(f"wk{j}_{nm}") for j in range(5)],
                "idx": [eng.alloc_register(f"ix{j}_{nm}") for j in range(2)],
            }

    @contextlib.contextmanager
    def scope(self):
        with contextlib.ExitStack() as inner:
            self._scopes.append(inner)
            try:
                yield
            finally:
                self._scopes.pop()
        self.reclaim_tmp_registers()

    def reclaim_tmp_registers(self):
        import re as _re
        engs = {"EngineType.SP": self.nc.sync, "EngineType.PE": self.nc.tensor,
                "EngineType.Activation": self.nc.scalar,
                "EngineType.DVE": self.nc.vector, "EngineType.Pool": self.nc.gpsimd}
        insts = list(self.nc.all_instructions())
        for inst in insts[self._reclaim_pos:]:
            s = str(inst)
            for m in _re.finditer(r"([A-Za-z]+_tmp_\d+)", s):
                nm = m.group(1)
                if nm in self._freed_tmp:
                    continue
                self._freed_tmp.add(nm)
                eng = engs.get(str(inst.engine))
                if eng is None:
                    continue
                try:
                    eng.free_register(RegisterHandle(name=nm, engine=inst.engine))
                except Exception:
                    pass
        self._reclaim_pos = len(insts)

    def sb(self, name, shape, dtype):
        self._name_i += 1
        return self._scopes[-1].enter_context(
            self.nc.sbuf_tensor(f"{name}_{self._name_i}", shape, dtype))

    def ps(self, name, shape=(128, 512), dtype=F32):
        self._name_i += 1
        return self._scopes[-1].enter_context(
            self.nc.psum_tensor(f"{name}_{self._name_i}", list(shape), dtype))

    def dram(self, name, shape, dtype, kind="Internal"):
        if name in DEBUG_TAPS:
            kind = "ExternalOutput"
        return self.nc.dram_tensor(name, list(shape), dtype, kind=kind).ap()


def rv(reg, maxv):
    return RuntimeValue(reg, min_val=0, max_val=maxv)


@contextlib.contextmanager
def loop(ctx, eng, n, lvl=0):
    if n == 1:
        yield 0
        return
    reg = ctx.regs[eng.engine]["loops"][lvl]
    b = eng.bass
    loop_id = b.next_id()
    loop_start = f"lp_{loop_id}_loop"
    loop_end = f"lp_{loop_id}_end"
    engines = OrderedEngineSet([eng.engine])
    eng.reg_mov(reg, 0)
    b.br(loop_start, engines=engines)
    with b.body(loop_start, valid_engines=engines):
        yield rv(reg, n - 1)
        eng.reg_add(reg, reg, 1)
        b.br_lt(reg, n, on_true=loop_start, on_false=loop_end, engines=engines)
    b.switch_bb(loop_end)


def combine_idx(ctx, eng, e, coef, i, slot=0, maxv=None):
    if isinstance(e, int):
        if e == 0:
            return i
        if isinstance(i, int):
            return e * coef + i
    reg = ctx.regs[eng.engine]["idx"][slot]
    eng.reg_mov(reg, e)
    if coef != 1:
        eng.reg_mul(reg, reg, coef)
    eng.reg_add(reg, reg, i)
    return rv(reg, maxv if maxv is not None else (1 << 30))


class Walker:

    def __init__(self, ctx, eng, sem, slot):
        self.eng = eng
        self.sem = sem
        self.reg = ctx.regs[eng.engine]["walk"][slot]
        self.cur = 0

    def start(self, base, terms=()):
        eng, reg = self.eng, self.reg
        first = True
        for coef, idx in terms:
            if isinstance(idx, int):
                base += coef * idx
                continue
            if first:
                eng.reg_mov(reg, idx)
                if coef != 1:
                    eng.reg_mul(reg, reg, coef)
                first = False
            else:
                if coef != 1:
                    raise NotImplementedError("second term must have coef 1")
                eng.reg_add(reg, reg, idx)
        if first:
            eng.reg_mov(reg, base)
        elif base:
            eng.reg_add(reg, reg, base)
        self.cur = 0

    def wait(self, off):
        if off < self.cur:
            return
        if off > self.cur:
            self.eng.reg_add(self.reg, self.reg, off - self.cur)
            self.cur = off
        self.eng.wait_ge(self.sem, self.reg)


class Seq:

    def __init__(self, ctx):
        self.ctx = ctx
        self.steps = []

    def op(self, eng, fn, inc=1):
        self.steps.append((eng, fn, inc))

    def dma(self, fn):
        self.steps.append(("gpsimd", fn, 16))

    def dma_s(self, fn):
        self.steps.append(("sync", fn, 16))

    def emit(self):
        ctx = self.ctx
        sem = ctx.sq
        base = sem.n
        cum = [base]
        for (_, _, inc) in self.steps:
            cum.append(cum[-1] + inc)
        with ctx.nc.Block() as blk:
            for eng_name in ("sync", "tensor", "scalar", "vector", "gpsimd"):
                def make(eng_label):
                    def body(eng):
                        last_wait = -1
                        for idx, (en, fn, inc) in enumerate(self.steps):
                            if en != eng_label:
                                continue
                            if cum[idx] > base and cum[idx] != last_wait:
                                eng.wait_ge(sem.h, cum[idx])
                            fn(eng).then_inc(sem.h, inc)
                            last_wait = -1
                        return
                    return body
                getattr(blk, eng_label := eng_name)(make(eng_name))
        sem.n = cum[-1]
        with ctx.nc.Block() as blk2:
            @blk2.gpsimd
            def _(g):
                g.wait_ge(sem.h, sem.n)



def _col_layout(b):
    return np.ascontiguousarray(np.asarray(b, np.float32).reshape(-1, 128).T)


def host_prep(inputs):
    d = {}
    f32 = np.float32
    d["c_ident"] = np.eye(128, dtype=f32)
    d["c_ones_col"] = np.ones((128, 1), dtype=f32)
    d["c_ones_row"] = np.ones((1, 128), dtype=f32)
    stair = np.zeros((4, 128, 512), dtype=f32)
    kk = np.arange(128)[:, None]
    qq = np.arange(512)[None, :]
    for j in range(4):
        stair[j] = ((kk + 128 * j) <= qq).astype(f32)
    d["c_stair"] = np.ascontiguousarray(stair.reshape(512, 512))

    w_qkv = np.asarray(inputs["L0_attn_w_qkv"], f32)
    wq = np.empty((D, D), f32); wk = np.empty((D, D), f32); wv = np.empty((D, D), f32)
    for h in range(NH):
        base = h * 192
        wq[:, h * 64:(h + 1) * 64] = w_qkv[:, base + 0:base + 192:3]
        wk[:, h * 64:(h + 1) * 64] = w_qkv[:, base + 1:base + 192:3]
        wv[:, h * 64:(h + 1) * 64] = w_qkv[:, base + 2:base + 192:3]
    d["l0a_wq"], d["l0a_wk"], d["l0a_wv"] = wq, wk, wv
    d["l0a_wo"] = np.asarray(inputs["L0_attn_w_proj"], f32)
    d["l0a_bo"] = _col_layout(inputs["L0_attn_b_proj"])
    d["l0a_ln_g"] = _col_layout(inputs["L0_attn_ln_scale"])
    d["l0a_ln_b"] = _col_layout(inputs["L0_attn_ln_bias"])

    d["l0f_lin0_w"] = np.asarray(inputs["L0_ffn_lin0_w"], f32)
    d["l0f_lin0_b"] = _col_layout(inputs["L0_ffn_lin0_b"])
    d["l0f_sw1_w1"] = np.asarray(inputs["L0_ffn_swi1_w1"], f32)
    d["l0f_sw1_b1"] = _col_layout(inputs["L0_ffn_swi1_b1"])
    d["l0f_sw1_w2"] = np.asarray(inputs["L0_ffn_swi1_w2"], f32)
    d["l0f_sw1_b2"] = _col_layout(inputs["L0_ffn_swi1_b2"])
    d["l0f_gwT"] = np.ascontiguousarray(np.asarray(inputs["L0_ffn_gate_w"], f32).T)
    d["l0f_ew1"] = np.asarray(inputs["L0_ffn_exp_w1"], f32).reshape(8 * D, D)
    d["l0f_ew3"] = np.asarray(inputs["L0_ffn_exp_w3"], f32).reshape(8 * D, D)
    d["l0f_ew2"] = np.asarray(inputs["L0_ffn_exp_w2"], f32).reshape(8 * D, D)
    d["l0f_shw1"] = np.asarray(inputs["L0_ffn_sh_w1"], f32)
    d["l0f_shw3"] = np.asarray(inputs["L0_ffn_sh_w3"], f32)
    d["l0f_shw2"] = np.asarray(inputs["L0_ffn_sh_w2"], f32)
    d["l0f_lin1_w"] = np.asarray(inputs["L0_ffn_lin1_w"], f32)
    d["l0f_lin1_b"] = _col_layout(inputs["L0_ffn_lin1_b"])
    d["l0f_sw2_w1"] = np.asarray(inputs["L0_ffn_swi2_w1"], f32)
    d["l0f_sw2_b1"] = _col_layout(inputs["L0_ffn_swi2_b1"])
    d["l0f_sw2_w2"] = np.asarray(inputs["L0_ffn_swi2_w2"], f32)
    d["l0f_sw2_b2"] = _col_layout(inputs["L0_ffn_swi2_b2"])

    for k in ["W_dq", "W_uq", "W_dkv", "W_uk", "W_uv", "W_o"]:
        d["l1a_" + k] = np.asarray(inputs["L1_attn_" + k], f32)

    d["l1f_Wq"] = np.asarray(inputs["L1_ffn_Wq"], f32).reshape(4 * D, D)
    d["l1f_Wk"] = np.asarray(inputs["L1_ffn_Wk"], f32).reshape(4 * D, D)
    d["l1f_Wv"] = np.asarray(inputs["L1_ffn_Wv"], f32).reshape(4 * D, D)
    d["l1f_bq"] = np.asarray(inputs["L1_ffn_bq"], f32)
    d["l1f_bk"] = np.asarray(inputs["L1_ffn_bk"], f32)
    d["l1f_bv"] = np.asarray(inputs["L1_ffn_bv"], f32)
    d["l1f_Wg1"] = np.asarray(inputs["L1_ffn_Wg1"], f32)
    d["l1f_bg1"] = np.repeat(np.asarray(inputs["L1_ffn_bg1"], f32).reshape(1, 128), 2, 0)
    d["l1f_Wg2p"] = np.zeros((128, 4), f32)
    d["l1f_Wg2p"][:, :] = np.asarray(inputs["L1_ffn_Wg2"], f32)
    d["l1f_bg2"] = np.repeat(np.asarray(inputs["L1_ffn_bg2"], f32).reshape(1, 4), 2, 0)
    d["l1f_Wo"] = np.asarray(inputs["L1_ffn_Wo"], f32)
    d["l1f_bo"] = _col_layout(inputs["L1_ffn_bo"])

    for k in ["Wq", "Wk", "Wv", "Wo"]:
        d["l2a_" + k + "T"] = np.ascontiguousarray(np.asarray(inputs["L2_attn_" + k], f32).T)
    xsl = (2.0 ** 8) ** (1.0 / NH)
    slopes = np.array([1.0 / xsl ** (i + 1) for i in range(NH)], f32)
    al = np.zeros((NH * 128, 16), f32)
    for h in range(NH):
        for kt in range(16):
            al[h * 128:(h + 1) * 128, kt] = -slopes[h] * (kt * 128 + np.arange(128))
    d["c_alibi"] = al
    d["c_ones_row_t"] = np.ones((1, TCH), f32)
    d["c_half_row"] = np.full((1, TCH), 0.5, f32)

    upw = np.asarray(inputs["L2_ffn_up_weights"], f32)
    dnw = np.asarray(inputs["L2_ffn_down_weights"], f32)
    up_rows = np.stack([np.stack([upw[s, :, 0, 0], upw[s, :, 0, 1],
                                  upw[s, :, 1, 0], upw[s, :, 1, 1]])
                        for s in range(12)])
    dn_rows = np.stack([np.stack([dnw[s, :, 0, 0], dnw[s, :, 1, 0],
                                  dnw[s, :, 0, 1], dnw[s, :, 1, 1]])
                        for s in range(11, -1, -1)])
    d["l2f_uprows"] = np.ascontiguousarray(up_rows.reshape(12 * 4, 2048))
    d["l2f_dnrows"] = np.ascontiguousarray(dn_rows.reshape(12 * 4, 2048))
    d["l2f_wexp"] = np.asarray(inputs["L2_ffn_W_exp"], f32)
    d["l2f_bexp"] = np.asarray(inputs["L2_ffn_b_exp"], f32).reshape(1, 4096)
    d["l2f_upbias"] = np.asarray(inputs["L2_ffn_up_bias"], f32).reshape(1, 4096)
    d["l2f_wconT"] = np.ascontiguousarray(np.asarray(inputs["L2_ffn_W_con"], f32).T)
    d["l2f_bcon"] = _col_layout(inputs["L2_ffn_b_con"])
    d["l2f_dbias"] = np.asarray(inputs["L2_ffn_down_bias"], f32).reshape(1, 1024)

    for k in ["wq", "wk", "wv", "wo"]:
        d["l3a_" + k] = np.asarray(inputs["L3_attn_" + k], f32)
    d["l3a_bq"] = _col_layout(inputs["L3_attn_bq"])
    d["l3a_bk"] = _col_layout(inputs["L3_attn_bk"])
    d["l3a_bv"] = np.asarray(inputs["L3_attn_bv"], f32).reshape(1, 1024)
    d["l3a_bo"] = _col_layout(inputs["L3_attn_bo"])
    d["l3a_g"] = _col_layout(inputs["L3_attn_gamma"])
    d["l3a_b"] = _col_layout(inputs["L3_attn_beta"])

    def _colset(b):
        n, Hh = b.shape
        return np.ascontiguousarray(
            np.asarray(b, f32).reshape(n, Hh // 128, 128).transpose(0, 2, 1)
            .reshape(n * 128, Hh // 128))
    d["l3f_gw"] = np.asarray(inputs["L3_ffn_gate_w"], f32)
    d["l3f_gb"] = np.asarray(inputs["L3_ffn_gate_b"], f32).reshape(1, 8)
    d["l3f_rw1"] = np.asarray(inputs["L3_ffn_rw1"], f32).reshape(8 * D, 2048)
    d["l3f_rw2"] = np.asarray(inputs["L3_ffn_rw2"], f32).reshape(8 * D, 2048)
    d["l3f_rw3"] = np.asarray(inputs["L3_ffn_rw3"], f32).reshape(8 * 2048, D)
    d["l3f_rb1"] = _colset(np.asarray(inputs["L3_ffn_rb1"], f32))
    d["l3f_rb2"] = _colset(np.asarray(inputs["L3_ffn_rb2"], f32))
    d["l3f_rb3"] = np.asarray(inputs["L3_ffn_rb3"], f32)
    d["l3f_sw1"] = np.asarray(inputs["L3_ffn_sw1"], f32).reshape(2 * D, 2048)
    d["l3f_sw2"] = np.asarray(inputs["L3_ffn_sw2"], f32).reshape(2 * D, 2048)
    d["l3f_sw3"] = np.asarray(inputs["L3_ffn_sw3"], f32).reshape(2 * 2048, D)
    d["l3f_sb1"] = _colset(np.asarray(inputs["L3_ffn_sb1"], f32))
    d["l3f_sb2"] = _colset(np.asarray(inputs["L3_ffn_sb2"], f32))
    d["l3f_sb3"] = np.asarray(inputs["L3_ffn_sb3"], f32)
    return d


def per_core_input(inputs, core):
    z = np.asarray(inputs["L0_attn_z"], np.float32)
    zc = z[core * B_PER_CORE:(core + 1) * B_PER_CORE]
    return np.ascontiguousarray(zc.reshape(T, D).T)


SEM_FLOOR = 4096
DMA_FLOOR = 4096 * 16


def fori(eng, n):
    if n == 1:
        return contextlib.nullcontext(0)
    return eng.Fori(0, n)


def load_consts(ctx, d):
    nc = ctx.nc
    with nc.Block() as blk0:
        @blk0.gpsimd
        def _(g):
            g.sem_inc(ctx.dmg.h, DMA_FLOOR)
            g.sem_inc(ctx.sq.h, SEM_FLOOR)
        @blk0.sync
        def _(s):
            s.sem_inc(ctx.dma.h, DMA_FLOOR)
        @blk0.tensor
        def _(t):
            t.sem_inc(ctx.mm.h, SEM_FLOOR)
        @blk0.vector
        def _(v):
            v.sem_inc(ctx.ev.h, SEM_FLOOR)
        @blk0.scalar
        def _(s):
            s.sem_inc(ctx.sc.h, SEM_FLOOR)
    ctx.dmg.n += DMA_FLOOR
    ctx.dma.n += DMA_FLOOR
    ctx.sq.n += SEM_FLOOR
    ctx.mm.n += SEM_FLOOR
    ctx.ev.n += SEM_FLOOR
    ctx.sc.n += SEM_FLOOR
    c = ctx.consts
    c["ones_col"] = ctx.sb("ones_col", [128, 1], F32)
    c["ones_col_bf"] = ctx.sb("ones_col_bf", [128, 1], BF16)
    c["ones_row"] = ctx.sb("ones_row", [1, 128], F32)
    c["ident"] = ctx.sb("ident", [128, 128], F32)
    c["stair"] = ctx.sb("stair", [128, 4 * 512], BF16)
    with nc.Block() as blk:
        @blk.gpsimd
        def _(g):
            g.dma_start(c["ones_col"][:], d["c_ones_col"]).then_inc(ctx.dmg.h, 16)
            g.dma_start(c["ones_col_bf"][:], d["c_ones_col"]).then_inc(ctx.dmg.h, 16)
            g.dma_start(c["ones_row"][:], d["c_ones_row"]).then_inc(ctx.dmg.h, 16)
            g.dma_start(c["ident"][:], d["c_ident"]).then_inc(ctx.dmg.h, 16)
            g.dma_start(c["stair"][:].rearrange("p (j t) -> p j t", j=4),
                        d["c_stair"].rearrange("(j p) t -> p j t", p=128)
                        ).then_inc(ctx.dmg.h, 16)
            g.wait_ge(ctx.dmg.h, ctx.dmg.inc(16 * 5))



def emit_cast_copy(ctx, dst_dram, src_dram):
    nc = ctx.nc
    with nc.Block() as blk:
        @blk.gpsimd
        def _(g):
            g.dma_start(dst_dram, src_dram).then_inc(ctx.dmg.h, 16)
            g.wait_ge(ctx.dmg.h, ctx.dmg.inc(16))


def emit_ln(ctx, x_dram, out_bf_dram=None, out_f32_dram=None, gamma=None, beta=None,
            res_dram=None):
    nc = ctx.nc
    have_res = res_dram is not None
    xs = ctx.sb("ln_x", [128, NCT * TCH], F32)
    xb = ctx.sb("ln_xb", [128, NCT * TCH], BF16)
    sq = ctx.sb("ln_sq", [128, NCT * TCH], BF16)
    rows = ctx.sb("ln_rows", [1, 6 * TCH], F32)
    ob = ctx.sb("ln_ob", [128, NCT * TCH], BF16) if out_bf_dram is not None else None
    of = ctx.sb("ln_of", [128, NCT * TCH], F32) if out_f32_dram is not None else None
    tmp = ctx.sb("ln_tmp", [128, NCT * TCH], F32)
    PS_S, PS_Q, PS_R, PS_M = (ctx.ps("ln_ps"), ctx.ps("ln_pq"), ctx.ps("ln_pr"),
                              ctx.ps("ln_pm"))

    x3 = x_dram.rearrange("(c p) t -> p c t", p=128)
    r3 = res_dram.rearrange("(c p) t -> p c t", p=128) if have_res else None
    ob3 = out_bf_dram.rearrange("(c p) t -> p c t", p=128) if out_bf_dram is not None else None
    of3 = out_f32_dram.rearrange("(c p) t -> p c t", p=128) if out_f32_dram is not None else None

    d0, m0, e0, s0, g0 = ctx.dma.n, ctx.mm.n, ctx.ev.n, ctx.sc.n, ctx.dmg.n
    LD = 2 if have_res else 1
    RESOFF = 1 if have_res else 0
    EV = RESOFF + 2 + NCT
    SC = 2
    MM = 2
    ST = (1 if ob is not None else 0) + (1 if of is not None else 0)

    def r_row(j, w=TCH):
        return rows[0:1, j * TCH:j * TCH + w]

    with nc.Block() as blk:
        @blk.sync
        def _(sync):
            wev = Walker(ctx, sync, ctx.ev.h, 0)
            with loop(ctx, sync, NCHUNK) as i:
                wev.start(e0, [(EV, i)])
                wev.wait(0)
                sync.dma_start(xs[:].rearrange("p (c t) -> p c t", c=NCT),
                               x3[:, :, bass.ds(i * TCH, TCH)]).then_inc(ctx.dma.h, 16)
                if have_res:
                    sync.dma_start(tmp[:].rearrange("p (c t) -> p c t", c=NCT),
                                   r3[:, :, bass.ds(i * TCH, TCH)]).then_inc(ctx.dma.h, 16)

        @blk.vector
        def _(v):
            wdma = Walker(ctx, v, ctx.dma.h, 0)
            wmm = Walker(ctx, v, ctx.mm.h, 1)
            wsc = Walker(ctx, v, ctx.sc.h, 2)
            wdg = Walker(ctx, v, ctx.dmg.h, 3)
            with loop(ctx, v, NCHUNK) as i:
                wdma.start(d0 + 16 * LD, [(16 * LD, i)])
                wdma.wait(0)
                if have_res:
                    v.tensor_add(xs[:], xs[:], tmp[:]).then_inc(ctx.ev.h, 1)
                wmm.start(m0 + 1, [(MM, i)])
                wmm.wait(0)
                v.tensor_scalar(r_row(0), PS_S[0:1, :], 1.0 / D, None, OP.mult)
                v.tensor_scalar(r_row(1), PS_Q[0:1, :], 1.0 / D, None, OP.mult)
                v.tensor_mul(r_row(2), r_row(0), r_row(0))
                v.tensor_tensor(r_row(2), r_row(1), r_row(2), OP.subtract)
                v.tensor_scalar(r_row(2), r_row(2), EPS, None, OP.add)
                v.reciprocal(r_row(3), r_row(2)).then_inc(ctx.ev.h, 1)
                wsc.start(s0 + 2, [(SC, i)])
                wsc.wait(0)
                v.tensor_mul(r_row(5), r_row(0), r_row(4))
                v.tensor_scalar(r_row(5), r_row(5), -1.0, None, OP.mult
                                ).then_inc(ctx.ev.h, 1)
                wmm.wait(1)
                if ST:
                    wdg.start(g0, [(ST * 16, i)])
                    wdg.wait(0)
                for ct in range(NCT):
                    sl = slice(ct * TCH, (ct + 1) * TCH)
                    v.tensor_mul(tmp[:, sl], xs[:, sl], PS_R[:, :])
                    v.tensor_add(tmp[:, sl], tmp[:, sl], PS_M[:, :])
                    last = None
                    if gamma is not None:
                        if of is not None:
                            last = v.tensor_scalar(of[:, sl], tmp[:, sl],
                                                   gamma[:, ct:ct + 1],
                                                   beta[:, ct:ct + 1], OP.mult, OP.add)
                        if ob is not None:
                            last = v.tensor_scalar(ob[:, sl], tmp[:, sl],
                                                   gamma[:, ct:ct + 1],
                                                   beta[:, ct:ct + 1], OP.mult, OP.add)
                    else:
                        if of is not None:
                            last = v.tensor_copy(of[:, sl], tmp[:, sl])
                        if ob is not None:
                            last = v.tensor_copy(ob[:, sl], tmp[:, sl])
                    (last if last is not None else v.engine_nop()).then_inc(ctx.ev.h, 1)

        @blk.scalar
        def _(s):
            wev = Walker(ctx, s, ctx.ev.h, 0)
            wdma = Walker(ctx, s, ctx.dma.h, 1)
            with loop(ctx, s, NCHUNK) as i:
                if have_res:
                    wev.start(e0 + 1, [(EV, i)])
                    wev.wait(0)
                else:
                    wdma.start(d0 + 16 * LD, [(16 * LD, i)])
                    wdma.wait(0)
                s.activation(xb[:], xs[:], AF.Copy)
                s.activation(sq[:], xs[:], AF.Square).then_inc(ctx.sc.h, 1)
                if have_res:
                    wev.wait(1)
                else:
                    wev.start(e0 + RESOFF + 1, [(EV, i)])
                    wev.wait(0)
                s.activation(r_row(4), r_row(3), AF.Sqrt).then_inc(ctx.sc.h, 1)

        @blk.tensor
        def _(t):
            onebf = ctx.consts["ones_col_bf"]
            oner = ctx.consts["ones_row"]
            wsc = Walker(ctx, t, ctx.sc.h, 0)
            wev = Walker(ctx, t, ctx.ev.h, 1)
            with loop(ctx, t, NCHUNK) as i:
                wsc.start(s0 + 1, [(SC, i)])
                wsc.wait(0)
                wev.start(e0, [(EV, i)])
                wev.wait(0)
                for ct in range(NCT):
                    t.matmul(PS_S[0:1, :], onebf[:], xb[:, bass.ts(ct, TCH)],
                             start=(ct == 0), stop=(ct == NCT - 1))
                mm_i = None
                for ct in range(NCT):
                    mm_i = t.matmul(PS_Q[0:1, :], onebf[:], sq[:, bass.ts(ct, TCH)],
                                    start=(ct == 0), stop=(ct == NCT - 1))
                mm_i.then_inc(ctx.mm.h, 1)
                wev.wait(RESOFF + 2)
                t.matmul(PS_R[:, :], oner[:], r_row(4), start=True, stop=True)
                t.matmul(PS_M[:, :], oner[:], r_row(5), start=True,
                         stop=True).then_inc(ctx.mm.h, 1)

        @blk.gpsimd
        def _(g):
            wev = Walker(ctx, g, ctx.ev.h, 0)
            with loop(ctx, g, NCHUNK) as i:
                wev.start(e0 + EV, [(EV, i)])
                wev.wait(0)
                if ob is not None:
                    g.dma_start(ob3[:, :, bass.ds(i * TCH, TCH)],
                                ob[:].rearrange("p (c t) -> p c t", c=NCT)
                                ).then_inc(ctx.dmg.h, 16)
                if of is not None:
                    g.dma_start(of3[:, :, bass.ds(i * TCH, TCH)],
                                of[:].rearrange("p (c t) -> p c t", c=NCT)
                                ).then_inc(ctx.dmg.h, 16)
            if ST:
                g.wait_ge(ctx.dmg.h, g0 + NCHUNK * ST * 16)

    ctx.dma.n = d0 + NCHUNK * 16 * LD
    ctx.dmg.n = g0 + NCHUNK * ST * 16
    ctx.mm.n = m0 + NCHUNK * MM
    ctx.ev.n = e0 + NCHUNK * EV
    ctx.sc.n = s0 + NCHUNK * SC



def emit_linear(ctx, x_dram, w_dram, out_dram, *, C=D, F=D, Tn=T,
                bias_dram=None, act=None, scale=1.0,
                extra_dram=None, extra_op=None, extra_dtype=BF16,
                out_dtype=BF16, out2_dram=None, bias2_row_dram=None,
                ones_row_t_dram=None):
    nc = ctx.nc
    ncc, nf = C // 128, F // 128
    nch = Tn // TCH
    assert nch % 2 == 0
    NB = min(nf, 4)
    have_extra = extra_dram is not None
    dbl_ex = have_extra and extra_dtype == BF16
    func = act if act is not None else AF.Identity
    wsb = ctx.sb("lin_w", [128, ncc * F], BF16)
    bsb = ctx.sb("lin_b", [128, nf], F32) if bias_dram is not None else None
    xs2 = [ctx.sb("lin_x", [128, ncc * TCH], BF16) for _ in range(2)]
    os_ = ctx.sb("lin_o", [128, nf * TCH], out_dtype)
    os2 = ctx.sb("lin_o2", [128, nf * TCH], BF16) if out2_dram is not None else None
    b2s = ctx.sb("lin_b2", [1, F], BF16) if bias2_row_dram is not None else None
    orow = ctx.sb("lin_or", [1, TCH], BF16) if bias2_row_dram is not None else None
    oacc = ctx.sb("lin_acc", [128, nf * TCH], F32) if have_extra else None
    ex2 = [ctx.sb("lin_ex", [128, nf * TCH], extra_dtype)
           for _ in range(2 if dbl_ex else 1)] if have_extra else None
    pbk = [ctx.ps("lin_p") for _ in range(NB)]

    x3 = x_dram.rearrange("(c p) t -> p c t", p=128)
    o3 = out_dram.rearrange("(f p) t -> p f t", p=128)
    o23 = out2_dram.rearrange("(f p) t -> p f t", p=128) if out2_dram is not None else None
    e3 = extra_dram.rearrange("(f p) t -> p f t", p=128) if have_extra else None
    w3 = w_dram.rearrange("(c p) f -> p c f", p=128)

    d0, m0, e0, s0, g0 = ctx.dma.n, ctx.mm.n, ctx.ev.n, ctx.sc.n, ctx.dmg.n
    LD = 2 if have_extra else 1
    EV = 1 if have_extra else 0
    NST = 1 + (1 if out2_dram is not None else 0)
    bofs = 16 if bias_dram is not None else 0
    WL2 = 48 if bias2_row_dram is not None else 16

    with nc.Block() as blk:
        @blk.gpsimd
        def _(g):
            g.dma_start(wsb[:].rearrange("p (c f) -> p c f", c=ncc), w3
                        ).then_inc(ctx.dmg.h, 16)
            if bias2_row_dram is not None:
                g.dma_start(b2s[:], bias2_row_dram).then_inc(ctx.dmg.h, 16)
                g.dma_start(orow[:], ones_row_t_dram).then_inc(ctx.dmg.h, 16)
            wk = (Walker(ctx, g, ctx.ev.h, 0) if have_extra
                  else Walker(ctx, g, ctx.sc.h, 0))
            with loop(ctx, g, nch) as i:
                if have_extra:
                    wk.start(e0 + EV, [(EV, i)])
                else:
                    wk.start(s0 + nf, [(nf, i)])
                wk.wait(0)
                g.dma_start(o3[:, :, bass.ds(i * TCH, TCH)],
                            os_[:].rearrange("p (f t) -> p f t", f=nf)
                            ).then_inc(ctx.dmg.h, 16)
                if out2_dram is not None:
                    g.dma_start(o23[:, :, bass.ds(i * TCH, TCH)],
                                os2[:].rearrange("p (f t) -> p f t", f=nf)
                                ).then_inc(ctx.dmg.h, 16)
            g.wait_ge(ctx.dmg.h, g0 + WL2 + nch * NST * 16)

        @blk.sync
        def _(sync):
            if bias_dram is not None:
                sync.dma_start(bsb[:], bias_dram).then_inc(ctx.dma.h, 16)
            wmm = Walker(ctx, sync, ctx.mm.h, 0)
            wev = Walker(ctx, sync, ctx.ev.h, 1)
            with loop(ctx, sync, nch // 2) as i2:
                for u in range(2):
                    wmm.start(m0 - nf, [(2 * nf, i2)])
                    wmm.wait(u * nf)
                    sync.dma_start(xs2[u][:].rearrange("p (c t) -> p c t", c=ncc),
                                   x3[:, :, bass.ds((i2 * 2 + u) * TCH, TCH)]
                                   ).then_inc(ctx.dma.h, 16)
                    if have_extra:
                        wev.start(e0 - EV if dbl_ex else e0, [(2 * EV, i2)])
                        wev.wait(u * EV if dbl_ex else u * EV + 0)
                        sync.dma_start(
                            ex2[u if dbl_ex else 0][:].rearrange(
                                "p (f t) -> p f t", f=nf),
                            e3[:, :, bass.ds((i2 * 2 + u) * TCH, TCH)]
                            ).then_inc(ctx.dma.h, 16)

        @blk.tensor
        def _(t):
            t.wait_ge(ctx.dmg.h, g0 + WL2)
            wdma = Walker(ctx, t, ctx.dma.h, 0)
            wsc = Walker(ctx, t, ctx.sc.h, 1)
            with loop(ctx, t, nch // 2) as i2:
                for u in range(2):
                    wdma.start(d0 + bofs + 16, [(2 * LD * 16, i2)])
                    wdma.wait(u * LD * 16)
                    wsc.start(s0 - NB + 1 + u * nf, [(2 * nf, i2)])
                    for ft in range(nf):
                        wsc.wait(ft)
                        pb = pbk[ft % NB]
                        for ct in range(ncc):
                            mmi = t.matmul(pb[:],
                                           wsb[:, ct * F + ft * 128:ct * F + (ft + 1) * 128],
                                           xs2[u][:, bass.ts(ct, TCH)],
                                           start=(ct == 0),
                                           stop=(ct == ncc - 1) and bias2_row_dram is None)
                        if bias2_row_dram is not None:
                            mmi = t.matmul(pb[:], b2s[0:1, ft * 128:(ft + 1) * 128],
                                           orow[:], start=False, stop=True)
                        mmi.then_inc(ctx.mm.h, 1)

        @blk.scalar
        def _(s):
            if bias_dram is not None:
                s.wait_ge(ctx.dma.h, d0 + 16)
            wdg = Walker(ctx, s, ctx.dmg.h, 0)
            wmm = Walker(ctx, s, ctx.mm.h, 1)
            with loop(ctx, s, nch) as i:
                if not have_extra:
                    wdg.start(g0 + WL2, [(NST * 16, i)])
                    wdg.wait(0)
                wmm.start(m0 + 1, [(nf, i)])
                for ft in range(nf):
                    wmm.wait(ft)
                    dst = oacc if have_extra else os_
                    s.activation(dst[:, bass.ts(ft, TCH)], pbk[ft % NB][:],
                                 func, bias=(bsb[:, ft:ft + 1] if bsb is not None else 0.0),
                                 scale=scale).then_inc(ctx.sc.h, 1)

        if have_extra:
            @blk.vector
            def _(v):
                wsc = Walker(ctx, v, ctx.sc.h, 0)
                wdma = Walker(ctx, v, ctx.dma.h, 1)
                wdg = Walker(ctx, v, ctx.dmg.h, 2)
                with loop(ctx, v, nch // 2) as i2:
                    for u in range(2):
                        wsc.start(s0 + (u + 1) * nf, [(2 * nf, i2)])
                        wsc.wait(0)
                        wdma.start(d0 + bofs + (u + 1) * LD * 16, [(2 * LD * 16, i2)])
                        wdma.wait(0)
                        wdg.start(g0 + WL2 + u * NST * 16, [(2 * NST * 16, i2)])
                        wdg.wait(0)
                        exb = ex2[u if dbl_ex else 0][:]
                        if out2_dram is not None:
                            v.tensor_tensor(os_[:], oacc[:], exb, extra_op)
                            v.tensor_copy(os2[:], os_[:]).then_inc(ctx.ev.h, 1)
                        else:
                            v.tensor_tensor(os_[:], oacc[:], exb, extra_op
                                            ).then_inc(ctx.ev.h, 1)

    ctx.dma.n = d0 + bofs + nch * LD * 16
    ctx.dmg.n = g0 + WL2 + nch * NST * 16
    ctx.mm.n = m0 + nch * nf
    ctx.sc.n = s0 + nch * nf
    ctx.ev.n = e0 + nch * EV



def emit_linear_tok(ctx, x_dram, w_dram, vn_dram, *, C=D, F=D, Tn=T,
                    bias_row_dram=None):
    nc = ctx.nc
    ncc, nf5 = C // 128, F // 512
    nhh = F // 64
    nch = Tn // TCH
    have_b = bias_row_dram is not None
    wsb = ctx.sb("lt_w", [128, ncc * F], BF16)
    xs = ctx.sb("lt_x", [128, ncc * TCH], BF16)
    vnb = ctx.sb("lt_v", [128, 4 * nhh * 72], BF16)
    pbk = [ctx.ps("lt_p") for _ in range(4)]
    brow = ctx.sb("lt_br", [1, F], F32) if have_b else None
    bbc = ctx.sb("lt_bb", [128, F], F32) if have_b else None
    pbb = [ctx.ps("lt_pb") for _ in range(2)] if have_b else None

    x3 = x_dram.rearrange("(c p) t -> p c t", p=128)
    w3 = w_dram.rearrange("(c p) f -> p c f", p=128)
    vn3 = vn_dram.rearrange("(u p) f -> p u f", p=128)

    d0, m0, e0, s0, g0 = ctx.dma.n, ctx.mm.n, ctx.ev.n, ctx.sc.n, ctx.dmg.n
    GRP = 4 * nf5
    EV = GRP + 1
    MOF = nf5 if have_b else 0
    EOF_ = 1 if have_b else 0
    WLD = (32 if have_b else 16)

    with nc.Block() as blk:
        @blk.gpsimd
        def _(g):
            g.dma_start(wsb[:].rearrange("p (c f) -> p c f", c=ncc), w3
                        ).then_inc(ctx.dmg.h, 16)
            if have_b:
                g.dma_start(brow[:], bias_row_dram).then_inc(ctx.dmg.h, 16)
            wev = Walker(ctx, g, ctx.ev.h, 0)
            with loop(ctx, g, nch) as i:
                wev.start(e0 + EOF_ + EV, [(EV, i)])
                wev.wait(0)
                g.dma_start(vn3[:, bass.ds(i * 4, 4), :],
                            vnb[:].rearrange("p (u f) -> p u f", u=4)
                            ).then_inc(ctx.dmg.h, 16)
            g.wait_ge(ctx.dmg.h, g0 + WLD + nch * 16)

        @blk.sync
        def _(sync):
            wmm = Walker(ctx, sync, ctx.mm.h, 0)
            with loop(ctx, sync, nch) as i:
                wmm.start(m0 + MOF, [(GRP, i)])
                wmm.wait(0)
                sync.dma_start(xs[:].rearrange("p (c t) -> p c t", c=ncc),
                               x3[:, :, bass.ds(i * TCH, TCH)]).then_inc(ctx.dma.h, 16)

        @blk.tensor
        def _(t):
            t.wait_ge(ctx.dmg.h, g0 + WLD)
            if have_b:
                assert nf5 <= 2
                for j in range(nf5):
                    t.matmul(pbb[j % 2][:], ctx.consts["ones_row"][:],
                             brow[0:1, bass.ts(j, 512)], start=True, stop=True
                             ).then_inc(ctx.mm.h, 1)
            wdma = Walker(ctx, t, ctx.dma.h, 0)
            wev = Walker(ctx, t, ctx.ev.h, 1)
            with loop(ctx, t, nch) as i:
                wdma.start(d0 + 16, [(16, i)])
                wdma.wait(0)
                wev.start(e0 + EOF_, [(EV, i)])
                wev.wait(0)
                for tt in range(4):
                    for fh in range(nf5):
                        r = tt * nf5 + fh
                        if r >= 4:
                            wev.wait(r - 2)
                        pb = pbk[r % 4]
                        for ct in range(ncc):
                            mmi = t.matmul(pb[:], xs[:, ct * TCH + tt * 128:ct * TCH + (tt + 1) * 128],
                                           wsb[:, ct * F + fh * 512:ct * F + (fh + 1) * 512],
                                           start=(ct == 0), stop=(ct == ncc - 1))
                        mmi.then_inc(ctx.mm.h, 1)

        @blk.vector
        def _(v):
            if have_b:
                wmm0 = Walker(ctx, v, ctx.mm.h, 2)
                wmm0.start(m0 + nf5, [])
                wmm0.wait(0)
                for j in range(nf5):
                    v.tensor_copy(bbc[:, bass.ts(j, 512)], pbb[j % 2][:])
                v.engine_nop().then_inc(ctx.ev.h, 1)
            wdg = Walker(ctx, v, ctx.dmg.h, 0)
            wmm = Walker(ctx, v, ctx.mm.h, 1)
            with loop(ctx, v, nch) as i:
                wdg.start(g0 + WLD, [(16, i)])
                wdg.wait(0)
                wmm.start(m0 + MOF + 1, [(GRP, i)])
                v.memset(vnb[:], 1.0).then_inc(ctx.ev.h, 1)
                for tt in range(4):
                    for fh in range(nf5):
                        r = tt * nf5 + fh
                        wmm.wait(r)
                        dst = vnb[:, tt * nhh * 72 + fh * 8 * 72:
                                  tt * nhh * 72 + (fh * 8 + 8) * 72
                                  ].rearrange("p (h f) -> p h f", h=8)[:, :, 0:64]
                        if have_b:
                            v.tensor_tensor(dst, pbk[r % 4][:].rearrange(
                                "p (h f) -> p h f", h=8),
                                bbc[:, bass.ts(fh, 512)].rearrange(
                                "p (h f) -> p h f", h=8), OP.add
                                ).then_inc(ctx.ev.h, 1)
                        else:
                            v.tensor_copy(dst, pbk[r % 4][:].rearrange(
                                "p (h f) -> p h f", h=8)).then_inc(ctx.ev.h, 1)

    ctx.dma.n = d0 + nch * 16
    ctx.dmg.n = g0 + WLD + nch * 16
    ctx.mm.n = m0 + MOF + nch * GRP
    ctx.ev.n = e0 + EOF_ + nch * EV
    ctx.sc.n = s0



def emit_attention(ctx, qT_sc, kT_sc, vn_sc, ctxT_sc, *, nbatch=B_PER_CORE, nh=NH,
                   causal=False, alibi_dram=None, scale=1.0 / 8.0, sq=S):
    nc = ctx.nc
    nkt = sq // 128
    nqc = sq // TCH
    stair = ctx.consts["stair"]
    qh = [ctx.sb("at_q", [64, sq], BF16) for _ in range(2)]
    kh = [ctx.sb("at_k", [64, sq], BF16) for _ in range(2)]
    vh = [ctx.sb("at_v", [128, nkt * 72], BF16) for _ in range(2)]
    ac = [ctx.sb("at_a", [128, nkt], F32) for _ in range(2)] if alibi_dram is not None else None
    pP = [ctx.sb("at_p", [128, TCH], BF16) for _ in range(2)]
    rr = ctx.sb("at_rr", [1, TCH], F32)
    rb = ctx.sb("at_rb", [64, TCH], F32)
    cb = [ctx.sb("at_c", [64, sq], BF16) for _ in range(2)]
    PS_s = [ctx.ps("at_ps") for _ in range(2)]
    PS_C2 = [ctx.ps("at_pc") for _ in range(2)]
    PS_D = ctx.ps("at_pd")

    q4 = qT_sc.rearrange("(h p) t -> p h t", p=64)
    k4 = kT_sc.rearrange("(h p) t -> p h t", p=64)
    c4 = ctxT_sc.rearrange("(h p) t -> p h t", p=64)
    vn3 = vn_sc.rearrange("(u p) f -> p u f", p=128)
    a3 = alibi_dram.rearrange("(h p) k -> p h k", p=128) if alibi_dram is not None else None

    d0, m0, e0, s0, g0 = ctx.dma.n, ctx.mm.n, ctx.ev.n, ctx.sc.n, ctx.dmg.n
    LDJ = 3 + (1 if alibi_dram is not None else 0)
    kts_per_qc = [list(range(4 * (qc + 1))) if causal else list(range(nkt))
                  for qc in range(nqc)]
    nsc_qc = [len(k) for k in kts_per_qc]
    SCBH = sum(nsc_qc)
    MMQC = [2 * n + 1 for n in nsc_qc]
    MMBH = sum(MMQC)
    NMASK = 4 if causal else 0
    EVQC = 2 + NMASK
    EVBH = nqc * EVQC

    def s_off(qc, pos):
        return sum(nsc_qc[:qc]) + pos + 1

    def _C(q):
        return 2 * sum(nsc_qc[:q]) + (q - 1 if q >= 1 else 0)

    def m_off(qc, kind, pos=0):
        n = nsc_qc[qc]
        hb = 1 if qc >= 1 else 0
        if kind == "scores":
            off = pos + 1 if pos <= 1 else 2 * pos + hb
        elif kind == "pv":
            off = (2 * n + hb) if pos == n - 1 else (2 * pos + 3 + hb)
        else:
            return _C(qc + 1) + 3 if qc <= nqc - 2 else MMBH
        return _C(qc) + off

    def e_off(qc, kind, idx=0):
        base = qc * EVQC
        if kind == "mask":
            return base + idx + 1
        return base + NMASK + (1 if kind == "recip" else 2)

    with nc.Block() as blk:
        @blk.sync
        def _(sync):
            wmm = Walker(ctx, sync, ctx.mm.h, 0)
            for b in range(nbatch):
                with loop(ctx, sync, nh // 2) as hh:
                    for u in range(2):
                        jb = (b * nh + u - 1) * MMBH
                        wmm.start(m0 + jb, [(2 * MMBH, hh)])
                        wmm.wait(0)
                        h = 2 * hh + u
                        sync.dma_start(qh[u][:], q4[:, bass.ds(h, 1),
                                                    b * sq:(b + 1) * sq].squeeze(1)
                                       ).then_inc(ctx.dma.h, 16)
                        sync.dma_start(kh[u][:], k4[:, bass.ds(h, 1),
                                                    b * sq:(b + 1) * sq].squeeze(1)
                                       ).then_inc(ctx.dma.h, 16)
                        sync.dma_start(vh[u][:].rearrange("p (u2 f) -> p u2 f", u2=nkt),
                                       vn3[:, b * nkt:(b + 1) * nkt,
                                           bass.ds(h * 72, 72)]
                                       ).then_inc(ctx.dma.h, 16)
                        if alibi_dram is not None:
                            sync.dma_start(ac[u][:], a3[:, bass.ds(h, 1), :].squeeze(1)
                                           ).then_inc(ctx.dma.h, 16)

        @blk.tensor
        def _(t):
            wdma = Walker(ctx, t, ctx.dma.h, 0)
            wev = Walker(ctx, t, ctx.ev.h, 1)
            wsc = Walker(ctx, t, ctx.sc.h, 2)
            for b in range(nbatch):
                with loop(ctx, t, nh // 2) as hh:
                    for u in range(2):
                        j = b * nh + u
                        wdma.start(d0 + (j + 1) * 16 * LDJ, [(2 * 16 * LDJ, hh)])
                        wdma.wait(0)
                        wev.start(e0 + (j - 1) * EVBH, [(2 * EVBH, hh)])
                        wsc.start(s0 + j * SCBH, [(2 * SCBH, hh)])
                        q, k, vv = qh[u], kh[u], vh[u]
                        pending_bcast = None
                        for qc in range(nqc):
                            kts = kts_per_qc[qc]
                            n = len(kts)
                            PS_C = PS_C2[qc % 2]
                            if qc >= 2:
                                wev.wait(EVBH + e_off(qc - 2, "evac"))
                            else:
                                wev.wait(e_off(nqc - 2 + qc, "evac"))
                            for pos, kt in enumerate(kts):
                                if pos >= 2:
                                    wsc.wait(s_off(qc, pos - 2))
                                elif qc >= 1:
                                    wsc.wait(s_off(qc - 1, nsc_qc[qc - 1] - 2 + pos))
                                t.matmul(PS_s[pos % 2][:], k[:, bass.ts(kt, 128)],
                                         q[:, bass.ts(qc, TCH)], start=True, stop=True
                                         ).then_inc(ctx.mm.h, 1)
                                if pos == 1 and pending_bcast is not None:
                                    wev.wait(EVBH + e_off(pending_bcast, "recip"))
                                    t.matmul(PS_D[0:64, :],
                                             ctx.consts["ones_row"][:, 0:64],
                                             rr[:], start=True, stop=True
                                             ).then_inc(ctx.mm.h, 1)
                                    pending_bcast = None
                                if pos >= 1:
                                    pv = pos - 1
                                    jj = kts[pv] - 4 * qc
                                    if causal and 0 <= jj < 4:
                                        wev.wait(EVBH + e_off(qc, "mask", jj))
                                    else:
                                        wsc.wait(s_off(qc, pv))
                                    t.matmul(PS_C[0:65, :],
                                             vv[:, kts[pv] * 72:kts[pv] * 72 + 65],
                                             pP[pv % 2][:], start=(pv == 0), stop=False
                                             ).then_inc(ctx.mm.h, 1)
                            pv = n - 1
                            jj = kts[pv] - 4 * qc
                            if causal and 0 <= jj < 4:
                                wev.wait(EVBH + e_off(qc, "mask", jj))
                            else:
                                wsc.wait(s_off(qc, pv))
                            t.matmul(PS_C[0:65, :], vv[:, kts[pv] * 72:kts[pv] * 72 + 65],
                                     pP[pv % 2][:], start=(pv == 0), stop=True
                                     ).then_inc(ctx.mm.h, 1)
                            pending_bcast = qc
                        wev.wait(EVBH + e_off(nqc - 1, "recip"))
                        t.matmul(PS_D[0:64, :], ctx.consts["ones_row"][:, 0:64],
                                 rr[:], start=True, stop=True).then_inc(ctx.mm.h, 1)

        @blk.scalar
        def _(s):
            wmm = Walker(ctx, s, ctx.mm.h, 0)
            for b in range(nbatch):
                with loop(ctx, s, nh // 2) as hh:
                    for u in range(2):
                        j = b * nh + u
                        wmm.start(m0 + j * MMBH, [(2 * MMBH, hh)])
                        for qc in range(nqc):
                            for pos, kt in enumerate(kts_per_qc[qc]):
                                wmm.wait(m_off(qc, "scores", pos))
                                bias = (ac[u][:, kt:kt + 1]
                                        if alibi_dram is not None else 0.0)
                                s.activation(pP[pos % 2][:], PS_s[pos % 2][:], AF.Exp,
                                             bias=bias, scale=scale
                                             ).then_inc(ctx.sc.h, 1)

        @blk.vector
        def _(v):
            wsc = Walker(ctx, v, ctx.sc.h, 0)
            wmm = Walker(ctx, v, ctx.mm.h, 1)
            wdg = Walker(ctx, v, ctx.dmg.h, 2)
            for b in range(nbatch):
                with loop(ctx, v, nh // 2) as hh:
                    for u in range(2):
                        j = b * nh + u
                        wsc.start(s0 + j * SCBH, [(2 * SCBH, hh)])
                        wmm.start(m0 + j * MMBH, [(2 * MMBH, hh)])
                        wdg.start(g0 + (j - 1) * 16, [(2 * 16, hh)])
                        wdg.wait(0)
                        for qc in range(nqc):
                            kts = kts_per_qc[qc]
                            n = len(kts)
                            PS_C = PS_C2[qc % 2]
                            if causal:
                                for pos, kt in enumerate(kts):
                                    jj = kt - 4 * qc
                                    if 0 <= jj < 4:
                                        wsc.wait(s_off(qc, pos))
                                        v.tensor_mul(pP[pos % 2][:], pP[pos % 2][:],
                                                     stair[:, bass.ts(jj, 512)]
                                                     ).then_inc(ctx.ev.h, 1)
                            wmm.wait(m_off(qc, "pv", n - 1))
                            v.reciprocal(rr[:], PS_C[64:65, :]).then_inc(ctx.ev.h, 1)
                            wmm.wait(m_off(qc, "bcast"))
                            v.tensor_copy(rb[:], PS_D[0:64, :])
                            v.tensor_mul(cb[u][:, bass.ts(qc, TCH)], PS_C[0:64, :],
                                         rb[:]).then_inc(ctx.ev.h, 1)

        @blk.gpsimd
        def _(g):
            wev = Walker(ctx, g, ctx.ev.h, 0)
            for b in range(nbatch):
                with loop(ctx, g, nh // 2) as hh:
                    for u in range(2):
                        j = b * nh + u
                        h = 2 * hh + u
                        wev.start(e0 + (j + 1) * EVBH, [(2 * EVBH, hh)])
                        wev.wait(0)
                        g.dma_start(c4[:, bass.ds(h, 1), b * sq:(b + 1) * sq].squeeze(1),
                                    cb[u][:]).then_inc(ctx.dmg.h, 16)
            g.wait_ge(ctx.dmg.h, g0 + nbatch * nh * 16)

    nbh = nbatch * nh
    ctx.dma.n = d0 + nbh * LDJ * 16
    ctx.dmg.n = g0 + nbh * 16
    ctx.mm.n = m0 + nbh * MMBH
    ctx.ev.n = e0 + nbh * EVBH
    ctx.sc.n = s0 + nbh * SCBH



def emit_gate_topk(ctx, x_dram, gwT_dram, cT_dram, *, E=8, topk=4, Tn=T,
                   denom="selected", gate_b_dram=None):
    nc = ctx.nc
    assert E <= 16
    ntt = Tn // 128
    gw = ctx.sb("gt_w", [128, NCT * E], BF16)
    xs = ctx.sb("gt_x", [128, NCT * 128], BF16)
    lg = ctx.sb("gt_lg", [128, E], F32)
    wk_ = ctx.sb("gt_wk", [128, E], F32)
    sel = ctx.sb("gt_sel", [128, E], F32)
    mcol = ctx.sb("gt_m", [128, 4], F32)
    csb = ctx.sb("gt_c", [128, E], F32)
    ctile = ctx.sb("gt_ct", [E, 128], F32)
    bb = ctx.sb("gt_bb", [128, E], F32) if gate_b_dram is not None else None
    brow = ctx.sb("gt_br", [1, E], F32) if gate_b_dram is not None else None
    PS_L = ctx.ps("gt_pl")
    PS_T = ctx.ps("gt_pt")

    x3 = x_dram.rearrange("(c p) t -> p c t", p=128)
    g3 = gwT_dram.rearrange("(c p) e -> p c e", p=128)

    d0, m0, e0, s0, g0 = ctx.dma.n, ctx.mm.n, ctx.ev.n, ctx.sc.n, ctx.dmg.n
    EV = 2 + (4 * topk) + 4 + 1
    MM = 2
    SC = 1
    MOF = 1 if gate_b_dram is not None else 0
    WLD = 32 if gate_b_dram is not None else 16

    with nc.Block() as blk:
        @blk.gpsimd
        def _(g):
            g.dma_start(gw[:].rearrange("p (c e) -> p c e", c=NCT), g3
                        ).then_inc(ctx.dmg.h, 16)
            if gate_b_dram is not None:
                g.dma_start(brow[:], gate_b_dram).then_inc(ctx.dmg.h, 16)
            wev = Walker(ctx, g, ctx.ev.h, 0)
            with loop(ctx, g, ntt) as i:
                wev.start(e0 + MOF + EV, [(EV, i)])
                wev.wait(0)
                g.dma_start(cT_dram[:, bass.ds(i * 128, 128)], ctile[:]
                            ).then_inc(ctx.dmg.h, 16)
            g.wait_ge(ctx.dmg.h, g0 + WLD + ntt * 16)

        @blk.sync
        def _(sync):
            wmm = Walker(ctx, sync, ctx.mm.h, 0)
            with loop(ctx, sync, ntt) as i:
                wmm.start(m0 + MOF - MM + 1, [(MM, i)])
                wmm.wait(0)
                sync.dma_start(xs[:].rearrange("p (c t) -> p c t", c=NCT),
                               x3[:, :, bass.ds(i * 128, 128)]).then_inc(ctx.dma.h, 16)

        @blk.tensor
        def _(t):
            t.wait_ge(ctx.dmg.h, g0 + WLD)
            if gate_b_dram is not None:
                t.matmul(PS_T[0:128, 0:E], ctx.consts["ones_row"][:],
                         brow[0:1, :], start=True, stop=True).then_inc(ctx.mm.h, 1)
            wdma = Walker(ctx, t, ctx.dma.h, 0)
            wev = Walker(ctx, t, ctx.ev.h, 1)
            with loop(ctx, t, ntt) as i:
                wdma.start(d0 + 16, [(16, i)])
                wdma.wait(0)
                wev.start(e0 + MOF, [(EV, i)])
                wev.wait(0)
                for ct in range(NCT):
                    mmi = t.matmul(PS_L[0:128, 0:E], xs[:, bass.ts(ct, 128)],
                                   gw[:, ct * E:(ct + 1) * E],
                                   start=(ct == 0), stop=(ct == NCT - 1))
                mmi.then_inc(ctx.mm.h, 1)
                wev.wait(2 + 4 * topk + 4)
                t.transpose(PS_T[0:E, 0:128], csb[:, 0:E], ctx.consts["ident"][:]
                            ).then_inc(ctx.mm.h, 1)

        @blk.scalar
        def _(s):
            wev = Walker(ctx, s, ctx.ev.h, 0)
            with loop(ctx, s, ntt) as i:
                wev.start(e0 + MOF + 2, [(EV, i)])
                wev.wait(0)
                s.activation(lg[:, 0:E], lg[:, 0:E], AF.Exp).then_inc(ctx.sc.h, 1)

        @blk.vector
        def _(v):
            if gate_b_dram is not None:
                wmm0 = Walker(ctx, v, ctx.mm.h, 3)
                wmm0.start(m0 + 1, [])
                wmm0.wait(0)
                v.tensor_copy(bb[:], PS_T[0:128, 0:E]).then_inc(ctx.ev.h, 1)
            wmm = Walker(ctx, v, ctx.mm.h, 0)
            wsc = Walker(ctx, v, ctx.sc.h, 1)
            wdg = Walker(ctx, v, ctx.dmg.h, 2)
            wev = Walker(ctx, v, ctx.ev.h, 3)
            with loop(ctx, v, ntt) as i:
                wev.start(e0 + MOF, [(EV, i)])
                cnt = 0

                def step(f, wait=True):
                    nonlocal cnt
                    if wait and cnt:
                        wev.wait(cnt)
                    f().then_inc(ctx.ev.h, 1)
                    cnt += 1

                wmm.start(m0 + MOF + 1, [(MM, i)])
                wmm.wait(0)
                if gate_b_dram is not None:
                    v.tensor_add(lg[:, 0:E], PS_L[0:128, 0:E], bb[:]
                                 ).then_inc(ctx.ev.h, 1)
                else:
                    v.tensor_copy(lg[:, 0:E], PS_L[0:128, 0:E]).then_inc(ctx.ev.h, 1)
                cnt += 1
                step(lambda: v.tensor_copy(wk_[:], lg[:, 0:E]))
                v.memset(sel[:], 0.0)
                for r in range(topk):
                    step(lambda: v.reduce_max(mcol[:, r:r + 1], wk_[:, 0:E], axis=AX))
                    step(lambda: v.tensor_scalar(csb[:, 0:E], wk_[:, 0:E],
                                                 mcol[:, r:r + 1], 0.0,
                                                 OP.subtract, OP.is_ge))
                    step(lambda: v.tensor_add(sel[:], sel[:], csb[:, 0:E]))
                    step(lambda: v.scalar_tensor_tensor(wk_[:, 0:E], csb[:, 0:E],
                                                        -1e30, wk_[:, 0:E],
                                                        OP.mult, OP.add))
                wsc.start(s0 + 1, [(SC, i)])
                wsc.wait(0)
                if denom == "selected":
                    step(lambda: v.tensor_mul(csb[:, 0:E], sel[:], lg[:, 0:E]))
                    step(lambda: v.reduce_sum(mcol[:, 0:1], csb[:, 0:E], axis=AX))
                else:
                    step(lambda: v.reduce_sum(mcol[:, 0:1], lg[:, 0:E], axis=AX))
                    step(lambda: v.tensor_mul(csb[:, 0:E], sel[:], lg[:, 0:E]))
                step(lambda: v.reciprocal(mcol[:, 1:2], mcol[:, 0:1]))
                step(lambda: v.tensor_scalar(csb[:, 0:E], csb[:, 0:E], mcol[:, 1:2],
                                             None, OP.mult))
                wmm.wait(1)
                wdg.start(g0 + WLD, [(16, i)])
                wdg.wait(0)
                v.tensor_copy(ctile[:], PS_T[0:E, 0:128]).then_inc(ctx.ev.h, 1)

    ctx.dma.n = d0 + ntt * 16
    ctx.dmg.n = g0 + WLD + ntt * 16
    ctx.mm.n = m0 + MOF + ntt * MM
    ctx.ev.n = e0 + MOF + ntt * EV
    ctx.sc.n = s0 + ntt * SC


def emit_moe_experts(ctx, x_dram, w1_dram, w3_dram, w2_dram, y_dram, *, n_exp=1,
                     H=1024, Tn=T, cT_dram=None, cT_base=0, const_scale=None,
                     accumulate=True, b1_dram=None, b3_dram=None, b2_dram=None,
                     scale_row_dram=None, Fout=D):
    nc = ctx.nc
    nht, nf = H // 128, Fout // 128
    ncc = D // 128
    nch = Tn // TCH
    assert nch % 2 == 0
    have_ct = cT_dram is not None
    have_cs = (const_scale is not None and const_scale != 1.0)
    need_b2 = b2_dram is not None
    w1s = ctx.sb("mx_w1", [128, ncc * H], BF16)
    w3s = ctx.sb("mx_w3", [128, ncc * H], BF16)
    w2s = ctx.sb("mx_w2", [128, nht * Fout], BF16)
    xs2 = [ctx.sb("mx_x", [128, ncc * TCH], BF16) for _ in range(2)]
    gs = ctx.sb("mx_g", [128, nht * TCH], BF16)
    glu = ctx.sb("mx_glu", [128, nht * TCH], BF16)
    ys = ctx.sb("mx_y", [128, nf * TCH], F32)
    yin2 = [ctx.sb("mx_yin", [128, nf * TCH], F32) for _ in range(2)] \
        if accumulate else None
    crow2 = [ctx.sb("mx_cr", [1, TCH], F32) for _ in range(2)] if have_ct else None
    crb2 = [ctx.sb("mx_crb", [1, TCH], BF16) for _ in range(2)] \
        if (have_ct and need_b2) else None
    srow = ctx.sb("mx_sr", [1, TCH], BF16) if (not have_ct and need_b2) else None
    b1s = ctx.sb("mx_b1", [128, nht], F32) if b1_dram is not None else None
    b3s = ctx.sb("mx_b3", [128, nht], F32) if b3_dram is not None else None
    b2r = ctx.sb("mx_b2", [1, Fout], BF16) if need_b2 else None
    pbk = [ctx.ps("mx_p") for _ in range(4)]
    PS_B = [ctx.ps("mx_pb"), ctx.ps("mx_pb2")] if have_ct else None

    x3 = x_dram.rearrange("(c p) t -> p c t", p=128)
    y3 = y_dram.rearrange("(f p) t -> p f t", p=128)
    w13 = w1_dram.rearrange("(e c p) h -> p e c h", p=128, c=ncc)
    w33 = w3_dram.rearrange("(e c p) h -> p e c h", p=128, c=ncc)
    w23 = w2_dram.rearrange("(e h p) f -> p e h f", p=128, h=nht)
    b13 = b1_dram.rearrange("(e p) h -> p e h", p=128) if b1_dram is not None else None
    b33 = b3_dram.rearrange("(e p) h -> p e h", p=128) if b3_dram is not None else None

    d0, m0, e0, s0, g0 = ctx.dma.n, ctx.mm.n, ctx.ev.n, ctx.sc.n, ctx.dmg.n
    NWL = 3 + sum(x is not None for x in (b1s, b3s, b2r))
    SROWL = 16 if srow is not None else 0
    LD = 1 + (1 if have_ct else 0) + (1 if accumulate else 0)
    NG = 2 * nht + nf
    assert NG % 4 == 0 and nf >= 4
    BC = 1 if have_ct else 0
    CRB = 1 if crb2 is not None else 0
    MMC = NG + BC
    SCC = nht + CRB
    EVC = nht + nf
    DQC = NWL + nch

    def gsched():
        out = []
        for r in range(NG):
            kind = ("g" if r < 2 * nht and r % 2 == 0 else
                    "u" if r < 2 * nht else "y")
            idx = r // 2 if r < 2 * nht else r - 2 * nht
            pr = r - 4
            pq_rel = 0 if pr >= 0 else -1
            prr = pr if pr >= 0 else pr + NG
            pkind = ("g" if prr < 2 * nht and prr % 2 == 0 else
                     "u" if prr < 2 * nht else "y")
            pidx = prr // 2 if prr < 2 * nht else prr - 2 * nht
            out.append((r, kind, idx, (pkind, pidx, pq_rel)))
        return out

    GS = gsched()

    with nc.Block() as blk:
        @blk.gpsimd
        def _(g):
            if srow is not None:
                g.dma_start(srow[:], scale_row_dram).then_inc(ctx.dmg.h, 16)
            wmm = Walker(ctx, g, ctx.mm.h, 0)
            wsc = Walker(ctx, g, ctx.sc.h, 1)
            wev = Walker(ctx, g, ctx.ev.h, 2)
            with loop(ctx, g, n_exp) as e:
                wmm.start(m0, [(nch * MMC, e)])
                wmm.wait(0)
                wsc.start(s0, [(nch * SCC, e)])
                wsc.wait(0)
                wev.start(e0, [(nch * EVC, e)])
                wev.wait(0)
                g.dma_start(w1s[:].rearrange("p (c h) -> p c h", c=ncc),
                            w13[:, bass.ds(e, 1), :, :].squeeze(1)).then_inc(ctx.dmg.h, 16)
                g.dma_start(w3s[:].rearrange("p (c h) -> p c h", c=ncc),
                            w33[:, bass.ds(e, 1), :, :].squeeze(1)).then_inc(ctx.dmg.h, 16)
                g.dma_start(w2s[:].rearrange("p (h f) -> p h f", h=nht),
                            w23[:, bass.ds(e, 1), :, :].squeeze(1)).then_inc(ctx.dmg.h, 16)
                if b1s is not None:
                    g.dma_start(b1s[:], b13[:, bass.ds(e, 1), :].squeeze(1)
                                ).then_inc(ctx.dmg.h, 16)
                if b3s is not None:
                    g.dma_start(b3s[:], b33[:, bass.ds(e, 1), :].squeeze(1)
                                ).then_inc(ctx.dmg.h, 16)
                if b2r is not None:
                    g.dma_start(b2r[:], b2_dram[bass.ds(e, 1), :]).then_inc(ctx.dmg.h, 16)
                with loop(ctx, g, nch // 2, 1) as i2:
                    for u in range(2):
                        q = combine_idx(ctx, g, e, nch, i2 * 2 + u,
                                        maxv=n_exp * nch - 1)
                        wev.start(e0 + EVC, [(EVC, q)])
                        wev.wait(0)
                        g.dma_start(y3[:, :, bass.ds((i2 * 2 + u) * TCH, TCH)],
                                    ys[:].rearrange("p (f t) -> p f t", f=nf)
                                    ).then_inc(ctx.dmg.h, 16)
            g.wait_ge(ctx.dmg.h, g0 + SROWL + n_exp * DQC * 16)

        @blk.sync
        def _(sync):
            wmm = Walker(ctx, sync, ctx.mm.h, 0)
            wev = Walker(ctx, sync, ctx.ev.h, 1)
            wdg = Walker(ctx, sync, ctx.dmg.h, 2)
            with loop(ctx, sync, n_exp) as e:
                with loop(ctx, sync, nch // 2, 1) as i2:
                    for u in range(2):
                        ch = i2 * 2 + u
                        q = combine_idx(ctx, sync, e, nch, ch,
                                        maxv=n_exp * nch - 1)
                        wmm.start(m0 - MMC, [(MMC, q)])
                        wmm.wait(0)
                        wev.start(e0 - EVC, [(EVC, q)])
                        wev.wait(0)
                        if accumulate:
                            dqv = combine_idx(ctx, sync, e, DQC, ch, slot=1,
                                              maxv=n_exp * DQC)
                            wdg.start(g0 + SROWL + (1 - nch) * 16, [(16, dqv)])
                            wdg.wait(0)
                        sync.dma_start(xs2[u][:].rearrange("p (c t) -> p c t",
                                                           c=NCT),
                                       x3[:, :, bass.ds(ch * TCH, TCH)]
                                       ).then_inc(ctx.dma.h, 16)
                        if have_ct:
                            sync.dma_start(crow2[u][:],
                                           cT_dram[bass.ds(cT_base + e, 1),
                                                   bass.ds(ch * TCH, TCH)]
                                           ).then_inc(ctx.dma.h, 16)
                        if accumulate:
                            sync.dma_start(yin2[u][:].rearrange(
                                "p (f t) -> p f t", f=nf),
                                y3[:, :, bass.ds(ch * TCH, TCH)]
                                ).then_inc(ctx.dma.h, 16)

        @blk.tensor
        def _(t):
            wwt = Walker(ctx, t, ctx.dmg.h, 4)
            wdma = Walker(ctx, t, ctx.dma.h, 1)
            wev = Walker(ctx, t, ctx.ev.h, 2)
            wsc = Walker(ctx, t, ctx.sc.h, 3)
            with loop(ctx, t, n_exp) as e:
                with loop(ctx, t, nch // 2, 1) as i2:
                    for u in range(2):
                        q = combine_idx(ctx, t, e, nch, i2 * 2 + u,
                                        maxv=n_exp * nch - 1)
                        wwt.start(g0 + SROWL + 16, [(DQC * 16, e)])
                        wdma.start(d0 + 16, [(LD * 16, q)])
                        wdma.wait(0)
                        wev.start(e0 - 2 * EVC, [(EVC, q)])
                        wsc.start(s0 + CRB + 1, [(SCC, q)])
                        if have_ct:
                            wdma.wait(16)
                            wev.wait(nht)
                            t.matmul(PS_B[u][:], ctx.consts["ones_row"][:],
                                     crow2[u][:], start=True, stop=True
                                     ).then_inc(ctx.mm.h, 1)
                        first = {"g": True, "u": True, "y": True}
                        for r, kind, idx, (pkind, pidx, pq_rel) in GS:
                            if pkind == "g":
                                wsc.wait((1 + pq_rel) * SCC + pidx - SCC)
                            elif pkind == "u":
                                wev.wait((2 + pq_rel) * EVC + pidx + 1)
                            else:
                                wev.wait((2 + pq_rel) * EVC + nht + pidx + 1)
                            if first[kind]:
                                first[kind] = False
                                wwt.wait({"g": 0, "u": 16,
                                          "y": (NWL - 1) * 16}[kind])
                            pb = pbk[r % 4]
                            if kind == "g":
                                for ct in range(ncc):
                                    mmi = t.matmul(pb[:],
                                                   w1s[:, ct * H + idx * 128:ct * H + (idx + 1) * 128],
                                                   xs2[u][:, bass.ts(ct, TCH)],
                                                   start=(ct == 0), stop=(ct == ncc - 1))
                            elif kind == "u":
                                for ct in range(ncc):
                                    mmi = t.matmul(pb[:],
                                                   w3s[:, ct * H + idx * 128:ct * H + (idx + 1) * 128],
                                                   xs2[u][:, bass.ts(ct, TCH)],
                                                   start=(ct == 0), stop=(ct == ncc - 1))
                            else:
                                if idx == 0:
                                    wev.wait(2 * EVC + nht)
                                    if crb2 is not None:
                                        wsc.wait(0)
                                for ht in range(nht):
                                    mmi = t.matmul(pb[:],
                                                   w2s[:, ht * Fout + idx * 128:ht * Fout + (idx + 1) * 128],
                                                   glu[:, bass.ts(ht, TCH)],
                                                   start=(ht == 0),
                                                   stop=(ht == nht - 1) and not need_b2)
                                if need_b2:
                                    mmi = t.matmul(pb[:],
                                                   b2r[0:1, idx * 128:(idx + 1) * 128],
                                                   (crb2[u] if have_ct else srow)[:],
                                                   start=False, stop=True)
                            mmi.then_inc(ctx.mm.h, 1)

        @blk.scalar
        def _(s):
            wdg = Walker(ctx, s, ctx.dmg.h, 0)
            wdma = Walker(ctx, s, ctx.dma.h, 1)
            wmm = Walker(ctx, s, ctx.mm.h, 2)
            with loop(ctx, s, n_exp) as e:
                with loop(ctx, s, nch // 2, 1) as i2:
                    for u in range(2):
                        q = combine_idx(ctx, s, e, nch, i2 * 2 + u,
                                        maxv=n_exp * nch - 1)
                        wdg.start(g0 + SROWL + NWL * 16, [(DQC * 16, e)])
                        wdg.wait(0)
                        wmm.start(m0 + BC + 1, [(MMC, q)])
                        if crb2 is not None:
                            wdma.start(d0 + 32, [(LD * 16, q)])
                            wdma.wait(0)
                            s.activation(crb2[u][:], crow2[u][:], AF.Copy
                                         ).then_inc(ctx.sc.h, 1)
                        for ht in range(nht):
                            wmm.wait(2 * ht)
                            s.activation(gs[:, bass.ts(ht, TCH)],
                                         pbk[(2 * ht) % 4][:], AF.Silu,
                                         bias=(b1s[:, ht:ht + 1]
                                               if b1s is not None else 0.0)
                                         ).then_inc(ctx.sc.h, 1)

        @blk.vector
        def _(v):
            wdg = Walker(ctx, v, ctx.dmg.h, 0)
            wdma = Walker(ctx, v, ctx.dma.h, 1)
            wmm = Walker(ctx, v, ctx.mm.h, 2)
            wsc = Walker(ctx, v, ctx.sc.h, 3)
            with loop(ctx, v, n_exp) as e:
                with loop(ctx, v, nch // 2, 1) as i2:
                    for u in range(2):
                        q = combine_idx(ctx, v, e, nch, i2 * 2 + u,
                                        maxv=n_exp * nch - 1)
                        dq = combine_idx(ctx, v, e, DQC, i2 * 2 + u, slot=1,
                                         maxv=n_exp * DQC)
                        wdg.start(g0 + SROWL, [(16, dq)])
                        wdg.wait(NWL * 16)
                        if accumulate:
                            wdma.start(d0 + LD * 16, [(LD * 16, q)])
                            wdma.wait(0)
                        wsc.start(s0 + CRB + 1, [(SCC, q)])
                        wmm.start(m0 + BC + 2, [(MMC, q)])
                        for ht in range(nht):
                            wsc.wait(ht)
                            wmm.wait(2 * ht)
                            upb = pbk[(2 * ht + 1) % 4]
                            gl = glu[:, bass.ts(ht, TCH)]
                            if b3s is not None:
                                v.tensor_scalar(gl, upb[:], b3s[:, ht:ht + 1],
                                                None, OP.add)
                                op = v.tensor_mul(gl, gl, gs[:, bass.ts(ht, TCH)])
                            else:
                                op = v.tensor_mul(gl, upb[:], gs[:, bass.ts(ht, TCH)])
                            if have_ct:
                                op = v.tensor_mul(gl, gl, PS_B[u][:])
                            elif have_cs:
                                op = v.tensor_scalar(gl, gl, float(const_scale),
                                                     None, OP.mult)
                            op.then_inc(ctx.ev.h, 1)
                        for ft in range(nf):
                            wmm.wait(2 * nht + ft - 1)
                            ypb = pbk[(2 * nht + ft) % 4]
                            if accumulate:
                                v.tensor_add(ys[:, bass.ts(ft, TCH)], ypb[:],
                                             yin2[u][:, bass.ts(ft, TCH)]
                                             ).then_inc(ctx.ev.h, 1)
                            else:
                                v.tensor_copy(ys[:, bass.ts(ft, TCH)], ypb[:]
                                              ).then_inc(ctx.ev.h, 1)

    ctx.dma.n = d0 + n_exp * nch * LD * 16
    ctx.dmg.n = g0 + SROWL + n_exp * DQC * 16
    ctx.mm.n = m0 + n_exp * nch * MMC
    ctx.ev.n = e0 + n_exp * nch * EVC
    ctx.sc.n = s0 + n_exp * nch * SCC




def emit_butterfly(ctx, win_dram, rows_dram, wout_dram, *, R=D, name="bf",
                   strides=None, extra_row_dram=None):
    nc = ctx.nc
    strides = strides if strides is not None else [2 ** s for s in range(12)]
    nct = (R + 127) // 128
    rows = 128 if R >= 128 else R
    mid = ctx.dram(f"{name}_mid", [R, 4096], BF16)
    A = ctx.sb("bf_a", [128, 4096], BF16)
    Bb = ctx.sb("bf_b", [128, 4096], BF16)
    T1 = ctx.sb("bf_t1", [128, 2048], BF16)
    T2 = ctx.sb("bf_t2", [128, 2048], BF16)
    rrows = ctx.sb("bf_rr", [1, 4 * 2048], F32)
    wbc = ctx.sb("bf_w", [128, 24 * 2048], BF16)
    xrow = ctx.sb("bf_xr", [1, 4096], F32) if extra_row_dram is not None else None
    PS = [ctx.ps("bf_p") for _ in range(2)]

    for p in range(2):
        stv = strides[p * 6:(p + 1) * 6]
        src_d = win_dram if p == 0 else mid
        dst_d = mid if p == 0 else wout_dram
        last = p == 1
        d0, m0, e0, g0 = ctx.dma.n, ctx.mm.n, ctx.ev.n, ctx.dmg.n
        xofs = 32 if (last and extra_row_dram is not None) else 16
        with nc.Block() as blk:
            @blk.sync
            def _(sync):
                for s6 in range(6):
                    if s6 >= 1:
                        sync.wait_ge(ctx.mm.h, m0 + s6 * 16)
                    sync.dma_start(rrows[:].rearrange("r (j t) -> r j t", j=4),
                                   rows_dram[p * 24 + s6 * 4:p * 24 + (s6 + 1) * 4,
                                             :].rearrange("(r j) t -> r j t", r=1)
                                   ).then_inc(ctx.dma.h, 16)
                if last and extra_row_dram is not None:
                    sync.dma_start(xrow[:], extra_row_dram).then_inc(ctx.dma.h, 16)

            @blk.tensor
            def _(t):
                wev = Walker(ctx, t, ctx.ev.h, 0)
                wev.start(e0 - 1, [])
                for idx in range(96):
                    s6, sub = divmod(idx, 16)
                    if sub == 0:
                        t.wait_ge(ctx.dma.h, d0 + (s6 + 1) * 16)
                    if idx >= 2:
                        wev.wait(idx)
                    t.matmul(PS[idx % 2][:], ctx.consts["ones_row"][:],
                             rrows[0:1, sub * 512:(sub + 1) * 512],
                             start=True, stop=True).then_inc(ctx.mm.h, 1)

            @blk.vector
            def _(v):
                wmm = Walker(ctx, v, ctx.mm.h, 0)
                wmm.start(m0 + 1, [])
                for idx in range(96):
                    wmm.wait(idx)
                    v.tensor_copy(wbc[:, idx * 512:(idx + 1) * 512],
                                  PS[idx % 2][:]).then_inc(ctx.ev.h, 1)
                wdma = Walker(ctx, v, ctx.dmg.h, 1)
                with loop(ctx, v, nct) as i:
                    wdma.start(g0 + 16, [(32, i)])
                    wdma.wait(0)
                    cur, nxt = A, Bb
                    for si in range(6):
                        st = stv[si]
                        nb = 4096 // (2 * st)

                        def view(buf, half):
                            return buf[0:rows, :].rearrange(
                                "p (nb two st) -> p nb two st", nb=nb,
                                st=st)[:, :, half, :]

                        def w_v(r):
                            return wbc[0:rows, (si * 4 + r) * 2048:
                                       (si * 4 + r + 1) * 2048].rearrange(
                                "p (nb st) -> p nb st", nb=nb)

                        t1v = T1[0:rows, :].rearrange("p (nb st) -> p nb st", nb=nb)
                        t2v = T2[0:rows, :].rearrange("p (nb st) -> p nb st", nb=nb)
                        v.tensor_mul(t1v, view(cur, 0), w_v(0))
                        v.tensor_mul(t2v, view(cur, 1), w_v(1))
                        v.tensor_add(view(nxt, 0), t1v, t2v)
                        v.tensor_mul(t1v, view(cur, 0), w_v(2))
                        v.tensor_mul(t2v, view(cur, 1), w_v(3))
                        v.tensor_add(view(nxt, 1), t1v, t2v)
                        cur, nxt = nxt, cur
                    if last and extra_row_dram is not None:
                        v.tensor_add(A[0:rows, :], A[0:rows, :], xrow[0:1, :]
                                     if rows == 1 else xrow[0:1, :])
                    v.engine_nop().then_inc(ctx.ev.h, 1)

            @blk.gpsimd
            def _(g):
                wev = Walker(ctx, g, ctx.ev.h, 0)
                with loop(ctx, g, nct) as i:
                    wev.start(e0 + 96, [(1, i)])
                    wev.wait(0)
                    g.dma_start(A[0:rows, :], src_d[bass.ds(i * 128, rows), :]
                                ).then_inc(ctx.dmg.h, 16)
                    wev.wait(1)
                    g.dma_start(dst_d[bass.ds(i * 128, rows), :], A[0:rows, :]
                                ).then_inc(ctx.dmg.h, 16)
                g.wait_ge(ctx.dmg.h, g0 + nct * 32)

        ctx.dma.n = d0 + 96 + (xofs - 16)
        ctx.dmg.n = g0 + nct * 32
        ctx.mm.n = m0 + 96
        ctx.ev.n = e0 + 96 + nct


def emit_transpose_mat(ctx, win_dram, wout_dram, *, RN=(D, 4096)):
    nc = ctx.nc
    Rr, Cc = RN
    nr, ncl = Rr // 128, Cc // 128
    tin = ctx.sb("tp_in", [128, Rr], BF16)
    tout = ctx.sb("tp_out", [128, Rr], BF16)
    identb = ctx.sb("tp_id", [128, 128], BF16)
    PT = [ctx.ps("tp_p", (128, 1024), BF16) for _ in range(2)]
    w3 = win_dram.rearrange("(r p) c -> p r c", p=128)

    d0, m0, e0, g0 = ctx.dma.n, ctx.mm.n, ctx.ev.n, ctx.dmg.n
    with nc.Block() as blk:
        @blk.vector
        def _(v):
            v.tensor_copy(identb[:], ctx.consts["ident"][:]).then_inc(ctx.ev.h, 1)
            wmm = Walker(ctx, v, ctx.mm.h, 0)
            wdg = Walker(ctx, v, ctx.dmg.h, 1)
            with loop(ctx, v, ncl) as j:
                wmm.start(m0 + 1, [(nr, j)])
                wdg.start(g0, [(16, j)])
                wdg.wait(0)
                for r in range(nr):
                    wmm.wait(r)
                    v.tensor_copy(tout[:, bass.ts(r, 128)],
                                  PT[r % 2][:, 0:128]).then_inc(ctx.ev.h, 1)

        @blk.sync
        def _(sync):
            wev = Walker(ctx, sync, ctx.ev.h, 0)
            with loop(ctx, sync, ncl) as j:
                wev.start(e0 + 1 - nr, [(nr, j)])
                wev.wait(0)
                sync.dma_start(tin[:].rearrange("p (r c) -> p r c", r=nr),
                               w3[:, :, bass.ds(j * 128, 128)]
                               ).then_inc(ctx.dma.h, 16)

        @blk.tensor
        def _(t):
            t.wait_ge(ctx.ev.h, e0 + 1)
            wdma = Walker(ctx, t, ctx.dma.h, 0)
            wev = Walker(ctx, t, ctx.ev.h, 1)
            with loop(ctx, t, ncl) as j:
                wdma.start(d0 + 16, [(16, j)])
                wdma.wait(0)
                wev.start(e0, [(nr, j)])
                for r in range(nr):
                    if r >= 2:
                        wev.wait(r)
                    t.transpose(PT[r % 2][:, 0:128], tin[:, bass.ts(r, 128)],
                                identb[:]).then_inc(ctx.mm.h, 1)

        @blk.gpsimd
        def _(g):
            wev = Walker(ctx, g, ctx.ev.h, 0)
            with loop(ctx, g, ncl) as j:
                wev.start(e0 + 1 + nr, [(nr, j)])
                wev.wait(0)
                g.dma_start(wout_dram[bass.ds(j * 128, 128), :], tout[:]
                            ).then_inc(ctx.dmg.h, 16)
            g.wait_ge(ctx.dmg.h, g0 + ncl * 16)

    ctx.dma.n = d0 + ncl * 16
    ctx.dmg.n = g0 + ncl * 16
    ctx.mm.n = m0 + ncl * nr
    ctx.ev.n = e0 + 1 + ncl * nr


def emit_l0_attn(ctx, dd, xin_f32, xin_bf, xout_f32, xout_bf):
    with ctx.scope():
        g = ctx.sb("l0a_g", [128, NCT], F32)
        b = ctx.sb("l0a_b", [128, NCT], F32)
        nc = ctx.nc
        with nc.Block() as blk:
            @blk.gpsimd
            def _(gp):
                gp.dma_start(g[:], dd["l0a_ln_g"]).then_inc(ctx.dmg.h, 16)
                gp.dma_start(b[:], dd["l0a_ln_b"]).then_inc(ctx.dmg.h, 16)
                gp.wait_ge(ctx.dmg.h, ctx.dmg.inc(32))
        zn = ctx.dram("l0a_zn", [D, T], BF16)
        qT = ctx.dram("l0a_qT", [D, T], BF16)
        kT = ctx.dram("l0a_kT", [D, T], BF16)
        vn = ctx.dram("l0a_vn", [T, NH * 72], BF16)
        cT = ctx.dram("l0a_cT", [D, T], BF16)
        y = ctx.dram("l0a_y", [D, T], F32)
        with ctx.scope():
            emit_ln(ctx, xin_f32, out_bf_dram=zn, gamma=g[:], beta=b[:])
        with ctx.scope():
            emit_linear(ctx, zn, dd["l0a_wq"], qT)
        with ctx.scope():
            emit_linear(ctx, zn, dd["l0a_wk"], kT)
        with ctx.scope():
            emit_linear_tok(ctx, zn, dd["l0a_wv"], vn)
        with ctx.scope():
            emit_attention(ctx, qT, kT, vn, cT, causal=False, scale=1.0 / 8.0)
        with ctx.scope():
            emit_linear(ctx, cT, dd["l0a_wo"], y, bias_dram=dd["l0a_bo"],
                        extra_dram=xin_f32, extra_op=OP.add, extra_dtype=F32,
                        out_dtype=F32)
        with ctx.scope():
            emit_ln(ctx, y, out_bf_dram=xout_bf, out_f32_dram=xout_f32,
                    res_dram=xin_f32)


def emit_l0_ffn(ctx, dd, xin_f32, xin_bf, xout_f32, xout_bf):
    with ctx.scope():
        h0 = ctx.dram("l0f_h0", [D, T], BF16)
        g1 = ctx.dram("l0f_g1", [D, T], BF16)
        h1 = ctx.dram("l0f_h1", [D, T], BF16)
        cT = ctx.dram("l0f_cT", [8, T], F32)
        ymoe = ctx.dram("l0f_ymoe", [D, T], F32)
        ymoe_bf = ctx.dram("l0f_ymoeb", [D, T], BF16)
        h3 = ctx.dram("l0f_h3", [D, T], BF16)
        g2 = ctx.dram("l0f_g2", [D, T], BF16)
        h4 = ctx.dram("l0f_h4", [D, T], F32)
        with ctx.scope():
            emit_linear(ctx, xin_bf, dd["l0f_lin0_w"], h0, bias_dram=dd["l0f_lin0_b"])
        with ctx.scope():
            emit_linear(ctx, h0, dd["l0f_sw1_w1"], g1, bias_dram=dd["l0f_sw1_b1"],
                        act=AF.Silu)
        with ctx.scope():
            emit_linear(ctx, h0, dd["l0f_sw1_w2"], h1, bias_dram=dd["l0f_sw1_b2"],
                        extra_dram=g1, extra_op=OP.mult)
        with ctx.scope():
            emit_gate_topk(ctx, h1, dd["l0f_gwT"], cT, E=8, topk=4, denom="selected")
        with ctx.scope():
            emit_moe_experts(ctx, h1, dd["l0f_shw1"], dd["l0f_shw3"], dd["l0f_shw2"],
                             ymoe, n_exp=1, H=2048, accumulate=False)
        with ctx.scope():
            emit_moe_experts(ctx, h1, dd["l0f_ew1"], dd["l0f_ew3"], dd["l0f_ew2"],
                             ymoe, n_exp=8, H=1024, cT_dram=cT, accumulate=True)
        emit_cast_copy(ctx, ymoe_bf, ymoe)
        with ctx.scope():
            emit_linear(ctx, ymoe_bf, dd["l0f_lin1_w"], h3, bias_dram=dd["l0f_lin1_b"])
        with ctx.scope():
            emit_linear(ctx, h3, dd["l0f_sw2_w1"], g2, bias_dram=dd["l0f_sw2_b1"],
                        act=AF.Silu)
        with ctx.scope():
            emit_linear(ctx, h3, dd["l0f_sw2_w2"], h4, bias_dram=dd["l0f_sw2_b2"],
                        extra_dram=g2, extra_op=OP.mult, out_dtype=F32)
        with ctx.scope():
            emit_ln(ctx, h4, out_bf_dram=xout_bf, out_f32_dram=xout_f32,
                    res_dram=xin_f32)


def emit_l1_attn(ctx, dd, xin_f32, xin_bf, xout_f32, xout_bf):
    with ctx.scope():
        cqT = ctx.dram("l1a_cqT", [256, T], BF16)
        ckvT = ctx.dram("l1a_ckvT", [256, T], BF16)
        qT = ctx.dram("l1a_qT", [D, T], BF16)
        kT = ctx.dram("l1a_kT", [D, T], BF16)
        vn = ctx.dram("l1a_vn", [T, NH * 72], BF16)
        cT = ctx.dram("l1a_cT", [D, T], BF16)
        y = ctx.dram("l1a_y", [D, T], F32)
        with ctx.scope():
            emit_linear(ctx, xin_bf, dd["l1a_W_dq"], cqT, F=256)
        with ctx.scope():
            emit_linear(ctx, xin_bf, dd["l1a_W_dkv"], ckvT, F=256)
        with ctx.scope():
            emit_linear(ctx, cqT, dd["l1a_W_uq"], qT, C=256)
        with ctx.scope():
            emit_linear(ctx, ckvT, dd["l1a_W_uk"], kT, C=256)
        with ctx.scope():
            emit_linear_tok(ctx, ckvT, dd["l1a_W_uv"], vn, C=256)
        with ctx.scope():
            emit_attention(ctx, qT, kT, vn, cT, causal=True, scale=1.0 / 8.0)
        with ctx.scope():
            emit_linear(ctx, cT, dd["l1a_W_o"], y, extra_dram=xin_f32,
                        extra_op=OP.add, extra_dtype=F32, out_dtype=F32)
        with ctx.scope():
            emit_ln(ctx, y, out_bf_dram=xout_bf, out_f32_dram=xout_f32)


def emit_l1_ffn(ctx, dd, xin_f32, xin_bf, xout_f32, xout_bf):
    nc = ctx.nc
    with ctx.scope():
        qT = ctx.dram("l1f_qT", [D, T], BF16)
        kT = ctx.dram("l1f_kT", [D, T], BF16)
        vn = ctx.dram("l1f_vn", [T, NH * 72], BF16)
        cT = ctx.dram("l1f_cT", [D, T], BF16)
        y = ctx.dram("l1f_y", [D, T], F32)
        wc = {p: [ctx.dram(f"l1f_wc_{p}{b}", [D, D], F32) for b in range(2)]
              for p in "qkv"}
        bcol = {p: [ctx.dram(f"l1f_bc_{p}{b}", [128, NCT], F32) for b in range(2)]
                for p in "qkv"}
        bvrow = [ctx.dram(f"l1f_bvr{b}", [1, D], F32) for b in range(2)]
        wcol = ctx.sb("l1f_wcol", [128, 8], F32)

        pooled = ctx.sb("l1f_pool", [128, NCT * 2], F32)
        with ctx.scope():
            xs = ctx.sb("l1f_xs", [128, NCT * TCH], BF16)
            tmpc = ctx.sb("l1f_tc", [128, 1], F32)
            x3 = xin_bf.rearrange("(c p) t -> p c t", p=128)
            d0, e0 = ctx.dma.n, ctx.ev.n
            EVC = 2 * NCT + 1
            with nc.Block() as blk:
                @blk.sync
                def _(sync):
                    for i in range(NCHUNK):
                        if i >= 1:
                            sync.wait_ge(ctx.ev.h, e0 + i * EVC)
                        sync.dma_start(xs[:].rearrange("p (c t) -> p c t", c=NCT),
                                       x3[:, :, i * TCH:(i + 1) * TCH]
                                       ).then_inc(ctx.dma.h, 16)

                @blk.vector
                def _(v):
                    v.memset(pooled[:], 0.0)
                    cnt = e0
                    for i in range(NCHUNK):
                        b = i // (NCHUNK // 2)
                        v.wait_ge(ctx.dma.h, d0 + (i + 1) * 16)
                        for ct in range(NCT):
                            v.reduce_sum(tmpc[:], xs[:, bass.ts(ct, TCH)], axis=AX
                                         ).then_inc(ctx.ev.h, 1)
                            cnt += 1
                            v.wait_ge(ctx.ev.h, cnt)
                            v.tensor_add(pooled[:, ct * 2 + b:ct * 2 + b + 1],
                                         pooled[:, ct * 2 + b:ct * 2 + b + 1],
                                         tmpc[:]).then_inc(ctx.ev.h, 1)
                            cnt += 1
                            v.wait_ge(ctx.ev.h, cnt)
                        v.engine_nop().then_inc(ctx.ev.h, 1)
                        cnt += 1
            ctx.dma.n = d0 + NCHUNK * 16
            ctx.ev.n = e0 + NCHUNK * EVC

        with ctx.scope():
            wg1 = ctx.sb("l1f_wg1", [128, NCT * 128], BF16)
            bg1r = ctx.sb("l1f_bg1", [2, 128], F32)
            wg2 = ctx.sb("l1f_wg2", [128, 4], BF16)
            bg2r = ctx.sb("l1f_bg2", [2, 4], F32)
            poolb = ctx.sb("l1f_poolb", [128, NCT * 2], BF16)
            g1 = ctx.sb("l1f_g1", [2, 128], F32)
            g1T = ctx.sb("l1f_g1T", [128, 2], BF16)
            lg2 = ctx.sb("l1f_lg2", [2, 4], F32)
            wk2 = ctx.sb("l1f_wk2", [2, 4], F32)
            sel2 = ctx.sb("l1f_sel2", [2, 4], F32)
            msk2 = ctx.sb("l1f_msk2", [2, 4], F32)
            mc2 = ctx.sb("l1f_mc2", [2, 2], F32)
            wb = ctx.sb("l1f_wb", [2, 4], F32)
            wbT = ctx.sb("l1f_wbT", [4, 2], F32)
            wflat = ctx.sb("l1f_wflat", [1, 8], F32)
            bqs = ctx.sb("l1f_bqs", [4, 3 * D], F32)
            bcsb = ctx.sb("l1f_bcsb", [2, 3 * D], F32)
            bcol_sb = ctx.sb("l1f_bcolsb", [128, 6 * NCT], F32)
            P1 = ctx.ps("l1f_p1")
            P2 = ctx.ps("l1f_p2")
            P3 = ctx.ps("l1f_p3")
            q = Seq(ctx)
            q.dma(lambda g: g.dma_start(
                wg1[:].rearrange("p (c f) -> p c f", c=NCT),
                dd["l1f_Wg1"].rearrange("(c p) f -> p c f", p=128)))
            q.dma(lambda g: g.dma_start(bg1r[:], dd["l1f_bg1"]))
            q.dma(lambda g: g.dma_start(wg2[:], dd["l1f_Wg2p"]))
            q.dma(lambda g: g.dma_start(bg2r[:], dd["l1f_bg2"]))
            q.dma(lambda g: g.dma_start(bqs[:, 0:D], dd["l1f_bq"]))
            q.dma(lambda g: g.dma_start(bqs[:, D:2 * D], dd["l1f_bk"]))
            q.dma(lambda g: g.dma_start(bqs[:, 2 * D:3 * D], dd["l1f_bv"]))
            q.op("vector", lambda v: v.tensor_scalar(poolb[:], pooled[:], 1.0 / S,
                                                     None, OP.mult))

            def mm_g1(t):
                for ct in range(NCT):
                    r = t.matmul(P1[0:2, 0:128], poolb[:, ct * 2:(ct + 1) * 2],
                                 wg1[:, bass.ts(ct, 128)], start=(ct == 0),
                                 stop=(ct == NCT - 1))
                return r
            q.op("tensor", mm_g1)
            q.op("vector", lambda v: v.tensor_tensor(g1[:], P1[0:2, 0:128], bg1r[:],
                                                     OP.add))
            q.op("vector", lambda v: v.tensor_scalar_max(g1[:], g1[:], 0.0))
            q.op("tensor", lambda t: t.transpose(P2[0:128, 0:2], g1[:],
                                                 ctx.consts["ident"][0:2, 0:2]))
            q.op("vector", lambda v: v.tensor_copy(g1T[:], P2[0:128, 0:2]))
            q.op("tensor", lambda t: t.matmul(P1[0:2, 0:4], g1T[:], wg2[:],
                                              start=True, stop=True))
            q.op("vector", lambda v: v.tensor_tensor(lg2[:], P1[0:2, 0:4], bg2r[:],
                                                     OP.add))
            q.op("vector", lambda v: v.tensor_copy(wk2[:], lg2[:]))
            q.op("vector", lambda v: v.memset(sel2[:], 0.0))
            for r in range(2):
                q.op("vector", lambda v: v.reduce_max(mc2[:, 0:1], wk2[:], axis=AX))
                q.op("vector", lambda v: v.tensor_scalar(msk2[:], wk2[:], mc2[:, 0:1],
                                                         0.0, OP.subtract, OP.is_ge))
                q.op("vector", lambda v: v.tensor_add(sel2[:], sel2[:], msk2[:]))
                q.op("vector", lambda v: v.scalar_tensor_tensor(
                    wk2[:], msk2[:], -1e30, wk2[:], OP.mult, OP.add))
            q.op("scalar", lambda s: s.activation(lg2[:], lg2[:], AF.Exp))
            q.op("vector", lambda v: v.tensor_mul(wb[:], sel2[:], lg2[:]))
            q.op("vector", lambda v: v.reduce_sum(mc2[:, 0:1], wb[:], axis=AX))
            q.op("vector", lambda v: v.reciprocal(mc2[:, 1:2], mc2[:, 0:1]))
            q.op("vector", lambda v: v.tensor_scalar(wb[:], wb[:], mc2[:, 1:2], None,
                                                     OP.mult))
            q.dma(lambda g: g.dma_start(wflat[0:1, 0:4], wb[0:1, :]))
            q.dma(lambda g: g.dma_start(wflat[0:1, 4:8], wb[1:2, :]))
            q.op("tensor", lambda t: t.matmul(P2[0:128, 0:8],
                                              ctx.consts["ones_row"][:], wflat[:],
                                              start=True, stop=True))
            q.op("vector", lambda v: v.tensor_copy(wcol[:], P2[0:128, 0:8]))
            q.op("tensor", lambda t: t.transpose(P3[0:4, 0:2], wb[:],
                                                 ctx.consts["ident"][0:2, 0:2]))
            q.op("vector", lambda v: v.tensor_copy(wbT[:], P3[0:4, 0:2]))
            for j in range(6):
                def mk(j):
                    return lambda t: t.matmul(P1[0:2, 0:512], wbT[:],
                                              bqs[:, bass.ts(j, 512)],
                                              start=True, stop=True)
                q.op("tensor", mk(j))
                def mkc(j):
                    return lambda v: v.tensor_copy(bcsb[:, bass.ts(j, 512)],
                                                   P1[0:2, 0:512])
                q.op("vector", mkc(j))
            for j in range(24):
                def mkt(j):
                    return lambda t: t.transpose(P3[0:128, 0:2],
                                                 bcsb[:, bass.ts(j, 128)],
                                                 ctx.consts["ident"][0:2, 0:2])
                q.op("tensor", mkt(j))
                pi, ft = divmod(j, NCT)
                def mkc2(j, pi, ft):
                    def f(v):
                        v.tensor_copy(bcol_sb[:, (pi * 2 + 0) * NCT + ft:
                                              (pi * 2 + 0) * NCT + ft + 1],
                                      P3[0:128, 0:1])
                        return v.tensor_copy(bcol_sb[:, (pi * 2 + 1) * NCT + ft:
                                                     (pi * 2 + 1) * NCT + ft + 1],
                                             P3[0:128, 1:2])
                    return f
                q.op("vector", mkc2(j, pi, ft))
            for pi, p in enumerate("qkv"):
                for b in range(2):
                    def mks(pi, p, b):
                        return lambda g: g.dma_start(
                            bcol[p][b],
                            bcol_sb[:, (pi * 2 + b) * NCT:(pi * 2 + b + 1) * NCT])
                    q.dma(mks(pi, p, b))
            for b in range(2):
                def mkv(b):
                    return lambda g: g.dma_start(bvrow[b], bcsb[b:b + 1, 2 * D:3 * D])
                q.dma(mkv(b))
            q.emit()

        with ctx.scope():
            wtile = ctx.sb("l1f_wt", [128, NCT * D], F32)
            acc = ctx.sb("l1f_acc", [128, NCT * D], F32)
            for pi, p in enumerate("qkv"):
                wsrc = dd["l1f_W" + p]
                for b in range(2):
                    q = Seq(ctx)
                    for e in range(4):
                        def mkl(e):
                            return lambda g: g.dma_start(
                                wtile[:].rearrange("p (c f) -> p c f", c=NCT),
                                wsrc[e * D:(e + 1) * D, :].rearrange(
                                    "(c p) f -> p c f", p=128))
                        q.dma(mkl(e))
                        col = wcol[:, b * 4 + e:b * 4 + e + 1]
                        if e == 0:
                            def mk0(col):
                                return lambda v: v.tensor_scalar(acc[:], wtile[:],
                                                                 col, None, OP.mult)
                            q.op("vector", mk0(col))
                        else:
                            def mke(col):
                                return lambda v: v.scalar_tensor_tensor(
                                    acc[:], wtile[:], col, acc[:], OP.mult, OP.add)
                            q.op("vector", mke(col))
                    q.dma(lambda g: g.dma_start(
                        wc[p][b].rearrange("(c p) f -> p c f", p=128),
                        acc[:].rearrange("p (c f) -> p c f", c=NCT)))
                    q.emit()

        for b in range(2):
            sl = slice(b * S, (b + 1) * S)
            with ctx.scope():
                emit_linear(ctx, xin_bf[:, sl], wc["q"][b], qT[:, sl], Tn=S,
                            bias_dram=bcol["q"][b])
            with ctx.scope():
                emit_linear(ctx, xin_bf[:, sl], wc["k"][b], kT[:, sl], Tn=S,
                            bias_dram=bcol["k"][b])
            with ctx.scope():
                emit_linear_tok(ctx, xin_bf[:, sl], wc["v"][b], vn[sl, :], Tn=S,
                                bias_row_dram=bvrow[b])
        with ctx.scope():
            emit_attention(ctx, qT, kT, vn, cT, causal=False, scale=1.0 / 8.0)
        with ctx.scope():
            emit_linear(ctx, cT, dd["l1f_Wo"], y, bias_dram=dd["l1f_bo"],
                        extra_dram=xin_f32, extra_op=OP.add, extra_dtype=F32,
                        out_dtype=F32)
        with ctx.scope():
            emit_ln(ctx, y, out_bf_dram=xout_bf, out_f32_dram=xout_f32)


def emit_l2_attn(ctx, dd, xin_f32, xin_bf, xout_f32, xout_bf):
    with ctx.scope():
        qT = ctx.dram("l2a_qT", [D, T], BF16)
        kT = ctx.dram("l2a_kT", [D, T], BF16)
        vn = ctx.dram("l2a_vn", [T, NH * 72], BF16)
        cT = ctx.dram("l2a_cT", [D, T], BF16)
        y = ctx.dram("l2a_y", [D, T], F32)
        with ctx.scope():
            emit_linear(ctx, xin_bf, dd["l2a_WqT"], qT)
        with ctx.scope():
            emit_linear(ctx, xin_bf, dd["l2a_WkT"], kT)
        with ctx.scope():
            emit_linear_tok(ctx, xin_bf, dd["l2a_WvT"], vn)
        with ctx.scope():
            emit_attention(ctx, qT, kT, vn, cT, causal=True,
                           alibi_dram=dd["c_alibi"], scale=1.0 / 8.0)
        with ctx.scope():
            emit_linear(ctx, cT, dd["l2a_WoT"], y, extra_dram=xin_f32,
                        extra_op=OP.add, extra_dtype=F32, out_dtype=F32)
        with ctx.scope():
            emit_ln(ctx, y, out_bf_dram=xout_bf, out_f32_dram=xout_f32)


def emit_l2_ffn(ctx, dd, xin_f32, xin_bf, xout_f32, xout_bf):
    with ctx.scope():
        wexp = ctx.dram("l2f_wexp_f", [D, 4096], BF16)
        brow = ctx.dram("l2f_brow", [1, 4096], BF16)
        bcol = ctx.dram("l2f_bcol", [128, 32], F32)
        wconTf = ctx.dram("l2f_wconT_f", [D, 4096], BF16)
        wcon = ctx.dram("l2f_wcon_f", [4096, D], BF16)
        h1 = ctx.dram("l2f_h1", [4096, T], BF16)
        y = ctx.dram("l2f_y", [D, T], F32)
        with ctx.scope():
            emit_butterfly(ctx, dd["l2f_wexp"], dd["l2f_uprows"], wexp,
                           R=D, name="l2f_up")
        with ctx.scope():
            emit_butterfly(ctx, dd["l2f_bexp"], dd["l2f_uprows"], brow,
                           R=1, name="l2f_upb", extra_row_dram=dd["l2f_upbias"])
        with ctx.scope():
            emit_butterfly(ctx, dd["l2f_wconT"], dd["l2f_dnrows"], wconTf,
                           R=D, name="l2f_dn",
                           strides=[2 ** s for s in range(11, -1, -1)])
        with ctx.scope():
            emit_transpose_mat(ctx, wconTf, wcon, RN=(D, 4096))
        with ctx.scope():
            brsb = ctx.sb("l2f_brs", [1, 4096], F32)
            nc = ctx.nc
            with nc.Block() as blk:
                @blk.gpsimd
                def _(g):
                    g.dma_start(brsb[:], brow[0:1, :]).then_inc(ctx.dmg.h, 16)
                    g.wait_ge(ctx.dmg.h, ctx.dmg.inc(16))
                    g.dma_start(bcol.rearrange("p j -> j p").unsqueeze(0),
                        brsb[0:1, :].rearrange("r (j p) -> r j p", p=128),
                        allow_slow_non_contiguous=True).then_inc(ctx.dmg.h, 16)
                    g.wait_ge(ctx.dmg.h, ctx.dmg.inc(16))
        with ctx.scope():
            emit_linear(ctx, xin_bf, wexp, h1, F=4096, bias_dram=bcol, act=AF.Gelu)
        with ctx.scope():
            emit_linear(ctx, h1, wcon, y, C=4096, F=D, bias_dram=dd["l2f_bcon"],
                        extra_dram=xin_f32, extra_op=OP.add, extra_dtype=F32,
                        out_dtype=F32, bias2_row_dram=dd["l2f_dbias"],
                        ones_row_t_dram=dd["c_ones_row_t"])
        with ctx.scope():
            emit_ln(ctx, y, out_bf_dram=xout_bf, out_f32_dram=xout_f32)


def emit_l3_attention_core(ctx, qT_sc, kT_sc, vn_sc, ctxT_sc):
    nc = ctx.nc
    ntt = T // 128
    qb = ctx.sb("l3_q", [64, 16 * 128], BF16)
    kb = ctx.sb("l3_k", [64, 16 * 128], BF16)
    vb = ctx.sb("l3_v", [128, NH * 72], BF16)
    pP = [ctx.sb("l3_p", [128, 512], BF16) for _ in range(2)]
    rr = ctx.sb("l3_rr", [1, 512], F32)
    rb = ctx.sb("l3_rb", [64, 512], F32)
    cbT = ctx.sb("l3_cb", [128, NCT * 128], BF16)
    PS_q = [ctx.ps("l3_ps") for _ in range(2)]
    PS_C2 = [ctx.ps("l3_pc") for _ in range(2)]
    PS_D = ctx.ps("l3_pd")

    q3 = qT_sc.rearrange("(chi p) t -> p chi t", p=64)
    k3 = kT_sc.rearrange("(chi p) t -> p chi t", p=64)
    c3 = ctxT_sc.rearrange("(c p) t -> p c t", p=128)
    vn3 = vn_sc.rearrange("(u p) f -> p u f", p=128)

    d0, m0, e0, s0, g0 = ctx.dma.n, ctx.mm.n, ctx.ev.n, ctx.sc.n, ctx.dmg.n
    MMQ = 33
    MMT = 4 * MMQ
    SCT = 4 * 16
    EVT = 4 * 2

    def _C(q):
        return 32 * q + (q - 1 if q >= 1 else 0)

    def m_off(qq, kind, pos=0):
        hb = 1 if qq >= 1 else 0
        if kind == "scores":
            off = pos + 1 if pos <= 1 else 2 * pos + hb
        elif kind == "pv":
            off = (32 + hb) if pos == 15 else (2 * pos + 3 + hb)
        else:
            return _C(qq + 1) + 3 if qq <= 2 else MMT
        return _C(qq) + off

    def s_off(qq, pos):
        return qq * 16 + pos + 1

    def e_off(qq, kind):
        return qq * 2 + (1 if kind == "recip" else 2)

    with nc.Block() as blk:
        @blk.sync
        def _(sync):
            wmm = Walker(ctx, sync, ctx.mm.h, 0)
            with loop(ctx, sync, ntt) as i:
                wmm.start(m0, [(MMT, i)])
                wmm.wait(0)
                sync.dma_start(qb[:].rearrange("p (c t) -> p c t", c=16),
                               q3[:, :, bass.ds(i * 128, 128)]).then_inc(ctx.dma.h, 16)
                sync.dma_start(kb[:].rearrange("p (c t) -> p c t", c=16),
                               k3[:, :, bass.ds(i * 128, 128)]).then_inc(ctx.dma.h, 16)
                sync.dma_start(vb[:], vn3[:, bass.ds(i, 1), :].squeeze(1)
                               ).then_inc(ctx.dma.h, 16)

        @blk.tensor
        def _(t):
            wdma = Walker(ctx, t, ctx.dma.h, 0)
            wev = Walker(ctx, t, ctx.ev.h, 1)
            wsc = Walker(ctx, t, ctx.sc.h, 2)
            with loop(ctx, t, ntt) as i:
                wdma.start(d0 + 48, [(48, i)])
                wdma.wait(0)
                wev.start(e0 - EVT, [(EVT, i)])
                wsc.start(s0, [(SCT, i)])
                pending = None
                for qq in range(4):
                    PS_C = PS_C2[qq % 2]
                    if qq >= 2:
                        wev.wait(EVT + e_off(qq - 2, "evac"))
                    else:
                        wev.wait(e_off(2 + qq, "evac"))
                    for kv in range(16):
                        if kv >= 2:
                            wsc.wait(s_off(qq, kv - 2))
                        elif qq >= 1:
                            wsc.wait(s_off(qq - 1, 14 + kv))
                        t.matmul(PS_q[kv % 2][:], kb[:, bass.ts(kv, 128)],
                                 qb[:, bass.ts(qq, 512)], start=True, stop=True
                                 ).then_inc(ctx.mm.h, 1)
                        if kv == 1 and pending is not None:
                            wev.wait(EVT + e_off(pending, "recip"))
                            t.matmul(PS_D[0:64, :],
                                     ctx.consts["ones_row"][:, 0:64],
                                     rr[:], start=True, stop=True
                                     ).then_inc(ctx.mm.h, 1)
                            pending = None
                        if kv >= 1:
                            pv = kv - 1
                            wsc.wait(s_off(qq, pv))
                            t.matmul(PS_C[0:65, :], vb[:, pv * 72:pv * 72 + 65],
                                     pP[pv % 2][:], start=(pv == 0), stop=False
                                     ).then_inc(ctx.mm.h, 1)
                    wsc.wait(s_off(qq, 15))
                    t.matmul(PS_C[0:65, :], vb[:, 15 * 72:15 * 72 + 65],
                             pP[1][:], start=False, stop=True).then_inc(ctx.mm.h, 1)
                    pending = qq
                wev.wait(EVT + e_off(3, "recip"))
                t.matmul(PS_D[0:64, :], ctx.consts["ones_row"][:, 0:64],
                         rr[:], start=True, stop=True).then_inc(ctx.mm.h, 1)

        @blk.scalar
        def _(s):
            wmm = Walker(ctx, s, ctx.mm.h, 0)
            with loop(ctx, s, ntt) as i:
                wmm.start(m0, [(MMT, i)])
                for qq in range(4):
                    for kv in range(16):
                        wmm.wait(m_off(qq, "scores", kv))
                        s.activation(pP[kv % 2][:], PS_q[kv % 2][:], AF.Exp,
                                     scale=1.0 / 8.0).then_inc(ctx.sc.h, 1)

        @blk.vector
        def _(v):
            wmm = Walker(ctx, v, ctx.mm.h, 0)
            wdg = Walker(ctx, v, ctx.dmg.h, 1)
            with loop(ctx, v, ntt) as i:
                wmm.start(m0, [(MMT, i)])
                wdg.start(g0, [(16, i)])
                wdg.wait(0)
                for qq in range(4):
                    PS_C = PS_C2[qq % 2]
                    wmm.wait(m_off(qq, "pv", 15))
                    v.reciprocal(rr[:], PS_C[64:65, :]).then_inc(ctx.ev.h, 1)
                    wmm.wait(m_off(qq, "bcast"))
                    v.tensor_copy(rb[:], PS_D[0:64, :])
                    for cl in range(4):
                        op = v.tensor_mul(
                            cbT[(cl % 2) * 64:(cl % 2) * 64 + 64,
                                (2 * qq + cl // 2) * 128:(2 * qq + cl // 2 + 1) * 128],
                            PS_C[0:64, bass.ts(cl, 128)],
                            rb[:, bass.ts(cl, 128)])
                        if cl == 3:
                            op.then_inc(ctx.ev.h, 1)

        @blk.gpsimd
        def _(g):
            wev = Walker(ctx, g, ctx.ev.h, 0)
            with loop(ctx, g, ntt) as i:
                wev.start(e0 + EVT, [(EVT, i)])
                wev.wait(0)
                g.dma_start(c3[:, :, bass.ds(i * 128, 128)],
                            cbT[:].rearrange("p (c t) -> p c t", c=NCT)
                            ).then_inc(ctx.dmg.h, 16)
            g.wait_ge(ctx.dmg.h, g0 + ntt * 16)

    ctx.dma.n = d0 + ntt * 48
    ctx.dmg.n = g0 + ntt * 16
    ctx.mm.n = m0 + ntt * MMT
    ctx.ev.n = e0 + ntt * EVT
    ctx.sc.n = s0 + ntt * SCT


def emit_l3_attn(ctx, dd, xin_f32, xin_bf, xout_f32, xout_bf):
    with ctx.scope():
        g = ctx.sb("l3a_gs", [128, NCT], F32)
        b = ctx.sb("l3a_bs", [128, NCT], F32)
        nc = ctx.nc
        with nc.Block() as blk:
            @blk.gpsimd
            def _(gp):
                gp.dma_start(g[:], dd["l3a_g"]).then_inc(ctx.dmg.h, 16)
                gp.dma_start(b[:], dd["l3a_b"]).then_inc(ctx.dmg.h, 16)
                gp.wait_ge(ctx.dmg.h, ctx.dmg.inc(32))
        qT = ctx.dram("l3a_qT", [D, T], BF16)
        kT = ctx.dram("l3a_kT", [D, T], BF16)
        vn = ctx.dram("l3a_vn", [T, NH * 72], BF16)
        cT = ctx.dram("l3a_cT", [D, T], BF16)
        y1 = ctx.dram("l3a_y1", [D, T], F32)
        y2 = ctx.dram("l3a_y2", [D, T], F32)
        with ctx.scope():
            emit_linear(ctx, xin_bf, dd["l3a_wq"], qT, bias_dram=dd["l3a_bq"])
        with ctx.scope():
            emit_linear(ctx, xin_bf, dd["l3a_wk"], kT, bias_dram=dd["l3a_bk"])
        with ctx.scope():
            emit_linear_tok(ctx, xin_bf, dd["l3a_wv"], vn,
                            bias_row_dram=dd["l3a_bv"])
        with ctx.scope():
            emit_l3_attention_core(ctx, qT, kT, vn, cT)
        with ctx.scope():
            emit_linear(ctx, cT, dd["l3a_wo"], y1, bias_dram=dd["l3a_bo"],
                        extra_dram=xin_f32, extra_op=OP.add, extra_dtype=F32,
                        out_dtype=F32)
        with ctx.scope():
            emit_ln(ctx, y1, out_f32_dram=y2, gamma=g[:], beta=b[:])
        with ctx.scope():
            emit_ln(ctx, y2, out_bf_dram=xout_bf, out_f32_dram=xout_f32,
                    res_dram=xin_f32)


def emit_l3_ffn(ctx, dd, xin_f32, xin_bf, xout_f32, xout_bf):
    with ctx.scope():
        cT = ctx.dram("l3f_cT", [8, T], F32)
        ymoe = ctx.dram("l3f_ymoe", [D, T], F32)
        with ctx.scope():
            emit_gate_topk(ctx, xin_bf, dd["l3f_gw"], cT, E=8, topk=2,
                           denom="all", gate_b_dram=dd["l3f_gb"])
        with ctx.scope():
            emit_moe_experts(ctx, xin_bf, dd["l3f_sw1"][0:D, :],
                             dd["l3f_sw2"][0:D, :], dd["l3f_sw3"][0:2048, :],
                             ymoe, n_exp=1, H=2048, const_scale=0.5,
                             accumulate=False, b1_dram=dd["l3f_sb1"][0:128, :],
                             b3_dram=dd["l3f_sb2"][0:128, :],
                             b2_dram=dd["l3f_sb3"][0:1, :],
                             scale_row_dram=dd["c_half_row"])
        with ctx.scope():
            emit_moe_experts(ctx, xin_bf, dd["l3f_sw1"][D:2 * D, :],
                             dd["l3f_sw2"][D:2 * D, :],
                             dd["l3f_sw3"][2048:4096, :],
                             ymoe, n_exp=1, H=2048, const_scale=0.5,
                             accumulate=True, b1_dram=dd["l3f_sb1"][128:256, :],
                             b3_dram=dd["l3f_sb2"][128:256, :],
                             b2_dram=dd["l3f_sb3"][1:2, :],
                             scale_row_dram=dd["c_half_row"])
        with ctx.scope():
            emit_moe_experts(ctx, xin_bf, dd["l3f_rw1"], dd["l3f_rw2"],
                             dd["l3f_rw3"], ymoe, n_exp=8, H=2048, cT_dram=cT,
                             accumulate=True, b1_dram=dd["l3f_rb1"],
                             b3_dram=dd["l3f_rb2"], b2_dram=dd["l3f_rb3"])
        with ctx.scope():
            emit_ln(ctx, ymoe, out_bf_dram=xout_bf, out_f32_dram=xout_f32,
                    res_dram=xin_f32)


STAGE_WEIGHTS = {
    "l0_attn": ["l0a_wq", "l0a_wk", "l0a_wv", "l0a_wo", "l0a_bo", "l0a_ln_g",
                "l0a_ln_b"],
    "l2_attn": ["l2a_WqT", "l2a_WkT", "l2a_WvT", "l2a_WoT", "c_alibi"],
    "l2_ffn": ["l2f_uprows", "l2f_dnrows", "l2f_wexp", "l2f_bexp", "l2f_upbias",
               "l2f_wconT", "l2f_bcon", "l2f_dbias", "c_ones_row_t"],
    "l3_attn": ["l3a_wq", "l3a_bq", "l3a_wk", "l3a_bk", "l3a_wv", "l3a_bv",
                "l3a_wo", "l3a_bo", "l3a_g", "l3a_b"],
    "l3_ffn": ["l3f_gw", "l3f_gb", "l3f_rw1", "l3f_rw2", "l3f_rw3", "l3f_rb1",
               "l3f_rb2", "l3f_rb3", "l3f_sw1", "l3f_sw2", "l3f_sw3", "l3f_sb1",
               "l3f_sb2", "l3f_sb3", "c_half_row"],
    "l1_ffn": ["l1f_Wq", "l1f_Wk", "l1f_Wv", "l1f_bq", "l1f_bk", "l1f_bv",
               "l1f_Wg1", "l1f_bg1", "l1f_Wg2p", "l1f_bg2", "l1f_Wo", "l1f_bo"],
    "l1_attn": ["l1a_W_dq", "l1a_W_uq", "l1a_W_dkv", "l1a_W_uk", "l1a_W_uv",
                "l1a_W_o"],
    "l0_ffn": ["l0f_lin0_w", "l0f_lin0_b", "l0f_sw1_w1", "l0f_sw1_b1", "l0f_sw1_w2",
               "l0f_sw1_b2", "l0f_gwT", "l0f_ew1", "l0f_ew3", "l0f_ew2", "l0f_shw1",
               "l0f_shw3", "l0f_shw2", "l0f_lin1_w", "l0f_lin1_b", "l0f_sw2_w1",
               "l0f_sw2_b1", "l0f_sw2_w2", "l0f_sw2_b2"],
}
CONST_KEYS = ["c_ident", "c_ones_col", "c_ones_row", "c_stair"]
STAGE_FNS = {"l0_attn": emit_l0_attn, "l0_ffn": emit_l0_ffn,
             "l1_attn": emit_l1_attn, "l1_ffn": emit_l1_ffn,
             "l2_attn": emit_l2_attn, "l2_ffn": emit_l2_ffn,
             "l3_attn": emit_l3_attn, "l3_ffn": emit_l3_ffn}
STAGE_ORDER = ["l0_attn", "l0_ffn", "l1_attn", "l1_ffn", "l2_attn", "l2_ffn",
               "l3_attn", "l3_ffn"]


def build_stage_program(stage_names, prep):
    nc = bass.Bass("TRN2", target_bir_lowering=False)
    es = contextlib.ExitStack()
    ctx = Ctx(nc, es)
    dd = {}
    for k in CONST_KEYS + sorted({w for s in stage_names for w in STAGE_WEIGHTS[s]}):
        v = prep[k]
        dd[k] = nc.dram_tensor(k, list(v.shape), F32, kind="ExternalInput").ap()
    xin = nc.dram_tensor("x_in", [D, T], F32, kind="ExternalInput").ap()
    xout = nc.dram_tensor("x_out", [D, T], F32, kind="ExternalOutput").ap()
    with es:
        load_consts(ctx, dd)
        cur = xin
        cur_bf = ctx.dram("xbf_0", [D, T], BF16)
        emit_cast_copy(ctx, cur_bf, cur)
        for i, s in enumerate(stage_names):
            last = i == len(stage_names) - 1
            nxt = xout if last else ctx.dram(f"x_{i+1}", [D, T], F32)
            nxt_bf = ctx.dram(f"xbf_{i+1}", [D, T], BF16)
            STAGE_FNS[s](ctx, dd, cur, cur_bf, nxt, nxt_bf)
            cur, cur_bf = nxt, nxt_bf
    return nc



def kernel(**inputs):
    prep = host_prep(inputs)
    nc = build_stage_program(STAGE_ORDER, prep)
    in_maps = []
    for core in range(NCORES):
        m = {k: prep[k] for k in prep}
        m["x_in"] = per_core_input(inputs, core)
        in_maps.append(m)
    res = run_bass_kernel_spmd(nc, in_maps, core_ids=list(range(NCORES)))
    out = np.empty((NCORES * B_PER_CORE, S, D), np.float32)
    for core in range(NCORES):
        xt = res.results[core]["x_out"]
        out[core * B_PER_CORE:(core + 1) * B_PER_CORE] = (
            xt.T.reshape(B_PER_CORE, S, D))
    return out


# version 05 (final)
# speedup vs baseline: 1.0649x; 1.0649x over previous
import contextlib
import math

import numpy as np

import concourse.bass as bass
import concourse.mybir as mybir
from bass_rust import RegisterHandle, RuntimeValue
from concourse.ordered_engine_set import OrderedEngineSet
from concourse.alu_op_type import AluOpType as OP
from concourse.bass_utils import run_bass_kernel_spmd

F32 = mybir.dt.float32
BF16 = mybir.dt.bfloat16
AF = mybir.ActivationFunctionType
AX = mybir.AxisListType.X

NCORES = 8
DEBUG_TAPS = set()
B_PER_CORE = 2
S = 2048
D = 1024
T = B_PER_CORE * S
NCT = D // 128
NH = 16
HD = 64
EPS = 1e-5
TCH = 512
NCHUNK = T // TCH


class Sem:
    def __init__(self, handle):
        self.h = handle
        self.n = 0

    def inc(self, k=1):
        self.n += k
        return self.n


class Ctx:
    def __init__(self, nc, es: contextlib.ExitStack):
        self.nc = nc
        self.es = es
        self.dma = Sem(es.enter_context(nc.semaphore("s_dma")))
        self.dmg = Sem(es.enter_context(nc.semaphore("s_dmg")))
        self.mm = Sem(es.enter_context(nc.semaphore("s_mm")))
        self.ev = Sem(es.enter_context(nc.semaphore("s_ev")))
        self.sc = Sem(es.enter_context(nc.semaphore("s_sc")))
        self.sq = Sem(es.enter_context(nc.semaphore("s_sq")))
        self.consts = {}
        self._name_i = 0
        self._scopes = [es]
        self._reclaim_pos = 0
        self._freed_tmp = set()
        self.regs = {}
        for eng in (nc.sync, nc.tensor, nc.scalar, nc.vector, nc.gpsimd):
            nm = str(eng.engine)
            self.regs[eng.engine] = {
                "loops": [eng.alloc_register(f"lp{j}_{nm}") for j in range(2)],
                "walk": [eng.alloc_register(f"wk{j}_{nm}") for j in range(5)],
                "idx": [eng.alloc_register(f"ix{j}_{nm}") for j in range(2)],
            }

    @contextlib.contextmanager
    def scope(self):
        with contextlib.ExitStack() as inner:
            self._scopes.append(inner)
            try:
                yield
            finally:
                self._scopes.pop()
        self.reclaim_tmp_registers()

    def reclaim_tmp_registers(self):
        import re as _re
        engs = {"EngineType.SP": self.nc.sync, "EngineType.PE": self.nc.tensor,
                "EngineType.Activation": self.nc.scalar,
                "EngineType.DVE": self.nc.vector, "EngineType.Pool": self.nc.gpsimd}
        insts = list(self.nc.all_instructions())
        for inst in insts[self._reclaim_pos:]:
            s = str(inst)
            for m in _re.finditer(r"([A-Za-z]+_tmp_\d+)", s):
                nm = m.group(1)
                if nm in self._freed_tmp:
                    continue
                self._freed_tmp.add(nm)
                eng = engs.get(str(inst.engine))
                if eng is None:
                    continue
                try:
                    eng.free_register(RegisterHandle(name=nm, engine=inst.engine))
                except Exception:
                    pass
        self._reclaim_pos = len(insts)

    def sb(self, name, shape, dtype):
        self._name_i += 1
        return self._scopes[-1].enter_context(
            self.nc.sbuf_tensor(f"{name}_{self._name_i}", shape, dtype))

    def ps(self, name, shape=(128, 512), dtype=F32):
        self._name_i += 1
        return self._scopes[-1].enter_context(
            self.nc.psum_tensor(f"{name}_{self._name_i}", list(shape), dtype))

    def dram(self, name, shape, dtype, kind="Internal"):
        if name in DEBUG_TAPS:
            kind = "ExternalOutput"
        return self.nc.dram_tensor(name, list(shape), dtype, kind=kind).ap()


def rv(reg, maxv):
    return RuntimeValue(reg, min_val=0, max_val=maxv)


@contextlib.contextmanager
def loop(ctx, eng, n, lvl=0):
    if n == 1:
        yield 0
        return
    reg = ctx.regs[eng.engine]["loops"][lvl]
    b = eng.bass
    loop_id = b.next_id()
    loop_start = f"lp_{loop_id}_loop"
    loop_end = f"lp_{loop_id}_end"
    engines = OrderedEngineSet([eng.engine])
    eng.reg_mov(reg, 0)
    b.br(loop_start, engines=engines)
    with b.body(loop_start, valid_engines=engines):
        yield rv(reg, n - 1)
        eng.reg_add(reg, reg, 1)
        b.br_lt(reg, n, on_true=loop_start, on_false=loop_end, engines=engines)
    b.switch_bb(loop_end)


def combine_idx(ctx, eng, e, coef, i, slot=0, maxv=None):
    if isinstance(e, int):
        if e == 0:
            return i
        if isinstance(i, int):
            return e * coef + i
    reg = ctx.regs[eng.engine]["idx"][slot]
    eng.reg_mov(reg, e)
    if coef != 1:
        eng.reg_mul(reg, reg, coef)
    eng.reg_add(reg, reg, i)
    return rv(reg, maxv if maxv is not None else (1 << 30))


class Walker:

    def __init__(self, ctx, eng, sem, slot):
        self.eng = eng
        self.sem = sem
        self.reg = ctx.regs[eng.engine]["walk"][slot]
        self.cur = 0

    def start(self, base, terms=()):
        eng, reg = self.eng, self.reg
        first = True
        for coef, idx in terms:
            if isinstance(idx, int):
                base += coef * idx
                continue
            if first:
                eng.reg_mov(reg, idx)
                if coef != 1:
                    eng.reg_mul(reg, reg, coef)
                first = False
            else:
                if coef != 1:
                    raise NotImplementedError("second term must have coef 1")
                eng.reg_add(reg, reg, idx)
        if first:
            eng.reg_mov(reg, base)
        elif base:
            eng.reg_add(reg, reg, base)
        self.cur = 0

    def wait(self, off):
        if off < self.cur:
            return
        if off > self.cur:
            self.eng.reg_add(self.reg, self.reg, off - self.cur)
            self.cur = off
        self.eng.wait_ge(self.sem, self.reg)


class Seq:

    def __init__(self, ctx):
        self.ctx = ctx
        self.steps = []

    def op(self, eng, fn, inc=1):
        self.steps.append((eng, fn, inc))

    def dma(self, fn):
        self.steps.append(("gpsimd", fn, 16))

    def dma_s(self, fn):
        self.steps.append(("sync", fn, 16))

    def emit(self):
        ctx = self.ctx
        sem = ctx.sq
        base = sem.n
        cum = [base]
        for (_, _, inc) in self.steps:
            cum.append(cum[-1] + inc)
        with ctx.nc.Block() as blk:
            for eng_name in ("sync", "tensor", "scalar", "vector", "gpsimd"):
                def make(eng_label):
                    def body(eng):
                        last_wait = -1
                        for idx, (en, fn, inc) in enumerate(self.steps):
                            if en != eng_label:
                                continue
                            if cum[idx] > base and cum[idx] != last_wait:
                                eng.wait_ge(sem.h, cum[idx])
                            fn(eng).then_inc(sem.h, inc)
                            last_wait = -1
                        return
                    return body
                getattr(blk, eng_label := eng_name)(make(eng_name))
        sem.n = cum[-1]
        with ctx.nc.Block() as blk2:
            @blk2.gpsimd
            def _(g):
                g.wait_ge(sem.h, sem.n)



def _col_layout(b):
    return np.ascontiguousarray(np.asarray(b, np.float32).reshape(-1, 128).T)


def host_prep(inputs):
    d = {}
    f32 = np.float32
    d["c_ident"] = np.eye(128, dtype=f32)
    d["c_ones_col"] = np.ones((128, 1), dtype=f32)
    d["c_ones_row"] = np.ones((1, 128), dtype=f32)
    stair = np.zeros((4, 128, 512), dtype=f32)
    kk = np.arange(128)[:, None]
    qq = np.arange(512)[None, :]
    for j in range(4):
        stair[j] = ((kk + 128 * j) <= qq).astype(f32)
    d["c_stair"] = np.ascontiguousarray(stair.reshape(512, 512))

    w_qkv = np.asarray(inputs["L0_attn_w_qkv"], f32)
    wq = np.empty((D, D), f32); wk = np.empty((D, D), f32); wv = np.empty((D, D), f32)
    for h in range(NH):
        base = h * 192
        wq[:, h * 64:(h + 1) * 64] = w_qkv[:, base + 0:base + 192:3]
        wk[:, h * 64:(h + 1) * 64] = w_qkv[:, base + 1:base + 192:3]
        wv[:, h * 64:(h + 1) * 64] = w_qkv[:, base + 2:base + 192:3]
    d["l0a_wq"], d["l0a_wk"], d["l0a_wv"] = wq, wk, wv
    d["l0a_wo"] = np.asarray(inputs["L0_attn_w_proj"], f32)
    d["l0a_bo"] = _col_layout(inputs["L0_attn_b_proj"])
    d["l0a_ln_g"] = _col_layout(inputs["L0_attn_ln_scale"])
    d["l0a_ln_b"] = _col_layout(inputs["L0_attn_ln_bias"])

    d["l0f_lin0_w"] = np.asarray(inputs["L0_ffn_lin0_w"], f32)
    d["l0f_lin0_b"] = _col_layout(inputs["L0_ffn_lin0_b"])
    d["l0f_sw1_w1"] = np.asarray(inputs["L0_ffn_swi1_w1"], f32)
    d["l0f_sw1_b1"] = _col_layout(inputs["L0_ffn_swi1_b1"])
    d["l0f_sw1_w2"] = np.asarray(inputs["L0_ffn_swi1_w2"], f32)
    d["l0f_sw1_b2"] = _col_layout(inputs["L0_ffn_swi1_b2"])
    d["l0f_gwT"] = np.ascontiguousarray(np.asarray(inputs["L0_ffn_gate_w"], f32).T)
    d["l0f_ew1"] = np.asarray(inputs["L0_ffn_exp_w1"], f32).reshape(8 * D, D)
    d["l0f_ew3"] = np.asarray(inputs["L0_ffn_exp_w3"], f32).reshape(8 * D, D)
    d["l0f_ew2"] = np.asarray(inputs["L0_ffn_exp_w2"], f32).reshape(8 * D, D)
    d["l0f_shw1"] = np.asarray(inputs["L0_ffn_sh_w1"], f32)
    d["l0f_shw3"] = np.asarray(inputs["L0_ffn_sh_w3"], f32)
    d["l0f_shw2"] = np.asarray(inputs["L0_ffn_sh_w2"], f32)
    d["l0f_lin1_w"] = np.asarray(inputs["L0_ffn_lin1_w"], f32)
    d["l0f_lin1_b"] = _col_layout(inputs["L0_ffn_lin1_b"])
    d["l0f_sw2_w1"] = np.asarray(inputs["L0_ffn_swi2_w1"], f32)
    d["l0f_sw2_b1"] = _col_layout(inputs["L0_ffn_swi2_b1"])
    d["l0f_sw2_w2"] = np.asarray(inputs["L0_ffn_swi2_w2"], f32)
    d["l0f_sw2_b2"] = _col_layout(inputs["L0_ffn_swi2_b2"])

    for k in ["W_dq", "W_uq", "W_dkv", "W_uk", "W_uv", "W_o"]:
        d["l1a_" + k] = np.asarray(inputs["L1_attn_" + k], f32)

    d["l1f_Wq"] = np.asarray(inputs["L1_ffn_Wq"], f32).reshape(4 * D, D)
    d["l1f_Wk"] = np.asarray(inputs["L1_ffn_Wk"], f32).reshape(4 * D, D)
    d["l1f_Wv"] = np.asarray(inputs["L1_ffn_Wv"], f32).reshape(4 * D, D)
    d["l1f_bq"] = np.asarray(inputs["L1_ffn_bq"], f32)
    d["l1f_bk"] = np.asarray(inputs["L1_ffn_bk"], f32)
    d["l1f_bv"] = np.asarray(inputs["L1_ffn_bv"], f32)
    d["l1f_Wg1"] = np.asarray(inputs["L1_ffn_Wg1"], f32)
    d["l1f_bg1"] = np.repeat(np.asarray(inputs["L1_ffn_bg1"], f32).reshape(1, 128), 2, 0)
    d["l1f_Wg2p"] = np.zeros((128, 4), f32)
    d["l1f_Wg2p"][:, :] = np.asarray(inputs["L1_ffn_Wg2"], f32)
    d["l1f_bg2"] = np.repeat(np.asarray(inputs["L1_ffn_bg2"], f32).reshape(1, 4), 2, 0)
    d["l1f_Wo"] = np.asarray(inputs["L1_ffn_Wo"], f32)
    d["l1f_bo"] = _col_layout(inputs["L1_ffn_bo"])

    for k in ["Wq", "Wk", "Wv", "Wo"]:
        d["l2a_" + k + "T"] = np.ascontiguousarray(np.asarray(inputs["L2_attn_" + k], f32).T)
    xsl = (2.0 ** 8) ** (1.0 / NH)
    slopes = np.array([1.0 / xsl ** (i + 1) for i in range(NH)], f32)
    al = np.zeros((NH * 128, 16), f32)
    for h in range(NH):
        for kt in range(16):
            al[h * 128:(h + 1) * 128, kt] = -slopes[h] * (kt * 128 + np.arange(128))
    d["c_alibi"] = al
    d["c_ones_row_t"] = np.ones((1, TCH), f32)
    d["c_half_row"] = np.full((1, TCH), 0.5, f32)

    upw = np.asarray(inputs["L2_ffn_up_weights"], f32)
    dnw = np.asarray(inputs["L2_ffn_down_weights"], f32)
    up_rows = np.stack([np.stack([upw[s, :, 0, 0], upw[s, :, 0, 1],
                                  upw[s, :, 1, 0], upw[s, :, 1, 1]])
                        for s in range(12)])
    dn_rows = np.stack([np.stack([dnw[s, :, 0, 0], dnw[s, :, 1, 0],
                                  dnw[s, :, 0, 1], dnw[s, :, 1, 1]])
                        for s in range(11, -1, -1)])
    d["l2f_uprows"] = np.ascontiguousarray(up_rows.reshape(12 * 4, 2048))
    d["l2f_dnrows"] = np.ascontiguousarray(dn_rows.reshape(12 * 4, 2048))
    d["l2f_wexp"] = np.asarray(inputs["L2_ffn_W_exp"], f32)
    d["l2f_bexp"] = np.asarray(inputs["L2_ffn_b_exp"], f32).reshape(1, 4096)
    d["l2f_upbias"] = np.asarray(inputs["L2_ffn_up_bias"], f32).reshape(1, 4096)
    d["l2f_wconT"] = np.ascontiguousarray(np.asarray(inputs["L2_ffn_W_con"], f32).T)
    d["l2f_bcon"] = _col_layout(inputs["L2_ffn_b_con"])
    d["l2f_dbias"] = np.asarray(inputs["L2_ffn_down_bias"], f32).reshape(1, 1024)

    for k in ["wq", "wk", "wv", "wo"]:
        d["l3a_" + k] = np.asarray(inputs["L3_attn_" + k], f32)
    d["l3a_bq"] = _col_layout(inputs["L3_attn_bq"])
    d["l3a_bk"] = _col_layout(inputs["L3_attn_bk"])
    d["l3a_bv"] = np.asarray(inputs["L3_attn_bv"], f32).reshape(1, 1024)
    d["l3a_bo"] = _col_layout(inputs["L3_attn_bo"])
    d["l3a_g"] = _col_layout(inputs["L3_attn_gamma"])
    d["l3a_b"] = _col_layout(inputs["L3_attn_beta"])

    def _colset(b):
        n, Hh = b.shape
        return np.ascontiguousarray(
            np.asarray(b, f32).reshape(n, Hh // 128, 128).transpose(0, 2, 1)
            .reshape(n * 128, Hh // 128))
    d["l3f_gw"] = np.asarray(inputs["L3_ffn_gate_w"], f32)
    d["l3f_gb"] = np.asarray(inputs["L3_ffn_gate_b"], f32).reshape(1, 8)
    d["l3f_rw1"] = np.asarray(inputs["L3_ffn_rw1"], f32).reshape(8 * D, 2048)
    d["l3f_rw2"] = np.asarray(inputs["L3_ffn_rw2"], f32).reshape(8 * D, 2048)
    d["l3f_rw3"] = np.asarray(inputs["L3_ffn_rw3"], f32).reshape(8 * 2048, D)
    d["l3f_rb1"] = _colset(np.asarray(inputs["L3_ffn_rb1"], f32))
    d["l3f_rb2"] = _colset(np.asarray(inputs["L3_ffn_rb2"], f32))
    d["l3f_rb3"] = np.asarray(inputs["L3_ffn_rb3"], f32)
    d["l3f_sw1"] = np.asarray(inputs["L3_ffn_sw1"], f32).reshape(2 * D, 2048)
    d["l3f_sw2"] = np.asarray(inputs["L3_ffn_sw2"], f32).reshape(2 * D, 2048)
    d["l3f_sw3"] = np.asarray(inputs["L3_ffn_sw3"], f32).reshape(2 * 2048, D)
    d["l3f_sb1"] = _colset(np.asarray(inputs["L3_ffn_sb1"], f32))
    d["l3f_sb2"] = _colset(np.asarray(inputs["L3_ffn_sb2"], f32))
    d["l3f_sb3"] = np.asarray(inputs["L3_ffn_sb3"], f32)
    return d


def per_core_input(inputs, core):
    z = np.asarray(inputs["L0_attn_z"], np.float32)
    zc = z[core * B_PER_CORE:(core + 1) * B_PER_CORE]
    return np.ascontiguousarray(zc.reshape(T, D).T)


SEM_FLOOR = 4096
DMA_FLOOR = 4096 * 16


def fori(eng, n):
    if n == 1:
        return contextlib.nullcontext(0)
    return eng.Fori(0, n)


def load_consts(ctx, d):
    nc = ctx.nc
    with nc.Block() as blk0:
        @blk0.gpsimd
        def _(g):
            g.sem_inc(ctx.dmg.h, DMA_FLOOR)
            g.sem_inc(ctx.sq.h, SEM_FLOOR)
        @blk0.sync
        def _(s):
            s.sem_inc(ctx.dma.h, DMA_FLOOR)
        @blk0.tensor
        def _(t):
            t.sem_inc(ctx.mm.h, SEM_FLOOR)
        @blk0.vector
        def _(v):
            v.sem_inc(ctx.ev.h, SEM_FLOOR)
        @blk0.scalar
        def _(s):
            s.sem_inc(ctx.sc.h, SEM_FLOOR)
    ctx.dmg.n += DMA_FLOOR
    ctx.dma.n += DMA_FLOOR
    ctx.sq.n += SEM_FLOOR
    ctx.mm.n += SEM_FLOOR
    ctx.ev.n += SEM_FLOOR
    ctx.sc.n += SEM_FLOOR
    c = ctx.consts
    c["ones_col"] = ctx.sb("ones_col", [128, 1], F32)
    c["ones_col_bf"] = ctx.sb("ones_col_bf", [128, 1], BF16)
    c["ones_row"] = ctx.sb("ones_row", [1, 128], F32)
    c["ident"] = ctx.sb("ident", [128, 128], F32)
    c["stair"] = ctx.sb("stair", [128, 4 * 512], BF16)
    with nc.Block() as blk:
        @blk.gpsimd
        def _(g):
            g.dma_start(c["ones_col"][:], d["c_ones_col"]).then_inc(ctx.dmg.h, 16)
            g.dma_start(c["ones_col_bf"][:], d["c_ones_col"]).then_inc(ctx.dmg.h, 16)
            g.dma_start(c["ones_row"][:], d["c_ones_row"]).then_inc(ctx.dmg.h, 16)
            g.dma_start(c["ident"][:], d["c_ident"]).then_inc(ctx.dmg.h, 16)
            g.dma_start(c["stair"][:].rearrange("p (j t) -> p j t", j=4),
                        d["c_stair"].rearrange("(j p) t -> p j t", p=128)
                        ).then_inc(ctx.dmg.h, 16)
            g.wait_ge(ctx.dmg.h, ctx.dmg.inc(16 * 5))



def emit_cast_copy(ctx, dst_dram, src_dram):
    nc = ctx.nc
    with nc.Block() as blk:
        @blk.gpsimd
        def _(g):
            g.dma_start(dst_dram, src_dram).then_inc(ctx.dmg.h, 16)
            g.wait_ge(ctx.dmg.h, ctx.dmg.inc(16))


def emit_ln(ctx, x_dram, out_bf_dram=None, out_f32_dram=None, gamma=None, beta=None,
            res_dram=None):
    nc = ctx.nc
    have_res = res_dram is not None
    xs2 = [ctx.sb("ln_x", [128, NCT * TCH], F32) for _ in range(2)]
    tmp2 = [ctx.sb("ln_tmp", [128, NCT * TCH], F32) for _ in range(2)]
    xb = ctx.sb("ln_xb", [128, NCT * TCH], BF16)
    sq = ctx.sb("ln_sq", [128, NCT * TCH], BF16)
    rows = ctx.sb("ln_rows", [1, 6 * TCH], F32)
    ob2 = [ctx.sb("ln_ob", [128, NCT * TCH], BF16) for _ in range(2)] \
        if out_bf_dram is not None else None
    of2 = [ctx.sb("ln_of", [128, NCT * TCH], F32) for _ in range(2)] \
        if out_f32_dram is not None else None
    PS_S, PS_Q, PS_R, PS_M = (ctx.ps("ln_ps"), ctx.ps("ln_pq"), ctx.ps("ln_pr"),
                              ctx.ps("ln_pm"))

    x3 = x_dram.rearrange("(c p) t -> p c t", p=128)
    r3 = res_dram.rearrange("(c p) t -> p c t", p=128) if have_res else None
    ob3 = out_bf_dram.rearrange("(c p) t -> p c t", p=128) if out_bf_dram is not None else None
    of3 = out_f32_dram.rearrange("(c p) t -> p c t", p=128) if out_f32_dram is not None else None

    d0, m0, e0, s0, g0 = ctx.dma.n, ctx.mm.n, ctx.ev.n, ctx.sc.n, ctx.dmg.n
    LD = 2 if have_res else 1
    RESOFF = 1 if have_res else 0
    EV = RESOFF + 2 + NCT
    SC = 2
    MM = 2
    ST = (1 if ob2 is not None else 0) + (1 if of2 is not None else 0)

    def r_row(j, w=TCH):
        return rows[0:1, j * TCH:j * TCH + w]

    with nc.Block() as blk:
        @blk.sync
        def _(sync):
            wev = Walker(ctx, sync, ctx.ev.h, 0)
            with loop(ctx, sync, NCHUNK // 2) as i2:
                for u in range(2):
                    wev.start(e0 - EV + u * EV, [(2 * EV, i2)])
                    wev.wait(0)
                    sync.dma_start(xs2[u][:].rearrange("p (c t) -> p c t", c=NCT),
                                   x3[:, :, bass.ds((i2 * 2 + u) * TCH, TCH)]
                                   ).then_inc(ctx.dma.h, 16)
                    if have_res:
                        sync.dma_start(tmp2[u][:].rearrange("p (c t) -> p c t", c=NCT),
                                       r3[:, :, bass.ds((i2 * 2 + u) * TCH, TCH)]
                                       ).then_inc(ctx.dma.h, 16)

        @blk.vector
        def _(v):
            wdma = Walker(ctx, v, ctx.dma.h, 0)
            wmm = Walker(ctx, v, ctx.mm.h, 1)
            wsc = Walker(ctx, v, ctx.sc.h, 2)
            wdg = Walker(ctx, v, ctx.dmg.h, 3)
            with loop(ctx, v, NCHUNK // 2) as i2:
                for u in range(2):
                    wdma.start(d0 + (u + 1) * 16 * LD, [(2 * 16 * LD, i2)])
                    wdma.wait(0)
                    if have_res:
                        v.tensor_add(xs2[u][:], xs2[u][:], tmp2[u][:]
                                     ).then_inc(ctx.ev.h, 1)
                    wmm.start(m0 + u * MM + 1, [(2 * MM, i2)])
                    wmm.wait(0)
                    v.tensor_scalar(r_row(0), PS_S[0:1, :], 1.0 / D, None, OP.mult)
                    v.tensor_scalar(r_row(1), PS_Q[0:1, :], 1.0 / D, None, OP.mult)
                    v.tensor_mul(r_row(2), r_row(0), r_row(0))
                    v.tensor_tensor(r_row(2), r_row(1), r_row(2), OP.subtract)
                    v.tensor_scalar(r_row(2), r_row(2), EPS, None, OP.add)
                    v.reciprocal(r_row(3), r_row(2)).then_inc(ctx.ev.h, 1)
                    wsc.start(s0 + u * SC + 2, [(2 * SC, i2)])
                    wsc.wait(0)
                    v.tensor_mul(r_row(5), r_row(0), r_row(4))
                    v.tensor_scalar(r_row(5), r_row(5), -1.0, None, OP.mult
                                    ).then_inc(ctx.ev.h, 1)
                    wmm.wait(1)
                    if ST:
                        wdg.start(g0 + (u - 1) * ST * 16, [(2 * ST * 16, i2)])
                        wdg.wait(0)
                    ob = ob2[u] if ob2 is not None else None
                    of = of2[u] if of2 is not None else None
                    for ct in range(NCT):
                        sl = slice(ct * TCH, (ct + 1) * TCH)
                        v.tensor_mul(tmp2[u][:, sl], xs2[u][:, sl], PS_R[:, :])
                        v.tensor_add(tmp2[u][:, sl], tmp2[u][:, sl], PS_M[:, :])
                        last = None
                        if gamma is not None:
                            if of is not None:
                                last = v.tensor_scalar(of[:, sl], tmp2[u][:, sl],
                                                       gamma[:, ct:ct + 1],
                                                       beta[:, ct:ct + 1],
                                                       OP.mult, OP.add)
                            if ob is not None:
                                last = v.tensor_scalar(ob[:, sl], tmp2[u][:, sl],
                                                       gamma[:, ct:ct + 1],
                                                       beta[:, ct:ct + 1],
                                                       OP.mult, OP.add)
                        else:
                            if of is not None:
                                last = v.tensor_copy(of[:, sl], tmp2[u][:, sl])
                            if ob is not None:
                                last = v.tensor_copy(ob[:, sl], tmp2[u][:, sl])
                        (last if last is not None else v.engine_nop()
                         ).then_inc(ctx.ev.h, 1)

        @blk.scalar
        def _(s):
            wev = Walker(ctx, s, ctx.ev.h, 0)
            wdma = Walker(ctx, s, ctx.dma.h, 1)
            with loop(ctx, s, NCHUNK // 2) as i2:
                for u in range(2):
                    if have_res:
                        wev.start(e0 + u * EV + 1, [(2 * EV, i2)])
                        wev.wait(0)
                    else:
                        wdma.start(d0 + (u + 1) * 16 * LD, [(2 * 16 * LD, i2)])
                        wdma.wait(0)
                    s.activation(xb[:], xs2[u][:], AF.Copy)
                    s.activation(sq[:], xs2[u][:], AF.Square).then_inc(ctx.sc.h, 1)
                    if have_res:
                        wev.wait(1)
                    else:
                        wev.start(e0 + u * EV + RESOFF + 1, [(2 * EV, i2)])
                        wev.wait(0)
                    s.activation(r_row(4), r_row(3), AF.Sqrt).then_inc(ctx.sc.h, 1)

        @blk.tensor
        def _(t):
            onebf = ctx.consts["ones_col_bf"]
            oner = ctx.consts["ones_row"]
            wsc = Walker(ctx, t, ctx.sc.h, 0)
            wev = Walker(ctx, t, ctx.ev.h, 1)
            with loop(ctx, t, NCHUNK // 2) as i2:
                for u in range(2):
                    wsc.start(s0 + u * SC + 1, [(2 * SC, i2)])
                    wsc.wait(0)
                    wev.start(e0 + u * EV, [(2 * EV, i2)])
                    wev.wait(0)
                    for ct in range(NCT):
                        t.matmul(PS_S[0:1, :], onebf[:], xb[:, bass.ts(ct, TCH)],
                                 start=(ct == 0), stop=(ct == NCT - 1))
                    mm_i = None
                    for ct in range(NCT):
                        mm_i = t.matmul(PS_Q[0:1, :], onebf[:], sq[:, bass.ts(ct, TCH)],
                                        start=(ct == 0), stop=(ct == NCT - 1))
                    mm_i.then_inc(ctx.mm.h, 1)
                    wev.wait(RESOFF + 2)
                    t.matmul(PS_R[:, :], oner[:], r_row(4), start=True, stop=True)
                    t.matmul(PS_M[:, :], oner[:], r_row(5), start=True,
                             stop=True).then_inc(ctx.mm.h, 1)

        @blk.gpsimd
        def _(g):
            wev = Walker(ctx, g, ctx.ev.h, 0)
            with loop(ctx, g, NCHUNK // 2) as i2:
                for u in range(2):
                    wev.start(e0 + (u + 1) * EV, [(2 * EV, i2)])
                    wev.wait(0)
                    if ob2 is not None:
                        g.dma_start(ob3[:, :, bass.ds((i2 * 2 + u) * TCH, TCH)],
                                    ob2[u][:].rearrange("p (c t) -> p c t", c=NCT)
                                    ).then_inc(ctx.dmg.h, 16)
                    if of2 is not None:
                        g.dma_start(of3[:, :, bass.ds((i2 * 2 + u) * TCH, TCH)],
                                    of2[u][:].rearrange("p (c t) -> p c t", c=NCT)
                                    ).then_inc(ctx.dmg.h, 16)
            if ST:
                g.wait_ge(ctx.dmg.h, g0 + NCHUNK * ST * 16)

    ctx.dma.n = d0 + NCHUNK * 16 * LD
    ctx.dmg.n = g0 + NCHUNK * ST * 16
    ctx.mm.n = m0 + NCHUNK * MM
    ctx.ev.n = e0 + NCHUNK * EV
    ctx.sc.n = s0 + NCHUNK * SC



def emit_linear(ctx, x_dram, w_dram, out_dram, *, C=D, F=D, Tn=T,
                bias_dram=None, act=None, scale=1.0,
                extra_dram=None, extra_op=None, extra_dtype=BF16,
                out_dtype=BF16, out2_dram=None, bias2_row_dram=None,
                ones_row_t_dram=None):
    nc = ctx.nc
    ncc, nf = C // 128, F // 128
    nch = Tn // TCH
    assert nch % 2 == 0
    NB = min(nf, 4)
    have_extra = extra_dram is not None
    dbl_ex = have_extra and extra_dtype == BF16
    func = act if act is not None else AF.Identity
    wsb = ctx.sb("lin_w", [128, ncc * F], BF16)
    bsb = ctx.sb("lin_b", [128, nf], F32) if bias_dram is not None else None
    xs2 = [ctx.sb("lin_x", [128, ncc * TCH], BF16) for _ in range(2)]
    os_ = ctx.sb("lin_o", [128, nf * TCH], out_dtype)
    os2 = ctx.sb("lin_o2", [128, nf * TCH], BF16) if out2_dram is not None else None
    b2s = ctx.sb("lin_b2", [1, F], BF16) if bias2_row_dram is not None else None
    orow = ctx.sb("lin_or", [1, TCH], BF16) if bias2_row_dram is not None else None
    oacc = ctx.sb("lin_acc", [128, nf * TCH], F32) if have_extra else None
    ex2 = [ctx.sb("lin_ex", [128, nf * TCH], extra_dtype)
           for _ in range(2 if dbl_ex else 1)] if have_extra else None
    pbk = [ctx.ps("lin_p") for _ in range(NB)]

    x3 = x_dram.rearrange("(c p) t -> p c t", p=128)
    o3 = out_dram.rearrange("(f p) t -> p f t", p=128)
    o23 = out2_dram.rearrange("(f p) t -> p f t", p=128) if out2_dram is not None else None
    e3 = extra_dram.rearrange("(f p) t -> p f t", p=128) if have_extra else None
    w3 = w_dram.rearrange("(c p) f -> p c f", p=128)

    d0, m0, e0, s0, g0 = ctx.dma.n, ctx.mm.n, ctx.ev.n, ctx.sc.n, ctx.dmg.n
    LD = 2 if have_extra else 1
    EV = 1 if have_extra else 0
    NST = 1 + (1 if out2_dram is not None else 0)
    bofs = 16 if bias_dram is not None else 0
    WL2 = 48 if bias2_row_dram is not None else 16

    with nc.Block() as blk:
        @blk.gpsimd
        def _(g):
            g.dma_start(wsb[:].rearrange("p (c f) -> p c f", c=ncc), w3
                        ).then_inc(ctx.dmg.h, 16)
            if bias2_row_dram is not None:
                g.dma_start(b2s[:], bias2_row_dram).then_inc(ctx.dmg.h, 16)
                g.dma_start(orow[:], ones_row_t_dram).then_inc(ctx.dmg.h, 16)
            wk = (Walker(ctx, g, ctx.ev.h, 0) if have_extra
                  else Walker(ctx, g, ctx.sc.h, 0))
            with loop(ctx, g, nch) as i:
                if have_extra:
                    wk.start(e0 + EV, [(EV, i)])
                else:
                    wk.start(s0 + nf, [(nf, i)])
                wk.wait(0)
                g.dma_start(o3[:, :, bass.ds(i * TCH, TCH)],
                            os_[:].rearrange("p (f t) -> p f t", f=nf)
                            ).then_inc(ctx.dmg.h, 16)
                if out2_dram is not None:
                    g.dma_start(o23[:, :, bass.ds(i * TCH, TCH)],
                                os2[:].rearrange("p (f t) -> p f t", f=nf)
                                ).then_inc(ctx.dmg.h, 16)
            g.wait_ge(ctx.dmg.h, g0 + WL2 + nch * NST * 16)

        @blk.sync
        def _(sync):
            if bias_dram is not None:
                sync.dma_start(bsb[:], bias_dram).then_inc(ctx.dma.h, 16)
            wmm = Walker(ctx, sync, ctx.mm.h, 0)
            wev = Walker(ctx, sync, ctx.ev.h, 1)
            with loop(ctx, sync, nch // 2) as i2:
                for u in range(2):
                    wmm.start(m0 - nf, [(2 * nf, i2)])
                    wmm.wait(u * nf)
                    sync.dma_start(xs2[u][:].rearrange("p (c t) -> p c t", c=ncc),
                                   x3[:, :, bass.ds((i2 * 2 + u) * TCH, TCH)]
                                   ).then_inc(ctx.dma.h, 16)
                    if have_extra:
                        wev.start(e0 - EV if dbl_ex else e0, [(2 * EV, i2)])
                        wev.wait(u * EV if dbl_ex else u * EV + 0)
                        sync.dma_start(
                            ex2[u if dbl_ex else 0][:].rearrange(
                                "p (f t) -> p f t", f=nf),
                            e3[:, :, bass.ds((i2 * 2 + u) * TCH, TCH)]
                            ).then_inc(ctx.dma.h, 16)

        @blk.tensor
        def _(t):
            t.wait_ge(ctx.dmg.h, g0 + WL2)
            wdma = Walker(ctx, t, ctx.dma.h, 0)
            wsc = Walker(ctx, t, ctx.sc.h, 1)
            with loop(ctx, t, nch // 2) as i2:
                for u in range(2):
                    wdma.start(d0 + bofs + 16, [(2 * LD * 16, i2)])
                    wdma.wait(u * LD * 16)
                    wsc.start(s0 - NB + 1 + u * nf, [(2 * nf, i2)])
                    for ft in range(nf):
                        wsc.wait(ft)
                        pb = pbk[ft % NB]
                        for ct in range(ncc):
                            mmi = t.matmul(pb[:],
                                           wsb[:, ct * F + ft * 128:ct * F + (ft + 1) * 128],
                                           xs2[u][:, bass.ts(ct, TCH)],
                                           start=(ct == 0),
                                           stop=(ct == ncc - 1) and bias2_row_dram is None)
                        if bias2_row_dram is not None:
                            mmi = t.matmul(pb[:], b2s[0:1, ft * 128:(ft + 1) * 128],
                                           orow[:], start=False, stop=True)
                        mmi.then_inc(ctx.mm.h, 1)

        @blk.scalar
        def _(s):
            if bias_dram is not None:
                s.wait_ge(ctx.dma.h, d0 + 16)
            wdg = Walker(ctx, s, ctx.dmg.h, 0)
            wmm = Walker(ctx, s, ctx.mm.h, 1)
            with loop(ctx, s, nch) as i:
                if not have_extra:
                    wdg.start(g0 + WL2, [(NST * 16, i)])
                    wdg.wait(0)
                wmm.start(m0 + 1, [(nf, i)])
                for ft in range(nf):
                    wmm.wait(ft)
                    dst = oacc if have_extra else os_
                    s.activation(dst[:, bass.ts(ft, TCH)], pbk[ft % NB][:],
                                 func, bias=(bsb[:, ft:ft + 1] if bsb is not None else 0.0),
                                 scale=scale).then_inc(ctx.sc.h, 1)

        if have_extra:
            @blk.vector
            def _(v):
                wsc = Walker(ctx, v, ctx.sc.h, 0)
                wdma = Walker(ctx, v, ctx.dma.h, 1)
                wdg = Walker(ctx, v, ctx.dmg.h, 2)
                with loop(ctx, v, nch // 2) as i2:
                    for u in range(2):
                        wsc.start(s0 + (u + 1) * nf, [(2 * nf, i2)])
                        wsc.wait(0)
                        wdma.start(d0 + bofs + (u + 1) * LD * 16, [(2 * LD * 16, i2)])
                        wdma.wait(0)
                        wdg.start(g0 + WL2 + u * NST * 16, [(2 * NST * 16, i2)])
                        wdg.wait(0)
                        exb = ex2[u if dbl_ex else 0][:]
                        if out2_dram is not None:
                            v.tensor_tensor(os_[:], oacc[:], exb, extra_op)
                            v.tensor_copy(os2[:], os_[:]).then_inc(ctx.ev.h, 1)
                        else:
                            v.tensor_tensor(os_[:], oacc[:], exb, extra_op
                                            ).then_inc(ctx.ev.h, 1)

    ctx.dma.n = d0 + bofs + nch * LD * 16
    ctx.dmg.n = g0 + WL2 + nch * NST * 16
    ctx.mm.n = m0 + nch * nf
    ctx.sc.n = s0 + nch * nf
    ctx.ev.n = e0 + nch * EV



def emit_linear_tok(ctx, x_dram, w_dram, vn_dram, *, C=D, F=D, Tn=T,
                    bias_row_dram=None):
    nc = ctx.nc
    ncc, nf5 = C // 128, F // 512
    nhh = F // 64
    nch = Tn // TCH
    have_b = bias_row_dram is not None
    wsb = ctx.sb("lt_w", [128, ncc * F], BF16)
    xs = ctx.sb("lt_x", [128, ncc * TCH], BF16)
    vnb = ctx.sb("lt_v", [128, 4 * nhh * 72], BF16)
    pbk = [ctx.ps("lt_p") for _ in range(4)]
    brow = ctx.sb("lt_br", [1, F], F32) if have_b else None
    bbc = ctx.sb("lt_bb", [128, F], F32) if have_b else None
    pbb = [ctx.ps("lt_pb") for _ in range(2)] if have_b else None

    x3 = x_dram.rearrange("(c p) t -> p c t", p=128)
    w3 = w_dram.rearrange("(c p) f -> p c f", p=128)
    vn3 = vn_dram.rearrange("(u p) f -> p u f", p=128)

    d0, m0, e0, s0, g0 = ctx.dma.n, ctx.mm.n, ctx.ev.n, ctx.sc.n, ctx.dmg.n
    GRP = 4 * nf5
    EV = GRP + 1
    MOF = nf5 if have_b else 0
    EOF_ = 1 if have_b else 0
    WLD = (32 if have_b else 16)

    with nc.Block() as blk:
        @blk.gpsimd
        def _(g):
            g.dma_start(wsb[:].rearrange("p (c f) -> p c f", c=ncc), w3
                        ).then_inc(ctx.dmg.h, 16)
            if have_b:
                g.dma_start(brow[:], bias_row_dram).then_inc(ctx.dmg.h, 16)
            wev = Walker(ctx, g, ctx.ev.h, 0)
            with loop(ctx, g, nch) as i:
                wev.start(e0 + EOF_ + EV, [(EV, i)])
                wev.wait(0)
                g.dma_start(vn3[:, bass.ds(i * 4, 4), :],
                            vnb[:].rearrange("p (u f) -> p u f", u=4)
                            ).then_inc(ctx.dmg.h, 16)
            g.wait_ge(ctx.dmg.h, g0 + WLD + nch * 16)

        @blk.sync
        def _(sync):
            wmm = Walker(ctx, sync, ctx.mm.h, 0)
            with loop(ctx, sync, nch) as i:
                wmm.start(m0 + MOF, [(GRP, i)])
                wmm.wait(0)
                sync.dma_start(xs[:].rearrange("p (c t) -> p c t", c=ncc),
                               x3[:, :, bass.ds(i * TCH, TCH)]).then_inc(ctx.dma.h, 16)

        @blk.tensor
        def _(t):
            t.wait_ge(ctx.dmg.h, g0 + WLD)
            if have_b:
                assert nf5 <= 2
                for j in range(nf5):
                    t.matmul(pbb[j % 2][:], ctx.consts["ones_row"][:],
                             brow[0:1, bass.ts(j, 512)], start=True, stop=True
                             ).then_inc(ctx.mm.h, 1)
            wdma = Walker(ctx, t, ctx.dma.h, 0)
            wev = Walker(ctx, t, ctx.ev.h, 1)
            with loop(ctx, t, nch) as i:
                wdma.start(d0 + 16, [(16, i)])
                wdma.wait(0)
                wev.start(e0 + EOF_, [(EV, i)])
                wev.wait(0)
                for tt in range(4):
                    for fh in range(nf5):
                        r = tt * nf5 + fh
                        if r >= 4:
                            wev.wait(r - 2)
                        pb = pbk[r % 4]
                        for ct in range(ncc):
                            mmi = t.matmul(pb[:], xs[:, ct * TCH + tt * 128:ct * TCH + (tt + 1) * 128],
                                           wsb[:, ct * F + fh * 512:ct * F + (fh + 1) * 512],
                                           start=(ct == 0), stop=(ct == ncc - 1))
                        mmi.then_inc(ctx.mm.h, 1)

        @blk.vector
        def _(v):
            if have_b:
                wmm0 = Walker(ctx, v, ctx.mm.h, 2)
                wmm0.start(m0 + nf5, [])
                wmm0.wait(0)
                for j in range(nf5):
                    v.tensor_copy(bbc[:, bass.ts(j, 512)], pbb[j % 2][:])
                v.engine_nop().then_inc(ctx.ev.h, 1)
            wdg = Walker(ctx, v, ctx.dmg.h, 0)
            wmm = Walker(ctx, v, ctx.mm.h, 1)
            with loop(ctx, v, nch) as i:
                wdg.start(g0 + WLD, [(16, i)])
                wdg.wait(0)
                wmm.start(m0 + MOF + 1, [(GRP, i)])
                v.memset(vnb[:], 1.0).then_inc(ctx.ev.h, 1)
                for tt in range(4):
                    for fh in range(nf5):
                        r = tt * nf5 + fh
                        wmm.wait(r)
                        dst = vnb[:, tt * nhh * 72 + fh * 8 * 72:
                                  tt * nhh * 72 + (fh * 8 + 8) * 72
                                  ].rearrange("p (h f) -> p h f", h=8)[:, :, 0:64]
                        if have_b:
                            v.tensor_tensor(dst, pbk[r % 4][:].rearrange(
                                "p (h f) -> p h f", h=8),
                                bbc[:, bass.ts(fh, 512)].rearrange(
                                "p (h f) -> p h f", h=8), OP.add
                                ).then_inc(ctx.ev.h, 1)
                        else:
                            v.tensor_copy(dst, pbk[r % 4][:].rearrange(
                                "p (h f) -> p h f", h=8)).then_inc(ctx.ev.h, 1)

    ctx.dma.n = d0 + nch * 16
    ctx.dmg.n = g0 + WLD + nch * 16
    ctx.mm.n = m0 + MOF + nch * GRP
    ctx.ev.n = e0 + EOF_ + nch * EV
    ctx.sc.n = s0



def emit_attention(ctx, qT_sc, kT_sc, vn_sc, ctxT_sc, *, nbatch=B_PER_CORE, nh=NH,
                   causal=False, alibi_dram=None, scale=1.0 / 8.0, sq=S):
    nc = ctx.nc
    nkt = sq // 128
    nqc = sq // TCH
    stair = ctx.consts["stair"]
    qh = [ctx.sb("at_q", [64, sq], BF16) for _ in range(2)]
    kh = [ctx.sb("at_k", [64, sq], BF16) for _ in range(2)]
    vh = [ctx.sb("at_v", [128, nkt * 72], BF16) for _ in range(2)]
    ac = [ctx.sb("at_a", [128, nkt], F32) for _ in range(2)] if alibi_dram is not None else None
    pP = [ctx.sb("at_p", [128, TCH], BF16) for _ in range(2)]
    rr = ctx.sb("at_rr", [1, TCH], F32)
    rb = ctx.sb("at_rb", [64, TCH], F32)
    cb = [ctx.sb("at_c", [64, sq], BF16) for _ in range(2)]
    PS_s = [ctx.ps("at_ps") for _ in range(2)]
    PS_C2 = [ctx.ps("at_pc") for _ in range(2)]
    PS_D = ctx.ps("at_pd")

    q4 = qT_sc.rearrange("(h p) t -> p h t", p=64)
    k4 = kT_sc.rearrange("(h p) t -> p h t", p=64)
    c4 = ctxT_sc.rearrange("(h p) t -> p h t", p=64)
    vn3 = vn_sc.rearrange("(u p) f -> p u f", p=128)
    a3 = alibi_dram.rearrange("(h p) k -> p h k", p=128) if alibi_dram is not None else None

    d0, m0, e0, s0, g0 = ctx.dma.n, ctx.mm.n, ctx.ev.n, ctx.sc.n, ctx.dmg.n
    LDJ = 3 + (1 if alibi_dram is not None else 0)
    kts_per_qc = [list(range(4 * (qc + 1))) if causal else list(range(nkt))
                  for qc in range(nqc)]
    nsc_qc = [len(k) for k in kts_per_qc]
    SCBH = sum(nsc_qc)
    MMQC = [2 * n + 1 for n in nsc_qc]
    MMBH = sum(MMQC)
    NMASK = 4 if causal else 0
    EVQC = 2 + NMASK
    EVBH = nqc * EVQC

    def s_off(qc, pos):
        return sum(nsc_qc[:qc]) + pos + 1

    def _C(q):
        return 2 * sum(nsc_qc[:q]) + (q - 1 if q >= 1 else 0)

    def m_off(qc, kind, pos=0):
        n = nsc_qc[qc]
        hb = 1 if qc >= 1 else 0
        if kind == "scores":
            off = pos + 1 if pos <= 1 else 2 * pos + hb
        elif kind == "pv":
            off = (2 * n + hb) if pos == n - 1 else (2 * pos + 3 + hb)
        else:
            return _C(qc + 1) + 3 if qc <= nqc - 2 else MMBH
        return _C(qc) + off

    def e_off(qc, kind, idx=0):
        base = qc * EVQC
        if kind == "mask":
            return base + idx + 1
        return base + NMASK + (1 if kind == "recip" else 2)

    with nc.Block() as blk:
        @blk.sync
        def _(sync):
            wmm = Walker(ctx, sync, ctx.mm.h, 0)
            for b in range(nbatch):
                with loop(ctx, sync, nh // 2) as hh:
                    for u in range(2):
                        jb = (b * nh + u - 1) * MMBH
                        wmm.start(m0 + jb, [(2 * MMBH, hh)])
                        wmm.wait(0)
                        h = 2 * hh + u
                        sync.dma_start(qh[u][:], q4[:, bass.ds(h, 1),
                                                    b * sq:(b + 1) * sq].squeeze(1)
                                       ).then_inc(ctx.dma.h, 16)
                        sync.dma_start(kh[u][:], k4[:, bass.ds(h, 1),
                                                    b * sq:(b + 1) * sq].squeeze(1)
                                       ).then_inc(ctx.dma.h, 16)
                        sync.dma_start(vh[u][:].rearrange("p (u2 f) -> p u2 f", u2=nkt),
                                       vn3[:, b * nkt:(b + 1) * nkt,
                                           bass.ds(h * 72, 72)]
                                       ).then_inc(ctx.dma.h, 16)
                        if alibi_dram is not None:
                            sync.dma_start(ac[u][:], a3[:, bass.ds(h, 1), :].squeeze(1)
                                           ).then_inc(ctx.dma.h, 16)

        @blk.tensor
        def _(t):
            wdma = Walker(ctx, t, ctx.dma.h, 0)
            wev = Walker(ctx, t, ctx.ev.h, 1)
            wsc = Walker(ctx, t, ctx.sc.h, 2)
            for b in range(nbatch):
                with loop(ctx, t, nh // 2) as hh:
                    for u in range(2):
                        j = b * nh + u
                        wdma.start(d0 + (j + 1) * 16 * LDJ, [(2 * 16 * LDJ, hh)])
                        wdma.wait(0)
                        wev.start(e0 + (j - 1) * EVBH, [(2 * EVBH, hh)])
                        wsc.start(s0 + j * SCBH, [(2 * SCBH, hh)])
                        q, k, vv = qh[u], kh[u], vh[u]
                        pending_bcast = None
                        for qc in range(nqc):
                            kts = kts_per_qc[qc]
                            n = len(kts)
                            PS_C = PS_C2[qc % 2]
                            if qc >= 2:
                                wev.wait(EVBH + e_off(qc - 2, "evac"))
                            else:
                                wev.wait(e_off(nqc - 2 + qc, "evac"))
                            for pos, kt in enumerate(kts):
                                if pos >= 2:
                                    wsc.wait(s_off(qc, pos - 2))
                                elif qc >= 1:
                                    wsc.wait(s_off(qc - 1, nsc_qc[qc - 1] - 2 + pos))
                                t.matmul(PS_s[pos % 2][:], k[:, bass.ts(kt, 128)],
                                         q[:, bass.ts(qc, TCH)], start=True, stop=True
                                         ).then_inc(ctx.mm.h, 1)
                                if pos == 1 and pending_bcast is not None:
                                    wev.wait(EVBH + e_off(pending_bcast, "recip"))
                                    t.matmul(PS_D[0:64, :],
                                             ctx.consts["ones_row"][:, 0:64],
                                             rr[:], start=True, stop=True
                                             ).then_inc(ctx.mm.h, 1)
                                    pending_bcast = None
                                if pos >= 1:
                                    pv = pos - 1
                                    jj = kts[pv] - 4 * qc
                                    if causal and 0 <= jj < 4:
                                        wev.wait(EVBH + e_off(qc, "mask", jj))
                                    else:
                                        wsc.wait(s_off(qc, pv))
                                    t.matmul(PS_C[0:65, :],
                                             vv[:, kts[pv] * 72:kts[pv] * 72 + 65],
                                             pP[pv % 2][:], start=(pv == 0), stop=False
                                             ).then_inc(ctx.mm.h, 1)
                            pv = n - 1
                            jj = kts[pv] - 4 * qc
                            if causal and 0 <= jj < 4:
                                wev.wait(EVBH + e_off(qc, "mask", jj))
                            else:
                                wsc.wait(s_off(qc, pv))
                            t.matmul(PS_C[0:65, :], vv[:, kts[pv] * 72:kts[pv] * 72 + 65],
                                     pP[pv % 2][:], start=(pv == 0), stop=True
                                     ).then_inc(ctx.mm.h, 1)
                            pending_bcast = qc
                        wev.wait(EVBH + e_off(nqc - 1, "recip"))
                        t.matmul(PS_D[0:64, :], ctx.consts["ones_row"][:, 0:64],
                                 rr[:], start=True, stop=True).then_inc(ctx.mm.h, 1)

        @blk.scalar
        def _(s):
            wmm = Walker(ctx, s, ctx.mm.h, 0)
            for b in range(nbatch):
                with loop(ctx, s, nh // 2) as hh:
                    for u in range(2):
                        j = b * nh + u
                        wmm.start(m0 + j * MMBH, [(2 * MMBH, hh)])
                        for qc in range(nqc):
                            for pos, kt in enumerate(kts_per_qc[qc]):
                                wmm.wait(m_off(qc, "scores", pos))
                                bias = (ac[u][:, kt:kt + 1]
                                        if alibi_dram is not None else 0.0)
                                s.activation(pP[pos % 2][:], PS_s[pos % 2][:], AF.Exp,
                                             bias=bias, scale=scale
                                             ).then_inc(ctx.sc.h, 1)

        @blk.vector
        def _(v):
            wsc = Walker(ctx, v, ctx.sc.h, 0)
            wmm = Walker(ctx, v, ctx.mm.h, 1)
            wdg = Walker(ctx, v, ctx.dmg.h, 2)
            for b in range(nbatch):
                with loop(ctx, v, nh // 2) as hh:
                    for u in range(2):
                        j = b * nh + u
                        wsc.start(s0 + j * SCBH, [(2 * SCBH, hh)])
                        wmm.start(m0 + j * MMBH, [(2 * MMBH, hh)])
                        wdg.start(g0 + (j - 1) * 16, [(2 * 16, hh)])
                        wdg.wait(0)
                        for qc in range(nqc):
                            kts = kts_per_qc[qc]
                            n = len(kts)
                            PS_C = PS_C2[qc % 2]
                            if causal:
                                for pos, kt in enumerate(kts):
                                    jj = kt - 4 * qc
                                    if 0 <= jj < 4:
                                        wsc.wait(s_off(qc, pos))
                                        v.tensor_mul(pP[pos % 2][:], pP[pos % 2][:],
                                                     stair[:, bass.ts(jj, 512)]
                                                     ).then_inc(ctx.ev.h, 1)
                            wmm.wait(m_off(qc, "pv", n - 1))
                            v.reciprocal(rr[:], PS_C[64:65, :]).then_inc(ctx.ev.h, 1)
                            wmm.wait(m_off(qc, "bcast"))
                            v.tensor_copy(rb[:], PS_D[0:64, :])
                            v.tensor_mul(cb[u][:, bass.ts(qc, TCH)], PS_C[0:64, :],
                                         rb[:]).then_inc(ctx.ev.h, 1)

        @blk.gpsimd
        def _(g):
            wev = Walker(ctx, g, ctx.ev.h, 0)
            for b in range(nbatch):
                with loop(ctx, g, nh // 2) as hh:
                    for u in range(2):
                        j = b * nh + u
                        h = 2 * hh + u
                        wev.start(e0 + (j + 1) * EVBH, [(2 * EVBH, hh)])
                        wev.wait(0)
                        g.dma_start(c4[:, bass.ds(h, 1), b * sq:(b + 1) * sq].squeeze(1),
                                    cb[u][:]).then_inc(ctx.dmg.h, 16)
            g.wait_ge(ctx.dmg.h, g0 + nbatch * nh * 16)

    nbh = nbatch * nh
    ctx.dma.n = d0 + nbh * LDJ * 16
    ctx.dmg.n = g0 + nbh * 16
    ctx.mm.n = m0 + nbh * MMBH
    ctx.ev.n = e0 + nbh * EVBH
    ctx.sc.n = s0 + nbh * SCBH



def emit_gate_topk(ctx, x_dram, gwT_dram, cT_dram, *, E=8, topk=4, Tn=T,
                   denom="selected", gate_b_dram=None):
    nc = ctx.nc
    assert E <= 16
    ntt = Tn // 128
    gw = ctx.sb("gt_w", [128, NCT * E], BF16)
    xs = ctx.sb("gt_x", [128, NCT * 128], BF16)
    lg = ctx.sb("gt_lg", [128, E], F32)
    wk_ = ctx.sb("gt_wk", [128, E], F32)
    sel = ctx.sb("gt_sel", [128, E], F32)
    mcol = ctx.sb("gt_m", [128, 4], F32)
    csb = ctx.sb("gt_c", [128, E], F32)
    ctile = ctx.sb("gt_ct", [E, 128], F32)
    bb = ctx.sb("gt_bb", [128, E], F32) if gate_b_dram is not None else None
    brow = ctx.sb("gt_br", [1, E], F32) if gate_b_dram is not None else None
    PS_L = ctx.ps("gt_pl")
    PS_T = ctx.ps("gt_pt")

    x3 = x_dram.rearrange("(c p) t -> p c t", p=128)
    g3 = gwT_dram.rearrange("(c p) e -> p c e", p=128)

    d0, m0, e0, s0, g0 = ctx.dma.n, ctx.mm.n, ctx.ev.n, ctx.sc.n, ctx.dmg.n
    EV = 2 + (4 * topk) + 4 + 1
    MM = 2
    SC = 1
    MOF = 1 if gate_b_dram is not None else 0
    WLD = 32 if gate_b_dram is not None else 16

    with nc.Block() as blk:
        @blk.gpsimd
        def _(g):
            g.dma_start(gw[:].rearrange("p (c e) -> p c e", c=NCT), g3
                        ).then_inc(ctx.dmg.h, 16)
            if gate_b_dram is not None:
                g.dma_start(brow[:], gate_b_dram).then_inc(ctx.dmg.h, 16)
            wev = Walker(ctx, g, ctx.ev.h, 0)
            with loop(ctx, g, ntt) as i:
                wev.start(e0 + MOF + EV, [(EV, i)])
                wev.wait(0)
                g.dma_start(cT_dram[:, bass.ds(i * 128, 128)], ctile[:]
                            ).then_inc(ctx.dmg.h, 16)
            g.wait_ge(ctx.dmg.h, g0 + WLD + ntt * 16)

        @blk.sync
        def _(sync):
            wmm = Walker(ctx, sync, ctx.mm.h, 0)
            with loop(ctx, sync, ntt) as i:
                wmm.start(m0 + MOF - MM + 1, [(MM, i)])
                wmm.wait(0)
                sync.dma_start(xs[:].rearrange("p (c t) -> p c t", c=NCT),
                               x3[:, :, bass.ds(i * 128, 128)]).then_inc(ctx.dma.h, 16)

        @blk.tensor
        def _(t):
            t.wait_ge(ctx.dmg.h, g0 + WLD)
            if gate_b_dram is not None:
                t.matmul(PS_T[0:128, 0:E], ctx.consts["ones_row"][:],
                         brow[0:1, :], start=True, stop=True).then_inc(ctx.mm.h, 1)
            wdma = Walker(ctx, t, ctx.dma.h, 0)
            wev = Walker(ctx, t, ctx.ev.h, 1)
            with loop(ctx, t, ntt) as i:
                wdma.start(d0 + 16, [(16, i)])
                wdma.wait(0)
                wev.start(e0 + MOF, [(EV, i)])
                wev.wait(0)
                for ct in range(NCT):
                    mmi = t.matmul(PS_L[0:128, 0:E], xs[:, bass.ts(ct, 128)],
                                   gw[:, ct * E:(ct + 1) * E],
                                   start=(ct == 0), stop=(ct == NCT - 1))
                mmi.then_inc(ctx.mm.h, 1)
                wev.wait(2 + 4 * topk + 4)
                t.transpose(PS_T[0:E, 0:128], csb[:, 0:E], ctx.consts["ident"][:]
                            ).then_inc(ctx.mm.h, 1)

        @blk.scalar
        def _(s):
            wev = Walker(ctx, s, ctx.ev.h, 0)
            with loop(ctx, s, ntt) as i:
                wev.start(e0 + MOF + 2, [(EV, i)])
                wev.wait(0)
                s.activation(lg[:, 0:E], lg[:, 0:E], AF.Exp).then_inc(ctx.sc.h, 1)

        @blk.vector
        def _(v):
            if gate_b_dram is not None:
                wmm0 = Walker(ctx, v, ctx.mm.h, 3)
                wmm0.start(m0 + 1, [])
                wmm0.wait(0)
                v.tensor_copy(bb[:], PS_T[0:128, 0:E]).then_inc(ctx.ev.h, 1)
            wmm = Walker(ctx, v, ctx.mm.h, 0)
            wsc = Walker(ctx, v, ctx.sc.h, 1)
            wdg = Walker(ctx, v, ctx.dmg.h, 2)
            wev = Walker(ctx, v, ctx.ev.h, 3)
            with loop(ctx, v, ntt) as i:
                wev.start(e0 + MOF, [(EV, i)])
                cnt = 0

                def step(f, wait=True):
                    nonlocal cnt
                    if wait and cnt:
                        wev.wait(cnt)
                    f().then_inc(ctx.ev.h, 1)
                    cnt += 1

                wmm.start(m0 + MOF + 1, [(MM, i)])
                wmm.wait(0)
                if gate_b_dram is not None:
                    v.tensor_add(lg[:, 0:E], PS_L[0:128, 0:E], bb[:]
                                 ).then_inc(ctx.ev.h, 1)
                else:
                    v.tensor_copy(lg[:, 0:E], PS_L[0:128, 0:E]).then_inc(ctx.ev.h, 1)
                cnt += 1
                step(lambda: v.tensor_copy(wk_[:], lg[:, 0:E]))
                v.memset(sel[:], 0.0)
                for r in range(topk):
                    step(lambda: v.reduce_max(mcol[:, r:r + 1], wk_[:, 0:E], axis=AX))
                    step(lambda: v.tensor_scalar(csb[:, 0:E], wk_[:, 0:E],
                                                 mcol[:, r:r + 1], 0.0,
                                                 OP.subtract, OP.is_ge))
                    step(lambda: v.tensor_add(sel[:], sel[:], csb[:, 0:E]))
                    step(lambda: v.scalar_tensor_tensor(wk_[:, 0:E], csb[:, 0:E],
                                                        -1e30, wk_[:, 0:E],
                                                        OP.mult, OP.add))
                wsc.start(s0 + 1, [(SC, i)])
                wsc.wait(0)
                if denom == "selected":
                    step(lambda: v.tensor_mul(csb[:, 0:E], sel[:], lg[:, 0:E]))
                    step(lambda: v.reduce_sum(mcol[:, 0:1], csb[:, 0:E], axis=AX))
                else:
                    step(lambda: v.reduce_sum(mcol[:, 0:1], lg[:, 0:E], axis=AX))
                    step(lambda: v.tensor_mul(csb[:, 0:E], sel[:], lg[:, 0:E]))
                step(lambda: v.reciprocal(mcol[:, 1:2], mcol[:, 0:1]))
                step(lambda: v.tensor_scalar(csb[:, 0:E], csb[:, 0:E], mcol[:, 1:2],
                                             None, OP.mult))
                wmm.wait(1)
                wdg.start(g0 + WLD, [(16, i)])
                wdg.wait(0)
                v.tensor_copy(ctile[:], PS_T[0:E, 0:128]).then_inc(ctx.ev.h, 1)

    ctx.dma.n = d0 + ntt * 16
    ctx.dmg.n = g0 + WLD + ntt * 16
    ctx.mm.n = m0 + MOF + ntt * MM
    ctx.ev.n = e0 + MOF + ntt * EV
    ctx.sc.n = s0 + ntt * SC


def emit_moe_experts(ctx, x_dram, w1_dram, w3_dram, w2_dram, y_dram, *, n_exp=1,
                     H=1024, Tn=T, cT_dram=None, cT_base=0, const_scale=None,
                     accumulate=True, b1_dram=None, b3_dram=None, b2_dram=None,
                     scale_row_dram=None, Fout=D):
    nc = ctx.nc
    nht, nf = H // 128, Fout // 128
    ncc = D // 128
    nch = Tn // TCH
    assert nch % 2 == 0
    have_ct = cT_dram is not None
    have_cs = (const_scale is not None and const_scale != 1.0)
    need_b2 = b2_dram is not None
    w1s = ctx.sb("mx_w1", [128, ncc * H], BF16)
    w3s = ctx.sb("mx_w3", [128, ncc * H], BF16)
    w2s = ctx.sb("mx_w2", [128, nht * Fout], BF16)
    xs2 = [ctx.sb("mx_x", [128, ncc * TCH], BF16) for _ in range(2)]
    gs = ctx.sb("mx_g", [128, nht * TCH], BF16)
    glu = ctx.sb("mx_glu", [128, nht * TCH], BF16)
    ys = ctx.sb("mx_y", [128, nf * TCH], F32)
    yin2 = [ctx.sb("mx_yin", [128, nf * TCH], F32) for _ in range(2)] \
        if accumulate else None
    crow2 = [ctx.sb("mx_cr", [1, TCH], F32) for _ in range(2)] if have_ct else None
    crb2 = [ctx.sb("mx_crb", [1, TCH], BF16) for _ in range(2)] \
        if (have_ct and need_b2) else None
    srow = ctx.sb("mx_sr", [1, TCH], BF16) if (not have_ct and need_b2) else None
    b1s = ctx.sb("mx_b1", [128, nht], F32) if b1_dram is not None else None
    b3s = ctx.sb("mx_b3", [128, nht], F32) if b3_dram is not None else None
    b2r = ctx.sb("mx_b2", [1, Fout], BF16) if need_b2 else None
    pbk = [ctx.ps("mx_p") for _ in range(4)]
    PS_B = [ctx.ps("mx_pb"), ctx.ps("mx_pb2")] if have_ct else None

    x3 = x_dram.rearrange("(c p) t -> p c t", p=128)
    y3 = y_dram.rearrange("(f p) t -> p f t", p=128)
    w13 = w1_dram.rearrange("(e c p) h -> p e c h", p=128, c=ncc)
    w33 = w3_dram.rearrange("(e c p) h -> p e c h", p=128, c=ncc)
    w23 = w2_dram.rearrange("(e h p) f -> p e h f", p=128, h=nht)
    b13 = b1_dram.rearrange("(e p) h -> p e h", p=128) if b1_dram is not None else None
    b33 = b3_dram.rearrange("(e p) h -> p e h", p=128) if b3_dram is not None else None

    d0, m0, e0, s0, g0 = ctx.dma.n, ctx.mm.n, ctx.ev.n, ctx.sc.n, ctx.dmg.n
    NWL = 3 + sum(x is not None for x in (b1s, b3s, b2r))
    SROWL = 16 if srow is not None else 0
    LD = 1 + (1 if have_ct else 0) + (1 if accumulate else 0)
    NG = 2 * nht + nf
    assert NG % 4 == 0 and nf >= 4
    BC = 1 if have_ct else 0
    CRB = 1 if crb2 is not None else 0
    MMC = NG + BC
    SCC = nht + CRB
    EVC = nht + nf
    DQC = NWL + nch

    def gsched():
        out = []
        for r in range(NG):
            kind = ("g" if r < 2 * nht and r % 2 == 0 else
                    "u" if r < 2 * nht else "y")
            idx = r // 2 if r < 2 * nht else r - 2 * nht
            pr = r - 4
            pq_rel = 0 if pr >= 0 else -1
            prr = pr if pr >= 0 else pr + NG
            pkind = ("g" if prr < 2 * nht and prr % 2 == 0 else
                     "u" if prr < 2 * nht else "y")
            pidx = prr // 2 if prr < 2 * nht else prr - 2 * nht
            out.append((r, kind, idx, (pkind, pidx, pq_rel)))
        return out

    GS = gsched()

    with nc.Block() as blk:
        @blk.gpsimd
        def _(g):
            if srow is not None:
                g.dma_start(srow[:], scale_row_dram).then_inc(ctx.dmg.h, 16)
            wmm = Walker(ctx, g, ctx.mm.h, 0)
            wsc = Walker(ctx, g, ctx.sc.h, 1)
            wev = Walker(ctx, g, ctx.ev.h, 2)
            with loop(ctx, g, n_exp) as e:
                wmm.start(m0, [(nch * MMC, e)])
                wmm.wait(0)
                wsc.start(s0, [(nch * SCC, e)])
                wsc.wait(0)
                wev.start(e0, [(nch * EVC, e)])
                wev.wait(0)
                g.dma_start(w1s[:].rearrange("p (c h) -> p c h", c=ncc),
                            w13[:, bass.ds(e, 1), :, :].squeeze(1)).then_inc(ctx.dmg.h, 16)
                g.dma_start(w3s[:].rearrange("p (c h) -> p c h", c=ncc),
                            w33[:, bass.ds(e, 1), :, :].squeeze(1)).then_inc(ctx.dmg.h, 16)
                g.dma_start(w2s[:].rearrange("p (h f) -> p h f", h=nht),
                            w23[:, bass.ds(e, 1), :, :].squeeze(1)).then_inc(ctx.dmg.h, 16)
                if b1s is not None:
                    g.dma_start(b1s[:], b13[:, bass.ds(e, 1), :].squeeze(1)
                                ).then_inc(ctx.dmg.h, 16)
                if b3s is not None:
                    g.dma_start(b3s[:], b33[:, bass.ds(e, 1), :].squeeze(1)
                                ).then_inc(ctx.dmg.h, 16)
                if b2r is not None:
                    g.dma_start(b2r[:], b2_dram[bass.ds(e, 1), :]).then_inc(ctx.dmg.h, 16)
                with loop(ctx, g, nch // 2, 1) as i2:
                    for u in range(2):
                        q = combine_idx(ctx, g, e, nch, i2 * 2 + u,
                                        maxv=n_exp * nch - 1)
                        wev.start(e0 + EVC, [(EVC, q)])
                        wev.wait(0)
                        g.dma_start(y3[:, :, bass.ds((i2 * 2 + u) * TCH, TCH)],
                                    ys[:].rearrange("p (f t) -> p f t", f=nf)
                                    ).then_inc(ctx.dmg.h, 16)
            g.wait_ge(ctx.dmg.h, g0 + SROWL + n_exp * DQC * 16)

        @blk.sync
        def _(sync):
            wmm = Walker(ctx, sync, ctx.mm.h, 0)
            wev = Walker(ctx, sync, ctx.ev.h, 1)
            wdg = Walker(ctx, sync, ctx.dmg.h, 2)
            with loop(ctx, sync, n_exp) as e:
                with loop(ctx, sync, nch // 2, 1) as i2:
                    for u in range(2):
                        ch = i2 * 2 + u
                        q = combine_idx(ctx, sync, e, nch, ch,
                                        maxv=n_exp * nch - 1)
                        wmm.start(m0 - MMC, [(MMC, q)])
                        wmm.wait(0)
                        wev.start(e0 - EVC, [(EVC, q)])
                        wev.wait(0)
                        if accumulate:
                            dqv = combine_idx(ctx, sync, e, DQC, ch, slot=1,
                                              maxv=n_exp * DQC)
                            wdg.start(g0 + SROWL + (1 - nch) * 16, [(16, dqv)])
                            wdg.wait(0)
                        sync.dma_start(xs2[u][:].rearrange("p (c t) -> p c t",
                                                           c=NCT),
                                       x3[:, :, bass.ds(ch * TCH, TCH)]
                                       ).then_inc(ctx.dma.h, 16)
                        if have_ct:
                            sync.dma_start(crow2[u][:],
                                           cT_dram[bass.ds(cT_base + e, 1),
                                                   bass.ds(ch * TCH, TCH)]
                                           ).then_inc(ctx.dma.h, 16)
                        if accumulate:
                            sync.dma_start(yin2[u][:].rearrange(
                                "p (f t) -> p f t", f=nf),
                                y3[:, :, bass.ds(ch * TCH, TCH)]
                                ).then_inc(ctx.dma.h, 16)

        @blk.tensor
        def _(t):
            wwt = Walker(ctx, t, ctx.dmg.h, 4)
            wdma = Walker(ctx, t, ctx.dma.h, 1)
            wev = Walker(ctx, t, ctx.ev.h, 2)
            wsc = Walker(ctx, t, ctx.sc.h, 3)
            with loop(ctx, t, n_exp) as e:
                with loop(ctx, t, nch // 2, 1) as i2:
                    for u in range(2):
                        q = combine_idx(ctx, t, e, nch, i2 * 2 + u,
                                        maxv=n_exp * nch - 1)
                        wwt.start(g0 + SROWL + 16, [(DQC * 16, e)])
                        wdma.start(d0 + 16, [(LD * 16, q)])
                        wdma.wait(0)
                        wev.start(e0 - 2 * EVC, [(EVC, q)])
                        wsc.start(s0 + CRB + 1, [(SCC, q)])
                        if have_ct:
                            wdma.wait(16)
                            wev.wait(nht)
                            t.matmul(PS_B[u][:], ctx.consts["ones_row"][:],
                                     crow2[u][:], start=True, stop=True
                                     ).then_inc(ctx.mm.h, 1)
                        first = {"g": True, "u": True, "y": True}
                        for r, kind, idx, (pkind, pidx, pq_rel) in GS:
                            if pkind == "g":
                                wsc.wait((1 + pq_rel) * SCC + pidx - SCC)
                            elif pkind == "u":
                                wev.wait((2 + pq_rel) * EVC + pidx + 1)
                            else:
                                wev.wait((2 + pq_rel) * EVC + nht + pidx + 1)
                            if first[kind]:
                                first[kind] = False
                                wwt.wait({"g": 0, "u": 16,
                                          "y": (NWL - 1) * 16}[kind])
                            pb = pbk[r % 4]
                            if kind == "g":
                                for ct in range(ncc):
                                    mmi = t.matmul(pb[:],
                                                   w1s[:, ct * H + idx * 128:ct * H + (idx + 1) * 128],
                                                   xs2[u][:, bass.ts(ct, TCH)],
                                                   start=(ct == 0), stop=(ct == ncc - 1))
                            elif kind == "u":
                                for ct in range(ncc):
                                    mmi = t.matmul(pb[:],
                                                   w3s[:, ct * H + idx * 128:ct * H + (idx + 1) * 128],
                                                   xs2[u][:, bass.ts(ct, TCH)],
                                                   start=(ct == 0), stop=(ct == ncc - 1))
                            else:
                                if idx == 0:
                                    wev.wait(2 * EVC + nht)
                                    if crb2 is not None:
                                        wsc.wait(0)
                                for ht in range(nht):
                                    mmi = t.matmul(pb[:],
                                                   w2s[:, ht * Fout + idx * 128:ht * Fout + (idx + 1) * 128],
                                                   glu[:, bass.ts(ht, TCH)],
                                                   start=(ht == 0),
                                                   stop=(ht == nht - 1) and not need_b2)
                                if need_b2:
                                    mmi = t.matmul(pb[:],
                                                   b2r[0:1, idx * 128:(idx + 1) * 128],
                                                   (crb2[u] if have_ct else srow)[:],
                                                   start=False, stop=True)
                            mmi.then_inc(ctx.mm.h, 1)

        @blk.scalar
        def _(s):
            wdg = Walker(ctx, s, ctx.dmg.h, 0)
            wdma = Walker(ctx, s, ctx.dma.h, 1)
            wmm = Walker(ctx, s, ctx.mm.h, 2)
            with loop(ctx, s, n_exp) as e:
                with loop(ctx, s, nch // 2, 1) as i2:
                    for u in range(2):
                        q = combine_idx(ctx, s, e, nch, i2 * 2 + u,
                                        maxv=n_exp * nch - 1)
                        wdg.start(g0 + SROWL + NWL * 16, [(DQC * 16, e)])
                        wdg.wait(0)
                        wmm.start(m0 + BC + 1, [(MMC, q)])
                        if crb2 is not None:
                            wdma.start(d0 + 32, [(LD * 16, q)])
                            wdma.wait(0)
                            s.activation(crb2[u][:], crow2[u][:], AF.Copy
                                         ).then_inc(ctx.sc.h, 1)
                        for ht in range(nht):
                            wmm.wait(2 * ht)
                            s.activation(gs[:, bass.ts(ht, TCH)],
                                         pbk[(2 * ht) % 4][:], AF.Silu,
                                         bias=(b1s[:, ht:ht + 1]
                                               if b1s is not None else 0.0)
                                         ).then_inc(ctx.sc.h, 1)

        @blk.vector
        def _(v):
            wdg = Walker(ctx, v, ctx.dmg.h, 0)
            wdma = Walker(ctx, v, ctx.dma.h, 1)
            wmm = Walker(ctx, v, ctx.mm.h, 2)
            wsc = Walker(ctx, v, ctx.sc.h, 3)
            with loop(ctx, v, n_exp) as e:
                with loop(ctx, v, nch // 2, 1) as i2:
                    for u in range(2):
                        q = combine_idx(ctx, v, e, nch, i2 * 2 + u,
                                        maxv=n_exp * nch - 1)
                        dq = combine_idx(ctx, v, e, DQC, i2 * 2 + u, slot=1,
                                         maxv=n_exp * DQC)
                        wdg.start(g0 + SROWL, [(16, dq)])
                        wdg.wait(NWL * 16)
                        if accumulate:
                            wdma.start(d0 + LD * 16, [(LD * 16, q)])
                            wdma.wait(0)
                        wsc.start(s0 + CRB + 1, [(SCC, q)])
                        wmm.start(m0 + BC + 2, [(MMC, q)])
                        for ht in range(nht):
                            wsc.wait(ht)
                            wmm.wait(2 * ht)
                            upb = pbk[(2 * ht + 1) % 4]
                            gl = glu[:, bass.ts(ht, TCH)]
                            if b3s is not None:
                                v.tensor_scalar(gl, upb[:], b3s[:, ht:ht + 1],
                                                None, OP.add)
                                op = v.tensor_mul(gl, gl, gs[:, bass.ts(ht, TCH)])
                            else:
                                op = v.tensor_mul(gl, upb[:], gs[:, bass.ts(ht, TCH)])
                            if have_ct:
                                op = v.tensor_mul(gl, gl, PS_B[u][:])
                            elif have_cs:
                                op = v.tensor_scalar(gl, gl, float(const_scale),
                                                     None, OP.mult)
                            op.then_inc(ctx.ev.h, 1)
                        for ft in range(nf):
                            wmm.wait(2 * nht + ft - 1)
                            ypb = pbk[(2 * nht + ft) % 4]
                            if accumulate:
                                v.tensor_add(ys[:, bass.ts(ft, TCH)], ypb[:],
                                             yin2[u][:, bass.ts(ft, TCH)]
                                             ).then_inc(ctx.ev.h, 1)
                            else:
                                v.tensor_copy(ys[:, bass.ts(ft, TCH)], ypb[:]
                                              ).then_inc(ctx.ev.h, 1)

    ctx.dma.n = d0 + n_exp * nch * LD * 16
    ctx.dmg.n = g0 + SROWL + n_exp * DQC * 16
    ctx.mm.n = m0 + n_exp * nch * MMC
    ctx.ev.n = e0 + n_exp * nch * EVC
    ctx.sc.n = s0 + n_exp * nch * SCC




def emit_butterfly(ctx, win_dram, rows_dram, wout_dram, *, R=D, name="bf",
                   strides=None, extra_row_dram=None):
    nc = ctx.nc
    strides = strides if strides is not None else [2 ** s for s in range(12)]
    nct = (R + 127) // 128
    rows = 128 if R >= 128 else R
    mid = ctx.dram(f"{name}_mid", [R, 4096], BF16)
    A = ctx.sb("bf_a", [128, 4096], BF16)
    Bb = ctx.sb("bf_b", [128, 4096], BF16)
    T1 = ctx.sb("bf_t1", [128, 2048], BF16)
    T2 = ctx.sb("bf_t2", [128, 2048], BF16)
    rrows = ctx.sb("bf_rr", [1, 4 * 2048], F32)
    wbc = ctx.sb("bf_w", [128, 24 * 2048], BF16)
    xrow = ctx.sb("bf_xr", [1, 4096], F32) if extra_row_dram is not None else None
    PS = [ctx.ps("bf_p") for _ in range(2)]

    for p in range(2):
        stv = strides[p * 6:(p + 1) * 6]
        src_d = win_dram if p == 0 else mid
        dst_d = mid if p == 0 else wout_dram
        last = p == 1
        d0, m0, e0, g0 = ctx.dma.n, ctx.mm.n, ctx.ev.n, ctx.dmg.n
        xofs = 32 if (last and extra_row_dram is not None) else 16
        with nc.Block() as blk:
            @blk.sync
            def _(sync):
                for s6 in range(6):
                    if s6 >= 1:
                        sync.wait_ge(ctx.mm.h, m0 + s6 * 16)
                    sync.dma_start(rrows[:].rearrange("r (j t) -> r j t", j=4),
                                   rows_dram[p * 24 + s6 * 4:p * 24 + (s6 + 1) * 4,
                                             :].rearrange("(r j) t -> r j t", r=1)
                                   ).then_inc(ctx.dma.h, 16)
                if last and extra_row_dram is not None:
                    sync.dma_start(xrow[:], extra_row_dram).then_inc(ctx.dma.h, 16)

            @blk.tensor
            def _(t):
                wev = Walker(ctx, t, ctx.ev.h, 0)
                wev.start(e0 - 1, [])
                for idx in range(96):
                    s6, sub = divmod(idx, 16)
                    if sub == 0:
                        t.wait_ge(ctx.dma.h, d0 + (s6 + 1) * 16)
                    if idx >= 2:
                        wev.wait(idx)
                    t.matmul(PS[idx % 2][:], ctx.consts["ones_row"][:],
                             rrows[0:1, sub * 512:(sub + 1) * 512],
                             start=True, stop=True).then_inc(ctx.mm.h, 1)

            @blk.vector
            def _(v):
                wmm = Walker(ctx, v, ctx.mm.h, 0)
                wmm.start(m0 + 1, [])
                for idx in range(96):
                    wmm.wait(idx)
                    v.tensor_copy(wbc[:, idx * 512:(idx + 1) * 512],
                                  PS[idx % 2][:]).then_inc(ctx.ev.h, 1)
                wdma = Walker(ctx, v, ctx.dmg.h, 1)
                with loop(ctx, v, nct) as i:
                    wdma.start(g0 + 16, [(32, i)])
                    wdma.wait(0)
                    cur, nxt = A, Bb
                    for si in range(6):
                        st = stv[si]
                        nb = 4096 // (2 * st)

                        def view(buf, half):
                            return buf[0:rows, :].rearrange(
                                "p (nb two st) -> p nb two st", nb=nb,
                                st=st)[:, :, half, :]

                        def w_v(r):
                            return wbc[0:rows, (si * 4 + r) * 2048:
                                       (si * 4 + r + 1) * 2048].rearrange(
                                "p (nb st) -> p nb st", nb=nb)

                        t1v = T1[0:rows, :].rearrange("p (nb st) -> p nb st", nb=nb)
                        t2v = T2[0:rows, :].rearrange("p (nb st) -> p nb st", nb=nb)
                        v.tensor_mul(t1v, view(cur, 0), w_v(0))
                        v.tensor_mul(t2v, view(cur, 1), w_v(1))
                        v.tensor_add(view(nxt, 0), t1v, t2v)
                        v.tensor_mul(t1v, view(cur, 0), w_v(2))
                        v.tensor_mul(t2v, view(cur, 1), w_v(3))
                        v.tensor_add(view(nxt, 1), t1v, t2v)
                        cur, nxt = nxt, cur
                    if last and extra_row_dram is not None:
                        v.tensor_add(A[0:rows, :], A[0:rows, :], xrow[0:1, :]
                                     if rows == 1 else xrow[0:1, :])
                    v.engine_nop().then_inc(ctx.ev.h, 1)

            @blk.gpsimd
            def _(g):
                wev = Walker(ctx, g, ctx.ev.h, 0)
                with loop(ctx, g, nct) as i:
                    wev.start(e0 + 96, [(1, i)])
                    wev.wait(0)
                    g.dma_start(A[0:rows, :], src_d[bass.ds(i * 128, rows), :]
                                ).then_inc(ctx.dmg.h, 16)
                    wev.wait(1)
                    g.dma_start(dst_d[bass.ds(i * 128, rows), :], A[0:rows, :]
                                ).then_inc(ctx.dmg.h, 16)
                g.wait_ge(ctx.dmg.h, g0 + nct * 32)

        ctx.dma.n = d0 + 96 + (xofs - 16)
        ctx.dmg.n = g0 + nct * 32
        ctx.mm.n = m0 + 96
        ctx.ev.n = e0 + 96 + nct


def emit_transpose_mat(ctx, win_dram, wout_dram, *, RN=(D, 4096)):
    nc = ctx.nc
    Rr, Cc = RN
    nr, ncl = Rr // 128, Cc // 128
    tin = ctx.sb("tp_in", [128, Rr], BF16)
    tout = ctx.sb("tp_out", [128, Rr], BF16)
    identb = ctx.sb("tp_id", [128, 128], BF16)
    PT = [ctx.ps("tp_p", (128, 1024), BF16) for _ in range(2)]
    w3 = win_dram.rearrange("(r p) c -> p r c", p=128)

    d0, m0, e0, g0 = ctx.dma.n, ctx.mm.n, ctx.ev.n, ctx.dmg.n
    with nc.Block() as blk:
        @blk.vector
        def _(v):
            v.tensor_copy(identb[:], ctx.consts["ident"][:]).then_inc(ctx.ev.h, 1)
            wmm = Walker(ctx, v, ctx.mm.h, 0)
            wdg = Walker(ctx, v, ctx.dmg.h, 1)
            with loop(ctx, v, ncl) as j:
                wmm.start(m0 + 1, [(nr, j)])
                wdg.start(g0, [(16, j)])
                wdg.wait(0)
                for r in range(nr):
                    wmm.wait(r)
                    v.tensor_copy(tout[:, bass.ts(r, 128)],
                                  PT[r % 2][:, 0:128]).then_inc(ctx.ev.h, 1)

        @blk.sync
        def _(sync):
            wev = Walker(ctx, sync, ctx.ev.h, 0)
            with loop(ctx, sync, ncl) as j:
                wev.start(e0 + 1 - nr, [(nr, j)])
                wev.wait(0)
                sync.dma_start(tin[:].rearrange("p (r c) -> p r c", r=nr),
                               w3[:, :, bass.ds(j * 128, 128)]
                               ).then_inc(ctx.dma.h, 16)

        @blk.tensor
        def _(t):
            t.wait_ge(ctx.ev.h, e0 + 1)
            wdma = Walker(ctx, t, ctx.dma.h, 0)
            wev = Walker(ctx, t, ctx.ev.h, 1)
            with loop(ctx, t, ncl) as j:
                wdma.start(d0 + 16, [(16, j)])
                wdma.wait(0)
                wev.start(e0, [(nr, j)])
                for r in range(nr):
                    if r >= 2:
                        wev.wait(r)
                    t.transpose(PT[r % 2][:, 0:128], tin[:, bass.ts(r, 128)],
                                identb[:]).then_inc(ctx.mm.h, 1)

        @blk.gpsimd
        def _(g):
            wev = Walker(ctx, g, ctx.ev.h, 0)
            with loop(ctx, g, ncl) as j:
                wev.start(e0 + 1 + nr, [(nr, j)])
                wev.wait(0)
                g.dma_start(wout_dram[bass.ds(j * 128, 128), :], tout[:]
                            ).then_inc(ctx.dmg.h, 16)
            g.wait_ge(ctx.dmg.h, g0 + ncl * 16)

    ctx.dma.n = d0 + ncl * 16
    ctx.dmg.n = g0 + ncl * 16
    ctx.mm.n = m0 + ncl * nr
    ctx.ev.n = e0 + 1 + ncl * nr


def emit_l0_attn(ctx, dd, xin_f32, xin_bf, xout_f32, xout_bf):
    with ctx.scope():
        g = ctx.sb("l0a_g", [128, NCT], F32)
        b = ctx.sb("l0a_b", [128, NCT], F32)
        nc = ctx.nc
        with nc.Block() as blk:
            @blk.gpsimd
            def _(gp):
                gp.dma_start(g[:], dd["l0a_ln_g"]).then_inc(ctx.dmg.h, 16)
                gp.dma_start(b[:], dd["l0a_ln_b"]).then_inc(ctx.dmg.h, 16)
                gp.wait_ge(ctx.dmg.h, ctx.dmg.inc(32))
        zn = ctx.dram("l0a_zn", [D, T], BF16)
        qT = ctx.dram("l0a_qT", [D, T], BF16)
        kT = ctx.dram("l0a_kT", [D, T], BF16)
        vn = ctx.dram("l0a_vn", [T, NH * 72], BF16)
        cT = ctx.dram("l0a_cT", [D, T], BF16)
        y = ctx.dram("l0a_y", [D, T], F32)
        with ctx.scope():
            emit_ln(ctx, xin_f32, out_bf_dram=zn, gamma=g[:], beta=b[:])
        with ctx.scope():
            emit_linear(ctx, zn, dd["l0a_wq"], qT)
        with ctx.scope():
            emit_linear(ctx, zn, dd["l0a_wk"], kT)
        with ctx.scope():
            emit_linear_tok(ctx, zn, dd["l0a_wv"], vn)
        with ctx.scope():
            emit_attention(ctx, qT, kT, vn, cT, causal=False, scale=1.0 / 8.0)
        with ctx.scope():
            emit_linear(ctx, cT, dd["l0a_wo"], y, bias_dram=dd["l0a_bo"],
                        extra_dram=xin_f32, extra_op=OP.add, extra_dtype=F32,
                        out_dtype=F32)
        with ctx.scope():
            emit_ln(ctx, y, out_bf_dram=xout_bf, out_f32_dram=xout_f32,
                    res_dram=xin_f32)


def emit_l0_ffn(ctx, dd, xin_f32, xin_bf, xout_f32, xout_bf):
    with ctx.scope():
        h0 = ctx.dram("l0f_h0", [D, T], BF16)
        g1 = ctx.dram("l0f_g1", [D, T], BF16)
        h1 = ctx.dram("l0f_h1", [D, T], BF16)
        cT = ctx.dram("l0f_cT", [8, T], F32)
        ymoe = ctx.dram("l0f_ymoe", [D, T], F32)
        ymoe_bf = ctx.dram("l0f_ymoeb", [D, T], BF16)
        h3 = ctx.dram("l0f_h3", [D, T], BF16)
        g2 = ctx.dram("l0f_g2", [D, T], BF16)
        h4 = ctx.dram("l0f_h4", [D, T], F32)
        with ctx.scope():
            emit_linear(ctx, xin_bf, dd["l0f_lin0_w"], h0, bias_dram=dd["l0f_lin0_b"])
        with ctx.scope():
            emit_linear(ctx, h0, dd["l0f_sw1_w1"], g1, bias_dram=dd["l0f_sw1_b1"],
                        act=AF.Silu)
        with ctx.scope():
            emit_linear(ctx, h0, dd["l0f_sw1_w2"], h1, bias_dram=dd["l0f_sw1_b2"],
                        extra_dram=g1, extra_op=OP.mult)
        with ctx.scope():
            emit_gate_topk(ctx, h1, dd["l0f_gwT"], cT, E=8, topk=4, denom="selected")
        with ctx.scope():
            emit_moe_experts(ctx, h1, dd["l0f_shw1"], dd["l0f_shw3"], dd["l0f_shw2"],
                             ymoe, n_exp=1, H=2048, accumulate=False)
        with ctx.scope():
            emit_moe_experts(ctx, h1, dd["l0f_ew1"], dd["l0f_ew3"], dd["l0f_ew2"],
                             ymoe, n_exp=8, H=1024, cT_dram=cT, accumulate=True)
        emit_cast_copy(ctx, ymoe_bf, ymoe)
        with ctx.scope():
            emit_linear(ctx, ymoe_bf, dd["l0f_lin1_w"], h3, bias_dram=dd["l0f_lin1_b"])
        with ctx.scope():
            emit_linear(ctx, h3, dd["l0f_sw2_w1"], g2, bias_dram=dd["l0f_sw2_b1"],
                        act=AF.Silu)
        with ctx.scope():
            emit_linear(ctx, h3, dd["l0f_sw2_w2"], h4, bias_dram=dd["l0f_sw2_b2"],
                        extra_dram=g2, extra_op=OP.mult, out_dtype=F32)
        with ctx.scope():
            emit_ln(ctx, h4, out_bf_dram=xout_bf, out_f32_dram=xout_f32,
                    res_dram=xin_f32)


def emit_l1_attn(ctx, dd, xin_f32, xin_bf, xout_f32, xout_bf):
    with ctx.scope():
        cqT = ctx.dram("l1a_cqT", [256, T], BF16)
        ckvT = ctx.dram("l1a_ckvT", [256, T], BF16)
        qT = ctx.dram("l1a_qT", [D, T], BF16)
        kT = ctx.dram("l1a_kT", [D, T], BF16)
        vn = ctx.dram("l1a_vn", [T, NH * 72], BF16)
        cT = ctx.dram("l1a_cT", [D, T], BF16)
        y = ctx.dram("l1a_y", [D, T], F32)
        with ctx.scope():
            emit_linear(ctx, xin_bf, dd["l1a_W_dq"], cqT, F=256)
        with ctx.scope():
            emit_linear(ctx, xin_bf, dd["l1a_W_dkv"], ckvT, F=256)
        with ctx.scope():
            emit_linear(ctx, cqT, dd["l1a_W_uq"], qT, C=256)
        with ctx.scope():
            emit_linear(ctx, ckvT, dd["l1a_W_uk"], kT, C=256)
        with ctx.scope():
            emit_linear_tok(ctx, ckvT, dd["l1a_W_uv"], vn, C=256)
        with ctx.scope():
            emit_attention(ctx, qT, kT, vn, cT, causal=True, scale=1.0 / 8.0)
        with ctx.scope():
            emit_linear(ctx, cT, dd["l1a_W_o"], y, extra_dram=xin_f32,
                        extra_op=OP.add, extra_dtype=F32, out_dtype=F32)
        with ctx.scope():
            emit_ln(ctx, y, out_bf_dram=xout_bf, out_f32_dram=xout_f32)


def emit_l1_ffn(ctx, dd, xin_f32, xin_bf, xout_f32, xout_bf):
    nc = ctx.nc
    with ctx.scope():
        qT = ctx.dram("l1f_qT", [D, T], BF16)
        kT = ctx.dram("l1f_kT", [D, T], BF16)
        vn = ctx.dram("l1f_vn", [T, NH * 72], BF16)
        cT = ctx.dram("l1f_cT", [D, T], BF16)
        y = ctx.dram("l1f_y", [D, T], F32)
        wc = {p: [ctx.dram(f"l1f_wc_{p}{b}", [D, D], F32) for b in range(2)]
              for p in "qkv"}
        bcol = {p: [ctx.dram(f"l1f_bc_{p}{b}", [128, NCT], F32) for b in range(2)]
                for p in "qkv"}
        bvrow = [ctx.dram(f"l1f_bvr{b}", [1, D], F32) for b in range(2)]
        wcol = ctx.sb("l1f_wcol", [128, 8], F32)

        pooled = ctx.sb("l1f_pool", [128, NCT * 2], F32)
        with ctx.scope():
            xs = ctx.sb("l1f_xs", [128, NCT * TCH], BF16)
            tmpc = ctx.sb("l1f_tc", [128, 1], F32)
            x3 = xin_bf.rearrange("(c p) t -> p c t", p=128)
            d0, e0 = ctx.dma.n, ctx.ev.n
            EVC = 2 * NCT + 1
            with nc.Block() as blk:
                @blk.sync
                def _(sync):
                    for i in range(NCHUNK):
                        if i >= 1:
                            sync.wait_ge(ctx.ev.h, e0 + i * EVC)
                        sync.dma_start(xs[:].rearrange("p (c t) -> p c t", c=NCT),
                                       x3[:, :, i * TCH:(i + 1) * TCH]
                                       ).then_inc(ctx.dma.h, 16)

                @blk.vector
                def _(v):
                    v.memset(pooled[:], 0.0)
                    cnt = e0
                    for i in range(NCHUNK):
                        b = i // (NCHUNK // 2)
                        v.wait_ge(ctx.dma.h, d0 + (i + 1) * 16)
                        for ct in range(NCT):
                            v.reduce_sum(tmpc[:], xs[:, bass.ts(ct, TCH)], axis=AX
                                         ).then_inc(ctx.ev.h, 1)
                            cnt += 1
                            v.wait_ge(ctx.ev.h, cnt)
                            v.tensor_add(pooled[:, ct * 2 + b:ct * 2 + b + 1],
                                         pooled[:, ct * 2 + b:ct * 2 + b + 1],
                                         tmpc[:]).then_inc(ctx.ev.h, 1)
                            cnt += 1
                            v.wait_ge(ctx.ev.h, cnt)
                        v.engine_nop().then_inc(ctx.ev.h, 1)
                        cnt += 1
            ctx.dma.n = d0 + NCHUNK * 16
            ctx.ev.n = e0 + NCHUNK * EVC

        with ctx.scope():
            wg1 = ctx.sb("l1f_wg1", [128, NCT * 128], BF16)
            bg1r = ctx.sb("l1f_bg1", [2, 128], F32)
            wg2 = ctx.sb("l1f_wg2", [128, 4], BF16)
            bg2r = ctx.sb("l1f_bg2", [2, 4], F32)
            poolb = ctx.sb("l1f_poolb", [128, NCT * 2], BF16)
            g1 = ctx.sb("l1f_g1", [2, 128], F32)
            g1T = ctx.sb("l1f_g1T", [128, 2], BF16)
            lg2 = ctx.sb("l1f_lg2", [2, 4], F32)
            wk2 = ctx.sb("l1f_wk2", [2, 4], F32)
            sel2 = ctx.sb("l1f_sel2", [2, 4], F32)
            msk2 = ctx.sb("l1f_msk2", [2, 4], F32)
            mc2 = ctx.sb("l1f_mc2", [2, 2], F32)
            wb = ctx.sb("l1f_wb", [2, 4], F32)
            wbT = ctx.sb("l1f_wbT", [4, 2], F32)
            wflat = ctx.sb("l1f_wflat", [1, 8], F32)
            bqs = ctx.sb("l1f_bqs", [4, 3 * D], F32)
            bcsb = ctx.sb("l1f_bcsb", [2, 3 * D], F32)
            bcol_sb = ctx.sb("l1f_bcolsb", [128, 6 * NCT], F32)
            P1 = ctx.ps("l1f_p1")
            P2 = ctx.ps("l1f_p2")
            P3 = ctx.ps("l1f_p3")
            q = Seq(ctx)
            q.dma(lambda g: g.dma_start(
                wg1[:].rearrange("p (c f) -> p c f", c=NCT),
                dd["l1f_Wg1"].rearrange("(c p) f -> p c f", p=128)))
            q.dma(lambda g: g.dma_start(bg1r[:], dd["l1f_bg1"]))
            q.dma(lambda g: g.dma_start(wg2[:], dd["l1f_Wg2p"]))
            q.dma(lambda g: g.dma_start(bg2r[:], dd["l1f_bg2"]))
            q.dma(lambda g: g.dma_start(bqs[:, 0:D], dd["l1f_bq"]))
            q.dma(lambda g: g.dma_start(bqs[:, D:2 * D], dd["l1f_bk"]))
            q.dma(lambda g: g.dma_start(bqs[:, 2 * D:3 * D], dd["l1f_bv"]))
            q.op("vector", lambda v: v.tensor_scalar(poolb[:], pooled[:], 1.0 / S,
                                                     None, OP.mult))

            def mm_g1(t):
                for ct in range(NCT):
                    r = t.matmul(P1[0:2, 0:128], poolb[:, ct * 2:(ct + 1) * 2],
                                 wg1[:, bass.ts(ct, 128)], start=(ct == 0),
                                 stop=(ct == NCT - 1))
                return r
            q.op("tensor", mm_g1)
            q.op("vector", lambda v: v.tensor_tensor(g1[:], P1[0:2, 0:128], bg1r[:],
                                                     OP.add))
            q.op("vector", lambda v: v.tensor_scalar_max(g1[:], g1[:], 0.0))
            q.op("tensor", lambda t: t.transpose(P2[0:128, 0:2], g1[:],
                                                 ctx.consts["ident"][0:2, 0:2]))
            q.op("vector", lambda v: v.tensor_copy(g1T[:], P2[0:128, 0:2]))
            q.op("tensor", lambda t: t.matmul(P1[0:2, 0:4], g1T[:], wg2[:],
                                              start=True, stop=True))
            q.op("vector", lambda v: v.tensor_tensor(lg2[:], P1[0:2, 0:4], bg2r[:],
                                                     OP.add))
            q.op("vector", lambda v: v.tensor_copy(wk2[:], lg2[:]))
            q.op("vector", lambda v: v.memset(sel2[:], 0.0))
            for r in range(2):
                q.op("vector", lambda v: v.reduce_max(mc2[:, 0:1], wk2[:], axis=AX))
                q.op("vector", lambda v: v.tensor_scalar(msk2[:], wk2[:], mc2[:, 0:1],
                                                         0.0, OP.subtract, OP.is_ge))
                q.op("vector", lambda v: v.tensor_add(sel2[:], sel2[:], msk2[:]))
                q.op("vector", lambda v: v.scalar_tensor_tensor(
                    wk2[:], msk2[:], -1e30, wk2[:], OP.mult, OP.add))
            q.op("scalar", lambda s: s.activation(lg2[:], lg2[:], AF.Exp))
            q.op("vector", lambda v: v.tensor_mul(wb[:], sel2[:], lg2[:]))
            q.op("vector", lambda v: v.reduce_sum(mc2[:, 0:1], wb[:], axis=AX))
            q.op("vector", lambda v: v.reciprocal(mc2[:, 1:2], mc2[:, 0:1]))
            q.op("vector", lambda v: v.tensor_scalar(wb[:], wb[:], mc2[:, 1:2], None,
                                                     OP.mult))
            q.dma(lambda g: g.dma_start(wflat[0:1, 0:4], wb[0:1, :]))
            q.dma(lambda g: g.dma_start(wflat[0:1, 4:8], wb[1:2, :]))
            q.op("tensor", lambda t: t.matmul(P2[0:128, 0:8],
                                              ctx.consts["ones_row"][:], wflat[:],
                                              start=True, stop=True))
            q.op("vector", lambda v: v.tensor_copy(wcol[:], P2[0:128, 0:8]))
            q.op("tensor", lambda t: t.transpose(P3[0:4, 0:2], wb[:],
                                                 ctx.consts["ident"][0:2, 0:2]))
            q.op("vector", lambda v: v.tensor_copy(wbT[:], P3[0:4, 0:2]))
            for j in range(6):
                def mk(j):
                    return lambda t: t.matmul(P1[0:2, 0:512], wbT[:],
                                              bqs[:, bass.ts(j, 512)],
                                              start=True, stop=True)
                q.op("tensor", mk(j))
                def mkc(j):
                    return lambda v: v.tensor_copy(bcsb[:, bass.ts(j, 512)],
                                                   P1[0:2, 0:512])
                q.op("vector", mkc(j))
            for j in range(24):
                def mkt(j):
                    return lambda t: t.transpose(P3[0:128, 0:2],
                                                 bcsb[:, bass.ts(j, 128)],
                                                 ctx.consts["ident"][0:2, 0:2])
                q.op("tensor", mkt(j))
                pi, ft = divmod(j, NCT)
                def mkc2(j, pi, ft):
                    def f(v):
                        v.tensor_copy(bcol_sb[:, (pi * 2 + 0) * NCT + ft:
                                              (pi * 2 + 0) * NCT + ft + 1],
                                      P3[0:128, 0:1])
                        return v.tensor_copy(bcol_sb[:, (pi * 2 + 1) * NCT + ft:
                                                     (pi * 2 + 1) * NCT + ft + 1],
                                             P3[0:128, 1:2])
                    return f
                q.op("vector", mkc2(j, pi, ft))
            for pi, p in enumerate("qkv"):
                for b in range(2):
                    def mks(pi, p, b):
                        return lambda g: g.dma_start(
                            bcol[p][b],
                            bcol_sb[:, (pi * 2 + b) * NCT:(pi * 2 + b + 1) * NCT])
                    q.dma(mks(pi, p, b))
            for b in range(2):
                def mkv(b):
                    return lambda g: g.dma_start(bvrow[b], bcsb[b:b + 1, 2 * D:3 * D])
                q.dma(mkv(b))
            q.emit()

        with ctx.scope():
            wtile = ctx.sb("l1f_wt", [128, NCT * D], F32)
            acc = ctx.sb("l1f_acc", [128, NCT * D], F32)
            for pi, p in enumerate("qkv"):
                wsrc = dd["l1f_W" + p]
                for b in range(2):
                    q = Seq(ctx)
                    for e in range(4):
                        def mkl(e):
                            return lambda g: g.dma_start(
                                wtile[:].rearrange("p (c f) -> p c f", c=NCT),
                                wsrc[e * D:(e + 1) * D, :].rearrange(
                                    "(c p) f -> p c f", p=128))
                        q.dma(mkl(e))
                        col = wcol[:, b * 4 + e:b * 4 + e + 1]
                        if e == 0:
                            def mk0(col):
                                return lambda v: v.tensor_scalar(acc[:], wtile[:],
                                                                 col, None, OP.mult)
                            q.op("vector", mk0(col))
                        else:
                            def mke(col):
                                return lambda v: v.scalar_tensor_tensor(
                                    acc[:], wtile[:], col, acc[:], OP.mult, OP.add)
                            q.op("vector", mke(col))
                    q.dma(lambda g: g.dma_start(
                        wc[p][b].rearrange("(c p) f -> p c f", p=128),
                        acc[:].rearrange("p (c f) -> p c f", c=NCT)))
                    q.emit()

        for b in range(2):
            sl = slice(b * S, (b + 1) * S)
            with ctx.scope():
                emit_linear(ctx, xin_bf[:, sl], wc["q"][b], qT[:, sl], Tn=S,
                            bias_dram=bcol["q"][b])
            with ctx.scope():
                emit_linear(ctx, xin_bf[:, sl], wc["k"][b], kT[:, sl], Tn=S,
                            bias_dram=bcol["k"][b])
            with ctx.scope():
                emit_linear_tok(ctx, xin_bf[:, sl], wc["v"][b], vn[sl, :], Tn=S,
                                bias_row_dram=bvrow[b])
        with ctx.scope():
            emit_attention(ctx, qT, kT, vn, cT, causal=False, scale=1.0 / 8.0)
        with ctx.scope():
            emit_linear(ctx, cT, dd["l1f_Wo"], y, bias_dram=dd["l1f_bo"],
                        extra_dram=xin_f32, extra_op=OP.add, extra_dtype=F32,
                        out_dtype=F32)
        with ctx.scope():
            emit_ln(ctx, y, out_bf_dram=xout_bf, out_f32_dram=xout_f32)


def emit_l2_attn(ctx, dd, xin_f32, xin_bf, xout_f32, xout_bf):
    with ctx.scope():
        qT = ctx.dram("l2a_qT", [D, T], BF16)
        kT = ctx.dram("l2a_kT", [D, T], BF16)
        vn = ctx.dram("l2a_vn", [T, NH * 72], BF16)
        cT = ctx.dram("l2a_cT", [D, T], BF16)
        y = ctx.dram("l2a_y", [D, T], F32)
        with ctx.scope():
            emit_linear(ctx, xin_bf, dd["l2a_WqT"], qT)
        with ctx.scope():
            emit_linear(ctx, xin_bf, dd["l2a_WkT"], kT)
        with ctx.scope():
            emit_linear_tok(ctx, xin_bf, dd["l2a_WvT"], vn)
        with ctx.scope():
            emit_attention(ctx, qT, kT, vn, cT, causal=True,
                           alibi_dram=dd["c_alibi"], scale=1.0 / 8.0)
        with ctx.scope():
            emit_linear(ctx, cT, dd["l2a_WoT"], y, extra_dram=xin_f32,
                        extra_op=OP.add, extra_dtype=F32, out_dtype=F32)
        with ctx.scope():
            emit_ln(ctx, y, out_bf_dram=xout_bf, out_f32_dram=xout_f32)


def emit_l2_ffn(ctx, dd, xin_f32, xin_bf, xout_f32, xout_bf):
    with ctx.scope():
        wexp = ctx.dram("l2f_wexp_f", [D, 4096], BF16)
        brow = ctx.dram("l2f_brow", [1, 4096], BF16)
        bcol = ctx.dram("l2f_bcol", [128, 32], F32)
        wconTf = ctx.dram("l2f_wconT_f", [D, 4096], BF16)
        wcon = ctx.dram("l2f_wcon_f", [4096, D], BF16)
        h1 = ctx.dram("l2f_h1", [4096, T], BF16)
        y = ctx.dram("l2f_y", [D, T], F32)
        with ctx.scope():
            emit_butterfly(ctx, dd["l2f_wexp"], dd["l2f_uprows"], wexp,
                           R=D, name="l2f_up")
        with ctx.scope():
            emit_butterfly(ctx, dd["l2f_bexp"], dd["l2f_uprows"], brow,
                           R=1, name="l2f_upb", extra_row_dram=dd["l2f_upbias"])
        with ctx.scope():
            emit_butterfly(ctx, dd["l2f_wconT"], dd["l2f_dnrows"], wconTf,
                           R=D, name="l2f_dn",
                           strides=[2 ** s for s in range(11, -1, -1)])
        with ctx.scope():
            emit_transpose_mat(ctx, wconTf, wcon, RN=(D, 4096))
        with ctx.scope():
            brsb = ctx.sb("l2f_brs", [1, 4096], F32)
            nc = ctx.nc
            with nc.Block() as blk:
                @blk.gpsimd
                def _(g):
                    g.dma_start(brsb[:], brow[0:1, :]).then_inc(ctx.dmg.h, 16)
                    g.wait_ge(ctx.dmg.h, ctx.dmg.inc(16))
                    g.dma_start(bcol.rearrange("p j -> j p").unsqueeze(0),
                        brsb[0:1, :].rearrange("r (j p) -> r j p", p=128),
                        allow_slow_non_contiguous=True).then_inc(ctx.dmg.h, 16)
                    g.wait_ge(ctx.dmg.h, ctx.dmg.inc(16))
        with ctx.scope():
            emit_linear(ctx, xin_bf, wexp, h1, F=4096, bias_dram=bcol, act=AF.Gelu)
        with ctx.scope():
            emit_linear(ctx, h1, wcon, y, C=4096, F=D, bias_dram=dd["l2f_bcon"],
                        extra_dram=xin_f32, extra_op=OP.add, extra_dtype=F32,
                        out_dtype=F32, bias2_row_dram=dd["l2f_dbias"],
                        ones_row_t_dram=dd["c_ones_row_t"])
        with ctx.scope():
            emit_ln(ctx, y, out_bf_dram=xout_bf, out_f32_dram=xout_f32)


def emit_l3_attention_core(ctx, qT_sc, kT_sc, vn_sc, ctxT_sc):
    nc = ctx.nc
    ntt = T // 128
    qb = ctx.sb("l3_q", [64, 16 * 128], BF16)
    kb = ctx.sb("l3_k", [64, 16 * 128], BF16)
    vb = ctx.sb("l3_v", [128, NH * 72], BF16)
    pP = [ctx.sb("l3_p", [128, 512], BF16) for _ in range(2)]
    rr = ctx.sb("l3_rr", [1, 512], F32)
    rb = ctx.sb("l3_rb", [64, 512], F32)
    cbT = ctx.sb("l3_cb", [128, NCT * 128], BF16)
    PS_q = [ctx.ps("l3_ps") for _ in range(2)]
    PS_C2 = [ctx.ps("l3_pc") for _ in range(2)]
    PS_D = ctx.ps("l3_pd")

    q3 = qT_sc.rearrange("(chi p) t -> p chi t", p=64)
    k3 = kT_sc.rearrange("(chi p) t -> p chi t", p=64)
    c3 = ctxT_sc.rearrange("(c p) t -> p c t", p=128)
    vn3 = vn_sc.rearrange("(u p) f -> p u f", p=128)

    d0, m0, e0, s0, g0 = ctx.dma.n, ctx.mm.n, ctx.ev.n, ctx.sc.n, ctx.dmg.n
    MMQ = 33
    MMT = 4 * MMQ
    SCT = 4 * 16
    EVT = 4 * 2

    def _C(q):
        return 32 * q + (q - 1 if q >= 1 else 0)

    def m_off(qq, kind, pos=0):
        hb = 1 if qq >= 1 else 0
        if kind == "scores":
            off = pos + 1 if pos <= 1 else 2 * pos + hb
        elif kind == "pv":
            off = (32 + hb) if pos == 15 else (2 * pos + 3 + hb)
        else:
            return _C(qq + 1) + 3 if qq <= 2 else MMT
        return _C(qq) + off

    def s_off(qq, pos):
        return qq * 16 + pos + 1

    def e_off(qq, kind):
        return qq * 2 + (1 if kind == "recip" else 2)

    with nc.Block() as blk:
        @blk.sync
        def _(sync):
            wmm = Walker(ctx, sync, ctx.mm.h, 0)
            with loop(ctx, sync, ntt) as i:
                wmm.start(m0, [(MMT, i)])
                wmm.wait(0)
                sync.dma_start(qb[:].rearrange("p (c t) -> p c t", c=16),
                               q3[:, :, bass.ds(i * 128, 128)]).then_inc(ctx.dma.h, 16)
                sync.dma_start(kb[:].rearrange("p (c t) -> p c t", c=16),
                               k3[:, :, bass.ds(i * 128, 128)]).then_inc(ctx.dma.h, 16)
                sync.dma_start(vb[:], vn3[:, bass.ds(i, 1), :].squeeze(1)
                               ).then_inc(ctx.dma.h, 16)

        @blk.tensor
        def _(t):
            wdma = Walker(ctx, t, ctx.dma.h, 0)
            wev = Walker(ctx, t, ctx.ev.h, 1)
            wsc = Walker(ctx, t, ctx.sc.h, 2)
            with loop(ctx, t, ntt) as i:
                wdma.start(d0 + 48, [(48, i)])
                wdma.wait(0)
                wev.start(e0 - EVT, [(EVT, i)])
                wsc.start(s0, [(SCT, i)])
                pending = None
                for qq in range(4):
                    PS_C = PS_C2[qq % 2]
                    if qq >= 2:
                        wev.wait(EVT + e_off(qq - 2, "evac"))
                    else:
                        wev.wait(e_off(2 + qq, "evac"))
                    for kv in range(16):
                        if kv >= 2:
                            wsc.wait(s_off(qq, kv - 2))
                        elif qq >= 1:
                            wsc.wait(s_off(qq - 1, 14 + kv))
                        t.matmul(PS_q[kv % 2][:], kb[:, bass.ts(kv, 128)],
                                 qb[:, bass.ts(qq, 512)], start=True, stop=True
                                 ).then_inc(ctx.mm.h, 1)
                        if kv == 1 and pending is not None:
                            wev.wait(EVT + e_off(pending, "recip"))
                            t.matmul(PS_D[0:64, :],
                                     ctx.consts["ones_row"][:, 0:64],
                                     rr[:], start=True, stop=True
                                     ).then_inc(ctx.mm.h, 1)
                            pending = None
                        if kv >= 1:
                            pv = kv - 1
                            wsc.wait(s_off(qq, pv))
                            t.matmul(PS_C[0:65, :], vb[:, pv * 72:pv * 72 + 65],
                                     pP[pv % 2][:], start=(pv == 0), stop=False
                                     ).then_inc(ctx.mm.h, 1)
                    wsc.wait(s_off(qq, 15))
                    t.matmul(PS_C[0:65, :], vb[:, 15 * 72:15 * 72 + 65],
                             pP[1][:], start=False, stop=True).then_inc(ctx.mm.h, 1)
                    pending = qq
                wev.wait(EVT + e_off(3, "recip"))
                t.matmul(PS_D[0:64, :], ctx.consts["ones_row"][:, 0:64],
                         rr[:], start=True, stop=True).then_inc(ctx.mm.h, 1)

        @blk.scalar
        def _(s):
            wmm = Walker(ctx, s, ctx.mm.h, 0)
            with loop(ctx, s, ntt) as i:
                wmm.start(m0, [(MMT, i)])
                for qq in range(4):
                    for kv in range(16):
                        wmm.wait(m_off(qq, "scores", kv))
                        s.activation(pP[kv % 2][:], PS_q[kv % 2][:], AF.Exp,
                                     scale=1.0 / 8.0).then_inc(ctx.sc.h, 1)

        @blk.vector
        def _(v):
            wmm = Walker(ctx, v, ctx.mm.h, 0)
            wdg = Walker(ctx, v, ctx.dmg.h, 1)
            with loop(ctx, v, ntt) as i:
                wmm.start(m0, [(MMT, i)])
                wdg.start(g0, [(16, i)])
                wdg.wait(0)
                for qq in range(4):
                    PS_C = PS_C2[qq % 2]
                    wmm.wait(m_off(qq, "pv", 15))
                    v.reciprocal(rr[:], PS_C[64:65, :]).then_inc(ctx.ev.h, 1)
                    wmm.wait(m_off(qq, "bcast"))
                    v.tensor_copy(rb[:], PS_D[0:64, :])
                    for cl in range(4):
                        op = v.tensor_mul(
                            cbT[(cl % 2) * 64:(cl % 2) * 64 + 64,
                                (2 * qq + cl // 2) * 128:(2 * qq + cl // 2 + 1) * 128],
                            PS_C[0:64, bass.ts(cl, 128)],
                            rb[:, bass.ts(cl, 128)])
                        if cl == 3:
                            op.then_inc(ctx.ev.h, 1)

        @blk.gpsimd
        def _(g):
            wev = Walker(ctx, g, ctx.ev.h, 0)
            with loop(ctx, g, ntt) as i:
                wev.start(e0 + EVT, [(EVT, i)])
                wev.wait(0)
                g.dma_start(c3[:, :, bass.ds(i * 128, 128)],
                            cbT[:].rearrange("p (c t) -> p c t", c=NCT)
                            ).then_inc(ctx.dmg.h, 16)
            g.wait_ge(ctx.dmg.h, g0 + ntt * 16)

    ctx.dma.n = d0 + ntt * 48
    ctx.dmg.n = g0 + ntt * 16
    ctx.mm.n = m0 + ntt * MMT
    ctx.ev.n = e0 + ntt * EVT
    ctx.sc.n = s0 + ntt * SCT


def emit_l3_attn(ctx, dd, xin_f32, xin_bf, xout_f32, xout_bf):
    with ctx.scope():
        g = ctx.sb("l3a_gs", [128, NCT], F32)
        b = ctx.sb("l3a_bs", [128, NCT], F32)
        nc = ctx.nc
        with nc.Block() as blk:
            @blk.gpsimd
            def _(gp):
                gp.dma_start(g[:], dd["l3a_g"]).then_inc(ctx.dmg.h, 16)
                gp.dma_start(b[:], dd["l3a_b"]).then_inc(ctx.dmg.h, 16)
                gp.wait_ge(ctx.dmg.h, ctx.dmg.inc(32))
        qT = ctx.dram("l3a_qT", [D, T], BF16)
        kT = ctx.dram("l3a_kT", [D, T], BF16)
        vn = ctx.dram("l3a_vn", [T, NH * 72], BF16)
        cT = ctx.dram("l3a_cT", [D, T], BF16)
        y1 = ctx.dram("l3a_y1", [D, T], F32)
        y2 = ctx.dram("l3a_y2", [D, T], F32)
        with ctx.scope():
            emit_linear(ctx, xin_bf, dd["l3a_wq"], qT, bias_dram=dd["l3a_bq"])
        with ctx.scope():
            emit_linear(ctx, xin_bf, dd["l3a_wk"], kT, bias_dram=dd["l3a_bk"])
        with ctx.scope():
            emit_linear_tok(ctx, xin_bf, dd["l3a_wv"], vn,
                            bias_row_dram=dd["l3a_bv"])
        with ctx.scope():
            emit_l3_attention_core(ctx, qT, kT, vn, cT)
        with ctx.scope():
            emit_linear(ctx, cT, dd["l3a_wo"], y1, bias_dram=dd["l3a_bo"],
                        extra_dram=xin_f32, extra_op=OP.add, extra_dtype=F32,
                        out_dtype=F32)
        with ctx.scope():
            emit_ln(ctx, y1, out_f32_dram=y2, gamma=g[:], beta=b[:])
        with ctx.scope():
            emit_ln(ctx, y2, out_bf_dram=xout_bf, out_f32_dram=xout_f32,
                    res_dram=xin_f32)


def emit_l3_ffn(ctx, dd, xin_f32, xin_bf, xout_f32, xout_bf):
    with ctx.scope():
        cT = ctx.dram("l3f_cT", [8, T], F32)
        ymoe = ctx.dram("l3f_ymoe", [D, T], F32)
        with ctx.scope():
            emit_gate_topk(ctx, xin_bf, dd["l3f_gw"], cT, E=8, topk=2,
                           denom="all", gate_b_dram=dd["l3f_gb"])
        with ctx.scope():
            emit_moe_experts(ctx, xin_bf, dd["l3f_sw1"][0:D, :],
                             dd["l3f_sw2"][0:D, :], dd["l3f_sw3"][0:2048, :],
                             ymoe, n_exp=1, H=2048, const_scale=0.5,
                             accumulate=False, b1_dram=dd["l3f_sb1"][0:128, :],
                             b3_dram=dd["l3f_sb2"][0:128, :],
                             b2_dram=dd["l3f_sb3"][0:1, :],
                             scale_row_dram=dd["c_half_row"])
        with ctx.scope():
            emit_moe_experts(ctx, xin_bf, dd["l3f_sw1"][D:2 * D, :],
                             dd["l3f_sw2"][D:2 * D, :],
                             dd["l3f_sw3"][2048:4096, :],
                             ymoe, n_exp=1, H=2048, const_scale=0.5,
                             accumulate=True, b1_dram=dd["l3f_sb1"][128:256, :],
                             b3_dram=dd["l3f_sb2"][128:256, :],
                             b2_dram=dd["l3f_sb3"][1:2, :],
                             scale_row_dram=dd["c_half_row"])
        with ctx.scope():
            emit_moe_experts(ctx, xin_bf, dd["l3f_rw1"], dd["l3f_rw2"],
                             dd["l3f_rw3"], ymoe, n_exp=8, H=2048, cT_dram=cT,
                             accumulate=True, b1_dram=dd["l3f_rb1"],
                             b3_dram=dd["l3f_rb2"], b2_dram=dd["l3f_rb3"])
        with ctx.scope():
            emit_ln(ctx, ymoe, out_bf_dram=xout_bf, out_f32_dram=xout_f32,
                    res_dram=xin_f32)


STAGE_WEIGHTS = {
    "l0_attn": ["l0a_wq", "l0a_wk", "l0a_wv", "l0a_wo", "l0a_bo", "l0a_ln_g",
                "l0a_ln_b"],
    "l2_attn": ["l2a_WqT", "l2a_WkT", "l2a_WvT", "l2a_WoT", "c_alibi"],
    "l2_ffn": ["l2f_uprows", "l2f_dnrows", "l2f_wexp", "l2f_bexp", "l2f_upbias",
               "l2f_wconT", "l2f_bcon", "l2f_dbias", "c_ones_row_t"],
    "l3_attn": ["l3a_wq", "l3a_bq", "l3a_wk", "l3a_bk", "l3a_wv", "l3a_bv",
                "l3a_wo", "l3a_bo", "l3a_g", "l3a_b"],
    "l3_ffn": ["l3f_gw", "l3f_gb", "l3f_rw1", "l3f_rw2", "l3f_rw3", "l3f_rb1",
               "l3f_rb2", "l3f_rb3", "l3f_sw1", "l3f_sw2", "l3f_sw3", "l3f_sb1",
               "l3f_sb2", "l3f_sb3", "c_half_row"],
    "l1_ffn": ["l1f_Wq", "l1f_Wk", "l1f_Wv", "l1f_bq", "l1f_bk", "l1f_bv",
               "l1f_Wg1", "l1f_bg1", "l1f_Wg2p", "l1f_bg2", "l1f_Wo", "l1f_bo"],
    "l1_attn": ["l1a_W_dq", "l1a_W_uq", "l1a_W_dkv", "l1a_W_uk", "l1a_W_uv",
                "l1a_W_o"],
    "l0_ffn": ["l0f_lin0_w", "l0f_lin0_b", "l0f_sw1_w1", "l0f_sw1_b1", "l0f_sw1_w2",
               "l0f_sw1_b2", "l0f_gwT", "l0f_ew1", "l0f_ew3", "l0f_ew2", "l0f_shw1",
               "l0f_shw3", "l0f_shw2", "l0f_lin1_w", "l0f_lin1_b", "l0f_sw2_w1",
               "l0f_sw2_b1", "l0f_sw2_w2", "l0f_sw2_b2"],
}
CONST_KEYS = ["c_ident", "c_ones_col", "c_ones_row", "c_stair"]
STAGE_FNS = {"l0_attn": emit_l0_attn, "l0_ffn": emit_l0_ffn,
             "l1_attn": emit_l1_attn, "l1_ffn": emit_l1_ffn,
             "l2_attn": emit_l2_attn, "l2_ffn": emit_l2_ffn,
             "l3_attn": emit_l3_attn, "l3_ffn": emit_l3_ffn}
STAGE_ORDER = ["l0_attn", "l0_ffn", "l1_attn", "l1_ffn", "l2_attn", "l2_ffn",
               "l3_attn", "l3_ffn"]


def build_stage_program(stage_names, prep):
    nc = bass.Bass("TRN2", target_bir_lowering=False)
    es = contextlib.ExitStack()
    ctx = Ctx(nc, es)
    dd = {}
    for k in CONST_KEYS + sorted({w for s in stage_names for w in STAGE_WEIGHTS[s]}):
        v = prep[k]
        dd[k] = nc.dram_tensor(k, list(v.shape), F32, kind="ExternalInput").ap()
    xin = nc.dram_tensor("x_in", [D, T], F32, kind="ExternalInput").ap()
    xout = nc.dram_tensor("x_out", [D, T], F32, kind="ExternalOutput").ap()
    with es:
        load_consts(ctx, dd)
        cur = xin
        cur_bf = ctx.dram("xbf_0", [D, T], BF16)
        emit_cast_copy(ctx, cur_bf, cur)
        for i, s in enumerate(stage_names):
            last = i == len(stage_names) - 1
            nxt = xout if last else ctx.dram(f"x_{i+1}", [D, T], F32)
            nxt_bf = ctx.dram(f"xbf_{i+1}", [D, T], BF16)
            STAGE_FNS[s](ctx, dd, cur, cur_bf, nxt, nxt_bf)
            cur, cur_bf = nxt, nxt_bf
    return nc



def kernel(**inputs):
    prep = host_prep(inputs)
    nc = build_stage_program(STAGE_ORDER, prep)
    in_maps = []
    for core in range(NCORES):
        m = {k: prep[k] for k in prep}
        m["x_in"] = per_core_input(inputs, core)
        in_maps.append(m)
    res = run_bass_kernel_spmd(nc, in_maps, core_ids=list(range(NCORES)))
    out = np.empty((NCORES * B_PER_CORE, S, D), np.float32)
    for core in range(NCORES):
        xt = res.results[core]["x_out"]
        out[core * B_PER_CORE:(core + 1) * B_PER_CORE] = (
            xt.T.reshape(B_PER_CORE, S, D))
    return out
